# version 18
# baseline (speedup 1.0000x reference)
import numpy as np
import ml_dtypes
import concourse.bass as bass
import concourse.mybir as mybir
import concourse.bass_utils as _bass_utils
from concourse.bass_utils import run_bass_kernel_spmd

if not getattr(_bass_utils, "_ldwopt_patched", False):
    _orig_run_command = _bass_utils.run_command

    def _run_command_ldwopt(cmd, **kw):
        cmd = [c.replace("--enable-ldw-opt=false", "--enable-ldw-opt=true")
               if isinstance(c, str) else c for c in cmd]
        return _orig_run_command(cmd, **kw)

    _bass_utils.run_command = _run_command_ldwopt
    _bass_utils._ldwopt_patched = True

F32 = mybir.dt.float32
BF16 = mybir.dt.bfloat16
AF = mybir.ActivationFunctionType
ALU = mybir.AluOpType
AX = mybir.AxisListType

KUNI = [4096, 3584, 3072, 2560, 2048, 1536, 1024, 512]
CHUNKS = [[15, 12, 11, 8, 7, 4, 3, 0], [14, 13, 10, 9, 6, 5, 2, 1]]
NH, RK, DM = 16, 128, 1024
NT = 2048
NQS = 16
TSEQ = 4096
NEG = -1.0e30


class Sem:
    def __init__(self, nc, name):
        self.h = nc.alloc_semaphore(name)
        self.n = 0

    def inc(self, instr, k=1):
        instr.then_inc(self.h, k)
        self.n += k
        return self.n


def build_nc(dbg=0):
    nc = bass.Bass("TRN2", target_bir_lowering=False)
    TEN, VEC, SCA, SYN, GPS = nc.tensor, nc.vector, nc.scalar, nc.sync, nc.gpsimd

    d_xseq = nc.dram_tensor("xseq", [TSEQ, DM], F32, kind="ExternalInput").ap()
    d_xown = nc.dram_tensor("xown", [NT, DM], F32, kind="ExternalInput").ap()
    d_mask = nc.dram_tensor("mask", [8, 256, 512], BF16, kind="ExternalInput").ap()
    d_idb = nc.dram_tensor("identb", [128, 128], BF16, kind="ExternalInput").ap()
    d_idf = nc.dram_tensor("identf", [128, 128], F32, kind="ExternalInput").ap()
    d_tri = nc.dram_tensor("tri", [128, 1024], BF16, kind="ExternalInput").ap()
    d_ebias = nc.dram_tensor("ebias", [128, 32], F32, kind="ExternalInput").ap()
    d_wkvd = nc.dram_tensor("wkvd", [DM, RK], F32, kind="ExternalInput").ap()
    d_wq = nc.dram_tensor("wq", [DM, DM], F32, kind="ExternalInput").ap()
    d_wkvu = nc.dram_tensor("wkvu", [RK, 2048], F32, kind="ExternalInput").ap()
    d_wkvuT = nc.dram_tensor("wkvuT", [128, 2048], F32, kind="ExternalInput").ap()
    d_wout = nc.dram_tensor("wout", [DM, DM], F32, kind="ExternalInput").ap()
    d_wrT = nc.dram_tensor("wrT", [DM, 32], F32, kind="ExternalInput").ap()
    d_wguT = nc.dram_tensor("wguT", [32, DM, 1024], F32, kind="ExternalInput").ap()
    d_wdT = nc.dram_tensor("wdT", [32, 512, DM], F32, kind="ExternalInput").ap()
    d_wsgT = nc.dram_tensor("wsgT", [DM, 1024], F32, kind="ExternalInput").ap()
    d_wsuT = nc.dram_tensor("wsuT", [DM, 1024], F32, kind="ExternalInput").ap()
    d_wsdT = nc.dram_tensor("wsdT", [1024, DM], F32, kind="ExternalInput").ap()
    d_yout = nc.dram_tensor("yout", [NT, DM], F32, kind="ExternalOutput").ap()

    dbg_outs = {}
    if dbg:
        for nm, shp in [("dlat", [128, 4096]), ("dq", [128, 16384]),
                        ("dxn2", [128, 16384]), ("dcmb", [32, 2048]),
                        ("dqt", [128, 2048]), ("dS", [128, 4096]),
                        ("dz", [128, 2048]), ("dL", [128, 128]),
                        ("dresT", [128, 8192])]:
            dbg_outs[nm] = nc.dram_tensor(nm, shp, F32, kind="ExternalOutput").ap()

    sb = nc.alloc_sbuf_tensor
    bigA = sb("bigA", [128, 16384], BF16)
    bigB = sb("bigB", [128, 16384], BF16)
    bigC = sb("bigC", [128, 16384], BF16)
    bigD = sb("bigD", [128, 8192], BF16)
    bigE = sb("bigE", [128, 8192], F32)
    latT = sb("latT", [128, 4096], BF16)
    latkm = sb("latkm", [128, 4096], BF16)
    identb = sb("identb_sb", [128, 128], BF16)
    identf = sb("identf_sb", [128, 128], F32)
    xstage = sb("xstage", [128, 2048], F32)
    xntm = sb("xntm", [128, 2048], BF16)
    wkvd_sb = sb("wkvd_sb", [128, 1024], BF16)
    wkvu_sb = sb("wkvu_sb", [128, 2048], BF16)
    wkvuT_sb = sb("wkvuT_sb", [128, 2048], BF16)
    wrT_sb = sb("wrT_sb", [128, 256], BF16)
    ebias_sb = sb("ebias_sb", [128, 32], F32)
    qt_sb = sb("qt_sb", [128, 2048], BF16)
    z_sb = sb("z_sb", [128, 2048], BF16)
    yy_sb = sb("yy_sb", [128, 2048], BF16)
    xownst = sb("xownst", [128, 2048], F32)
    st_sums = sb("st_sums", [128, 32], F32)
    st_sqs = sb("st_sqs", [128, 32], F32)
    st_nms = sb("st_nms", [128, 32], F32)
    st_e2s = sb("st_e2s", [128, 32], F32)
    st_vars = sb("st_vars", [128, 32], F32)
    st_stds = sb("st_stds", [128, 32], F32)
    st_rstds = sb("st_rstds", [128, 32], F32)
    st_mx = sb("st_mx", [128, 1], F32)
    st_mlat = sb("st_mlat", [128, 1], F32)
    st_mlatb = sb("st_mlatb", [128, 1], BF16)
    eps_sb = sb("eps_sb", [128, 1], F32)
    scl8 = sb("scl8", [128, 1], F32)
    one_sb = sb("one_sb", [128, 1], F32)
    mxp = sb("mxp", [128, 8], F32)
    nmx8 = sb("nmx8", [128, 16], F32)
    Lbuf = sb("Lbuf", [128, 128], F32)
    Lsum = sb("Lsum", [128, 16], F32)
    Linv = sb("Linv", [128, 16], F32)

    pbS = nc.alloc_psum_tensor("pbS", [128, 2048], F32)
    pbB = nc.alloc_psum_tensor("pbB", [128, 1024], F32)
    pbt = nc.alloc_psum_tensor("pbt", [128, 2048], BF16)

    sWA = Sem(nc, "sWA")
    sWG = [Sem(nc, "sWG0"), Sem(nc, "sWG1")]
    sWD = [Sem(nc, f"sWD{i}") for i in range(4)]
    sXC = Sem(nc, "sXC")
    sXS = [Sem(nc, "sXS0"), Sem(nc, "sXS1")]
    sXO = [Sem(nc, "sXO0"), Sem(nc, "sXO1")]
    sZB = [Sem(nc, "sZB0"), Sem(nc, "sZB1")]
    sCM = Sem(nc, "sCM")
    sOS = [Sem(nc, "sOS0"), Sem(nc, "sOS1")]
    sDB = Sem(nc, "sDB")
    sP = Sem(nc, "sP")
    sV = Sem(nc, "sV")
    sS = Sem(nc, "sS")
    sBAR = Sem(nc, "sBAR")

    xnstage = bigE[:, 0:2048].bitcast(BF16)
    y_sb = yy_sb[:, 0:1024]
    yT_sb = yy_sb[:, 1024:2048]
    absq = yy_sb[:, 0:2048]
    Ssb2 = bigE[:, 2048:4096].bitcast(BF16)
    outst = xstage
    tri_sb = y_sb
    _rsc = yT_sb[:, 0:1024].bitcast(F32)
    cmbT = wkvuT_sb[0:32, 0:2048]
    Ssb = bigE[:, 0:2048].bitcast(BF16)
    Ssb3 = bigE[:, 4096:6144].bitcast(BF16)
    Ssb4 = bigE[:, 6144:8192].bitcast(BF16)
    resT = bigE
    smid = latT[:, 0:4096]
    cbc_sb = latkm[:, 0:1024]
    hsilu = latkm[:, 1024:2048]
    cmb_bf = latkm[:, 2048:2176]
    xn1ownT = bigA
    STsb = bigA
    xn2T = bigA
    qT = bigB
    wq_sb = bigC[:, 0:8192]
    wout_sb = bigC[:, 8192:16384]
    mask_sb = bigD

    def wgu_slot(s):
        return bigB[:, s * 8192:(s + 1) * 8192]

    def wd_slot(s):
        return bigC[:, s * 4096:(s + 1) * 4096]

    def hm_slot(ei, gp):
        off = ei * 4096 + gp * 1024
        return bigD[:, off:off + 512]

    VEC.memset(eps_sb[:], 1e-5)
    VEC.memset(scl8[:], 0.125)
    VEC.memset(one_sb[:], 1.0)
    VEC.memset(Lbuf[:], 0.0)
    sV.inc(VEC.drain())

    def gdma(sem, out, in_):
        GPS.dma_start(out=out, in_=in_).then_inc(sem.h, 16)
        sem.n += 16
        return sem.n

    def xdma(sem, out, in_):
        SYN.dma_start(out=out, in_=in_).then_inc(sem.h, 16)
        sem.n += 16
        return sem.n

    def wload(sem, dst2d, src2d, g, n):
        dst3 = dst2d.rearrange("p (g n) -> p g n", g=g)
        src3 = src2d.rearrange("(g p) n -> p g n", p=128)
        return gdma(sem, dst3, src3)

    w_kvd = wload(sZB[0], wkvd_sb[:], d_wkvd, 8, 128)
    w_q = wload(sZB[1], wq_sb, d_wq, 8, 1024)
    gdma(sCM, wkvu_sb[:], d_wkvu[:, :])
    w_kvu = gdma(sCM, wkvuT_sb[:], d_wkvuT[:, :])
    wload(sWA, wout_sb, d_wout, 8, 1024)
    w_all = wload(sWA, wrT_sb[:], d_wrT, 8, 32)

    xdma(sXC, identb[:], d_idb[:, :])
    xdma(sXC, identf[:], d_idf[:, :])
    xdma(sXC, tri_sb[:], d_tri[:, :])
    xdma(sXC, ebias_sb[:], d_ebias[:, :])
    _mdst = mask_sb[:, :].rearrange("p (s qq k) -> p s qq k", s=8, qq=2)
    _msrc = d_mask.rearrange("s (qq p) k -> p s qq k", p=128)
    x_const = xdma(sXC, _mdst, _msrc)

    SCA.preload_act_table(AF.Sqrt)

    slot_gate = {0: (0, 0), 1: (0, 0)}
    tp_gate = {0: ("V", 0), 1: ("V", 0)}

    def wait_tp(eng, g):
        if g[1]:
            eng.wait_ge(sS.h if g[0] == "S" else sV.h, g[1])

    def emit_ln_batch(srcs, dsts3, post_cb=None):
        nt = len(srcs)
        for it, src in enumerate(srcs):
            slot = it % 2
            gv, gs = slot_gate[slot]
            if gv:
                SYN.wait_ge(sV.h, gv)
            if gs:
                SYN.wait_ge(sS.h, gs)
            n = xdma(sXS[slot], xstage[:, slot * 1024:(slot + 1) * 1024], src)
            VEC.wait_ge(sXS[slot].h, n)
            SCA.wait_ge(sXS[slot].h, n)
            ins = VEC.reduce_sum(st_sums[:, it:it + 1], xstage[:, slot * 1024:(slot + 1) * 1024], axis=AX.X)
            gv = sV.inc(ins)
            SCA.activation(out=pbB[:, 0:1024], in_=xstage[:, slot * 1024:(slot + 1) * 1024],
                           func=AF.Square, accum_out=st_sqs[:, it:it + 1])
            gs = sS.inc(SCA.drain())
            slot_gate[slot] = (gv, gs)
        VEC.wait_ge(sS.h, sS.n)
        VEC.drain()
        VEC.tensor_scalar(out=st_nms[:, 0:nt], in0=st_sums[:, 0:nt], scalar1=-1.0 / DM, scalar2=None, op0=ALU.mult)
        VEC.tensor_scalar(out=st_e2s[:, 0:nt], in0=st_sqs[:, 0:nt], scalar1=1.0 / DM, scalar2=None, op0=ALU.mult)
        VEC.drain()
        VEC.tensor_tensor(out=st_vars[:, 0:nt], in0=st_nms[:, 0:nt], in1=st_nms[:, 0:nt], op=ALU.mult)
        VEC.drain()
        VEC.tensor_tensor(out=st_vars[:, 0:nt], in0=st_e2s[:, 0:nt], in1=st_vars[:, 0:nt], op=ALU.subtract)
        sV.inc(VEC.drain())
        SCA.wait_ge(sV.h, sV.n)
        SCA.activation(out=st_stds[:, 0:nt], in_=st_vars[:, 0:nt], func=AF.Sqrt,
                       bias=eps_sb[:], scale=one_sb[:])
        sS.inc(SCA.drain())
        VEC.wait_ge(sS.h, sS.n)
        VEC.reciprocal(st_rstds[:, 0:nt], st_stds[:, 0:nt])
        stats_v = sV.inc(VEC.drain())
        for it, (src, dst3) in enumerate(zip(srcs, dsts3)):
            slot = it % 2
            gv, gs = slot_gate[slot]
            if gv:
                SYN.wait_ge(sV.h, gv)
            if gs:
                SYN.wait_ge(sS.h, gs)
            n = xdma(sXS[slot], xstage[:, slot * 1024:(slot + 1) * 1024], src)
            VEC.wait_ge(sXS[slot].h, n)
            ins = VEC.tensor_scalar(out=xntm[:, slot * 1024:(slot + 1) * 1024],
                                    in0=xstage[:, slot * 1024:(slot + 1) * 1024],
                                    scalar1=st_nms[:, it:it + 1], scalar2=st_rstds[:, it:it + 1],
                                    op0=ALU.add, op1=ALU.mult)
            gv = sV.inc(ins)
            slot_gate[slot] = (gv, 0)
            TEN.wait_ge(sV.h, gv)
            half = slot * 1024
            wait_tp(TEN, tp_gate[slot])
            for cc in range(8):
                ins = TEN.transpose(pbt[:, half + cc * 128: half + (cc + 1) * 128],
                                    xntm[:, slot * 1024 + cc * 128: slot * 1024 + (cc + 1) * 128],
                                    identb[:])
            sP.inc(ins)
            VEC.wait_ge(sP.h, sP.n)
            ins = VEC.tensor_copy(dst3, pbt[:, half:half + 1024].rearrange("p (cc k) -> p cc k", cc=8))
            tp_gate[slot] = ("V", sV.inc(ins))
            if post_cb is not None:
                post_cb(it)

    TEN.wait_ge(sXC.h, x_const)
    tags = []

    def a1_cb(it):
        if it % 4 != 3:
            return
        g = it // 4
        TEN.wait_ge(sV.h, sV.n)
        if g == 0:
            TEN.wait_ge(sZB[0].h, w_kvd)
        ph = (g % 2) * 512
        for cc in range(8):
            ins = TEN.matmul(pbS[:, ph:ph + 512],
                             wkvd_sb[:, cc * 128:(cc + 1) * 128],
                             xnstage[:, cc * 512:(cc + 1) * 512],
                             start=(cc == 0), stop=(cc == 7))
        sP.inc(ins)
        VEC.wait_ge(sP.h, sP.n)
        ins = VEC.tensor_copy(latT[:, g * 512:(g + 1) * 512], pbS[:, ph:ph + 512])
        sV.inc(ins)

    srcs = [d_xseq[it * 128:(it + 1) * 128, :] for it in range(32)]
    dsts = [xnstage[:].rearrange("p (cc k) -> p cc k", cc=8)[:, :, (it % 4) * 128:(it % 4) * 128 + 128]
            for it in range(32)]
    emit_ln_batch(srcs, dsts, post_cb=a1_cb)

    TEN.wait_ge(sV.h, sV.n)
    lk_ev = {}
    for b in range(4):
        half = (b % 2) * 1024
        if b >= 2:
            TEN.wait_ge(sV.h, lk_ev[b - 2])
        for j in range(8):
            kc = b * 8 + j
            ins = TEN.transpose(pbt[:, half + j * 128: half + (j + 1) * 128],
                                latT[:, kc * 128:(kc + 1) * 128], identb[:])
        sP.inc(ins)
        VEC.wait_ge(sP.h, sP.n)
        ins = VEC.tensor_copy(latkm[:, b * 1024:(b + 1) * 1024], pbt[:, half:half + 1024])
        lk_ev[b] = sV.inc(ins)

    VEC.reduce_max(st_mlat[:], latT[:], axis=AX.X, apply_absolute_value=True)
    sV.inc(VEC.drain())
    VEC.tensor_copy(st_mlatb[:], st_mlat[:])
    sV.inc(VEC.drain())
    TEN.wait_ge(sV.h, sV.n)
    tp_gate[0] = ("V", sV.n)
    tp_gate[1] = ("V", sV.n)

    srcs = [d_xown[it * 128:(it + 1) * 128, :] for it in range(16)]
    dsts = [xn1ownT[:].rearrange("p (cc k) -> p cc k", cc=8)[:, :, it * 128:(it + 1) * 128]
            for it in range(16)]
    emit_ln_batch(srcs, dsts)

    TEN.wait_ge(sV.h, sV.n)
    TEN.wait_ge(sZB[1].h, w_q)
    qp_ev = {}
    j = 0
    for hdc in range(8):
        for tq in range(4):
            ph = (j % 2) * 512
            if j >= 2:
                TEN.wait_ge(sV.h, qp_ev[j - 2])
            for cc in range(8):
                ins = TEN.matmul(pbB[:, ph:ph + 512],
                                 wq_sb[:, cc * 1024 + hdc * 128: cc * 1024 + (hdc + 1) * 128],
                                 xn1ownT[:, cc * 2048 + tq * 512: cc * 2048 + (tq + 1) * 512],
                                 start=(cc == 0), stop=(cc == 7))
            sP.inc(ins)
            VEC.wait_ge(sP.h, sP.n)
            ins = VEC.tensor_copy(qT[:, hdc * 2048 + tq * 512: hdc * 2048 + (tq + 1) * 512],
                                  pbB[:, ph:ph + 512])
            qp_ev[j] = sV.inc(ins)
            j += 1
    wq_region_free = sP.n

    if dbg == 1:
        VEC.wait_ge(sV.h, sV.n)
        sV.inc(VEC.drain())
        SYN.wait_ge(sV.h, sV.n)
        GPS.wait_ge(sV.h, sV.n)
        gdma(sDB, dbg_outs["dlat"][:, :], latT[:])
        gdma(sDB, dbg_outs["dq"][:, :], qT[:])
    if dbg == 1:
        GPS.wait_ge(sDB.h, sDB.n)
        return nc, dbg_outs

    SCA.preload_act_table(AF.Exp)
    TEN.wait_ge(sCM.h, w_kvu)

    res_tt_done = {}
    o_count = {}
    exp_quad_done = [0, 0, 0, 0]

    for qs in range(NQS):
        slot = qs // 2
        kuni = KUNI[slot]
        nk5 = kuni // 512
        nkc = kuni // 128

        if qs >= 2:
            SYN.wait_ge(sV.h, res_tt_done[qs - 2])
        xo_n = xdma(sXO[qs % 2], xownst[:, (qs % 2) * 1024:(qs % 2 + 1) * 1024],
                    d_xown[qs * 128:(qs + 1) * 128, :])

        TEN.wait_ge(sV.h, sV.n)
        qt_ev = {}
        for hb in range(4):
            ph = (hb % 2) * 512
            if hb >= 2:
                TEN.wait_ge(sV.h, qt_ev[hb - 2])
            for hh in range(4):
                h = hb * 4 + hh
                ins = TEN.matmul(pbB[:, ph + hh * 128: ph + (hh + 1) * 128],
                                 wkvuT_sb[:, h * 128:(h + 1) * 128],
                                 qT[:, (h // 2) * 2048 + qs * 128:(h // 2) * 2048 + (qs + 1) * 128],
                                 start=True, stop=True)
            sP.inc(ins)
            VEC.wait_ge(sP.h, sP.n)
            ins = VEC.tensor_scalar(out=qt_sb[:, hb * 512:(hb + 1) * 512], in0=pbB[:, ph:ph + 512],
                                    scalar1=0.125, scalar2=None, op0=ALU.mult)
            qt_ev[hb] = sV.inc(ins)
            SCA.wait_ge(sV.h, qt_ev[hb])
            insA = SCA.activation(out=absq[:, hb * 512:(hb + 1) * 512],
                                  in_=qt_sb[:, hb * 512:(hb + 1) * 512], func=AF.Abs)
            abs_cnt = sS.inc(insA)
        if qs == NQS - 1:
            qT_region_free = sP.n

        if dbg == 21 and qs == 0:
            GPS.wait_ge(sV.h, sV.n)
            gdma(sDB, dbg_outs["dqt"][:, :], qt_sb[:])
            GPS.wait_ge(sDB.h, sDB.n)
            return nc, dbg_outs

        VEC.wait_ge(sS.h, sS.n)
        VEC.memset(Lbuf[:], 0.0)
        sV.inc(VEC.drain())
        SCA.wait_ge(sV.h, sV.n)
        exp_done_cnt = {}

        tp_done = {}
        av_gate = [0]

        def emit_tp(h_t, exp_cnt):
            TEN.wait_ge(sS.h, exp_cnt)
            ng8 = (nkc + 7) // 8
            tp_ev = {}
            for b8 in range(ng8):
                half = (b8 % 2) * 1024
                g = tp_ev[b8 - 2] if b8 >= 2 else tp_gate[b8 % 2]
                wait_tp(TEN, g)
                cnt = min(8, nkc - b8 * 8)
                Sbuf = (Ssb, Ssb2, Ssb3, Ssb4)[h_t % 4]
                for jj in range(cnt):
                    kc = b8 * 8 + jj
                    ins2 = TEN.transpose(pbt[:, half + jj * 128: half + (jj + 1) * 128],
                                         Sbuf[:, kc * 128:(kc + 1) * 128], identb[:])
                sP.inc(ins2)
                dst = STsb[:].rearrange("p (kc r) -> p kc r", kc=32, r=512)[
                    :, b8 * 8: b8 * 8 + cnt, (h_t % 4) * 128:(h_t % 4 + 1) * 128]
                src = pbt[:, half: half + cnt * 128].rearrange("p (kc r) -> p kc r", kc=cnt)
                VEC.wait_ge(sP.h, sP.n)
                ins2 = VEC.tensor_copy(dst, src)
                tp_ev[b8] = ("V", sV.inc(ins2))
                tp_gate[b8 % 2] = tp_ev[b8]
            tp_done[h_t] = tp_ev[ng8 - 1][1]

        def emit_av(hg, part):
            nk2 = nkc // 2
            lo, hi = (0, nk2) if part == 0 else (nk2, nkc)
            if part == 0:
                TEN.wait_ge(sV.h, max(tp_done[hg * 4], tp_done[hg * 4 + 1],
                                      tp_done[hg * 4 + 2], tp_done[hg * 4 + 3], av_gate[0]))
            for kc in range(lo, hi):
                ins2 = TEN.matmul(pbB[:, 0:512], latkm[:, kc * 128:(kc + 1) * 128],
                                  STsb[:, kc * 512:(kc + 1) * 512],
                                  start=(kc == 0), stop=(kc == nkc - 1))
            sP.inc(ins2)
            if part == 1:
                VEC.wait_ge(sP.h, sP.n)
                ins2 = VEC.tensor_copy(z_sb[:, hg * 512:(hg + 1) * 512], pbB[:, 0:512])
                av_gate[0] = sV.inc(ins2)

        TEN.wait_ge(sS.h, abs_cnt)
        for h in range(16):
            ins = TEN.matmul(pbB[:, h:h + 1], absq[:, h * 128:(h + 1) * 128],
                             st_mlatb[:], start=True, stop=True)
        sP.inc(ins)
        VEC.wait_ge(sP.h, sP.n)
        VEC.tensor_scalar(out=nmx8[:, 0:16], in0=pbB[:, 0:16], scalar1=-1.0, scalar2=None, op0=ALU.mult)
        nmx_ready = sV.inc(VEC.drain())
        SCA.wait_ge(sV.h, nmx_ready)

        exp_last = {}
        for qd4 in range(5):
            todo = []
            if qd4 >= 1:
                for j in range(4):
                    todo.append(("tp", 4 * (qd4 - 1) + j))
                todo.append(("av", qd4 - 1, 0))
                todo.append(("av", qd4 - 1, 1))

            def run_blob():
                job = todo.pop(0)
                if job[0] == "tp":
                    emit_tp(job[1], exp_done_cnt[job[1]])
                else:
                    emit_av(job[1], job[2])

            while todo and todo[0][0] == "tp":
                run_blob()
            if qd4 == 4:
                while todo:
                    run_blob()
                continue
            heads = [4 * qd4 + j for j in range(4)]
            p2 = {h: {} for h in heads}
            mask_cnts = {}
            for kj in range(nk5):
                last = (kj == nk5 - 1)
                for h in heads:
                    pp = (h % 4) * 512
                    if kj >= 1:
                        TEN.wait_ge(sS.h, p2[h][kj - 1])
                    else:
                        TEN.wait_ge(sS.h, exp_quad_done[h % 4])
                    ins = TEN.matmul(pbS[:, pp:pp + 512],
                                     qt_sb[:, h * 128:(h + 1) * 128],
                                     latT[:, kj * 512:(kj + 1) * 512],
                                     start=True, stop=True)
                    pe_mm = sP.inc(ins)
                    if last:
                        VEC.wait_ge(sP.h, pe_mm)
                        mslice = mask_sb[:, slot * 1024 + (qs % 2) * 512: slot * 1024 + (qs % 2) * 512 + 512]
                        ins2 = VEC.tensor_tensor(out=pbS[:, pp:pp + 512], in0=pbS[:, pp:pp + 512],
                                                 in1=mslice, op=ALU.add)
                        mask_cnts[h] = sV.inc(ins2)
                        SCA.wait_ge(sV.h, mask_cnts[h])
                    else:
                        SCA.wait_ge(sP.h, pe_mm)
                    Sout = (Ssb, Ssb2, Ssb3, Ssb4)[h % 4]
                    ins = SCA.activation(out=Sout[:, kj * 512: (kj + 1) * 512],
                                         in_=pbS[:, pp:pp + 512],
                                         func=AF.Exp, scale=one_sb[:], bias=nmx8[:, h:h + 1],
                                         accum_out=Lbuf[:, h * 8 + kj: h * 8 + kj + 1])
                    p2[h][kj] = sS.inc(ins)
                    exp_last[h] = p2[h][kj]
                if not last and todo:
                    run_blob()
            while todo:
                run_blob()
            for h in heads:
                exp_quad_done[h % 4] = exp_last[h]
                exp_done_cnt[h] = exp_last[h]

        VEC.wait_ge(sS.h, sS.n)
        VEC.tensor_reduce(Lsum[:], Lbuf[:].rearrange("p (h k) -> p h k", h=16), axis=AX.X, op=ALU.add)
        VEC.drain()
        VEC.reciprocal(Linv[:], Lsum[:])
        sV.inc(VEC.drain())

        TEN.wait_ge(sV.h, sV.n)
        for h in range(NH):
            ins = TEN.matmul(pbB[:, h * 64:(h + 1) * 64],
                             z_sb[:, h * 128:(h + 1) * 128],
                             wkvu_sb[:, 1024 + h * 64: 1024 + (h + 1) * 64],
                             start=True, stop=True)
        sP.inc(ins)
        VEC.wait_ge(sP.h, sP.n)
        linv_b = Linv[:].unsqueeze(2).broadcast_to([128, 16, 64])
        ins = VEC.tensor_tensor(out=y_sb[:].rearrange("p (h d) -> p h d", h=16),
                                in0=pbB[:].rearrange("p (h d) -> p h d", h=16),
                                in1=linv_b, op=ALU.mult)
        sV.inc(ins)
        TEN.wait_ge(sV.h, sV.n)
        for cc in range(8):
            ins = TEN.transpose(pbt[:, cc * 128:(cc + 1) * 128],
                                y_sb[:, cc * 128:(cc + 1) * 128], identb[:])
        sP.inc(ins)
        VEC.wait_ge(sP.h, sP.n)
        ins = VEC.tensor_copy(yT_sb[:], pbt[:, 0:1024])
        sV.inc(ins)
        TEN.wait_ge(sV.h, sV.n)
        if qs == 0:
            TEN.wait_ge(sWA.h, w_all)
        for cc in range(2):
            for hdc in range(8):
                ins = TEN.matmul(pbB[:, cc * 512:(cc + 1) * 512],
                                 yT_sb[:, hdc * 128:(hdc + 1) * 128],
                                 wout_sb[:, hdc * 1024 + cc * 512: hdc * 1024 + cc * 512 + 512],
                                 start=(hdc == 0), stop=(hdc == 7))
        sP.inc(ins)
        VEC.wait_ge(sP.h, sP.n)
        VEC.wait_ge(sXO[qs % 2].h, xo_n)
        if qs >= 2:
            VEC.wait_ge(sOS[qs % 2].h, o_count[qs - 2])
        ins = VEC.tensor_tensor(out=outst[:, (qs % 2) * 1024:(qs % 2 + 1) * 1024],
                                in0=pbB[:],
                                in1=xownst[:, (qs % 2) * 1024:(qs % 2 + 1) * 1024], op=ALU.add)
        res_tt_done[qs] = sV.inc(ins)
        if qs == NQS - 1:
            wout_region_free = sP.n
        GPS.wait_ge(sV.h, sV.n)
        o_count[qs] = gdma(sOS[qs % 2], d_yout[qs * 128:(qs + 1) * 128, :],
                           outst[:, (qs % 2) * 1024:(qs % 2 + 1) * 1024])
        if dbg == 24 and qs == 0:
            GPS.wait_ge(sOS[0].h, sOS[0].n)
            return nc, dbg_outs

    if dbg == 2:
        GPS.wait_ge(sOS[0].h, sOS[0].n)
        GPS.wait_ge(sOS[1].h, sOS[1].n)
        return nc, dbg_outs

    sBAR.inc(TEN.nop())
    sBAR.inc(VEC.drain())
    sBAR.inc(SCA.drain())
    for eng in (TEN, VEC, SCA, SYN, GPS):
        eng.wait_ge(sBAR.h, 3)

    SCA.preload_act_table(AF.Sqrt)
    SYN.wait_ge(sOS[0].h, sOS[0].n)
    SYN.wait_ge(sOS[1].h, sOS[1].n)
    srcs = [d_yout[it * 128:(it + 1) * 128, :] for it in range(16)]
    dsts = [xn2T[:].rearrange("p (cc k) -> p cc k", cc=8)[:, :, it * 128:(it + 1) * 128]
            for it in range(16)]
    emit_ln_batch(srcs, dsts)

    SCA.preload_act_table(AF.Exp)
    VEC.memset(cmb_bf[:], 0.0)
    sV.inc(VEC.drain())
    TEN.wait_ge(sV.h, sV.n)
    TEN.wait_ge(sS.h, sS.n)
    zF = z_sb[:, 0:2048].bitcast(F32)
    qF = qt_sb[:, 0:2048].bitcast(F32)
    A, B = zF[:, 0:512], zF[:, 512:1024]
    C, D = qF[:, 0:512], qF[:, 512:1024]
    m16 = _rsc[:, 0:16]
    s16 = _rsc[:, 16:32]
    l16 = _rsc[:, 32:48]
    s16b = _rsc[:, 48:64]
    r16b = _rsc[:, 64:80]
    gtb = Ssb[:, 0:2048]
    eqb = Ssb[:, 2048:4096]
    for it in range(16):
        for cc in range(8):
            ins = TEN.matmul(pbB[:, it * 32:(it + 1) * 32],
                             xn2T[:, cc * 2048 + it * 128: cc * 2048 + (it + 1) * 128],
                             wrT_sb[:, cc * 32:(cc + 1) * 32],
                             start=(cc == 0), stop=(cc == 7))
    sP.inc(ins)
    VEC.wait_ge(sP.h, sP.n)
    VEC.tensor_copy(A, pbB[:, 0:512])
    sV.inc(VEC.drain())
    VEC.tensor_reduce(m16, A.rearrange("p (g e) -> p g e", g=16), axis=AX.X, op=ALU.max, negate=True)
    VEC.drain()
    mb = m16.unsqueeze(2).broadcast_to([128, 16, 32])
    VEC.tensor_tensor(out=B.rearrange("p (g e) -> p g e", g=16), in0=A.rearrange("p (g e) -> p g e", g=16),
                      in1=mb, op=ALU.add)
    sV.inc(VEC.drain())
    SCA.wait_ge(sV.h, sV.n)
    ins = SCA.activation(out=C, in_=B, func=AF.Exp)
    sS.inc(ins)
    VEC.wait_ge(sS.h, sS.n)
    VEC.tensor_reduce(s16, C.rearrange("p (g e) -> p g e", g=16), axis=AX.X, op=ALU.add)
    VEC.drain()
    VEC.reciprocal(l16, s16)
    VEC.drain()
    lb = l16.unsqueeze(2).broadcast_to([128, 16, 32])
    VEC.tensor_tensor(out=D.rearrange("p (g e) -> p g e", g=16), in0=C.rearrange("p (g e) -> p g e", g=16),
                      in1=lb, op=ALU.mult)
    VEC.drain()
    eb = ebias_sb[:].unsqueeze(1).broadcast_to([128, 16, 32])
    VEC.tensor_tensor(out=B.rearrange("p (g e) -> p g e", g=16), in0=D.rearrange("p (g e) -> p g e", g=16),
                      in1=eb, op=ALU.add)
    VEC.drain()
    for bt in range(8):
        corr2 = B[:, bt * 64:(bt + 1) * 64].rearrange("p (g e) -> p g e", g=2)
        cj = corr2.unsqueeze(2).broadcast_to([128, 2, 32, 32])
        ce = corr2.unsqueeze(3).broadcast_to([128, 2, 32, 32])
        VEC.tensor_tensor(out=gtb.rearrange("p (g e j) -> p g e j", g=2, e=32, j=32), in0=cj, in1=ce, op=ALU.is_gt)
        VEC.tensor_tensor(out=eqb.rearrange("p (g e j) -> p g e j", g=2, e=32, j=32), in0=cj, in1=ce, op=ALU.is_equal)
        VEC.drain()
        trib = tri_sb[:].unsqueeze(1).broadcast_to([128, 2, 1024]).rearrange("p g (e j) -> p g e j", e=32)
        VEC.tensor_tensor(out=eqb.rearrange("p (g e j) -> p g e j", g=2, e=32, j=32),
                          in0=eqb.rearrange("p (g e j) -> p g e j", g=2, e=32, j=32), in1=trib, op=ALU.mult)
        VEC.drain()
        VEC.tensor_tensor(out=gtb, in0=gtb, in1=eqb, op=ALU.add)
        VEC.drain()
        VEC.tensor_reduce(A[:, bt * 64:(bt + 1) * 64].rearrange("p (g e) -> p g e", g=2),
                          gtb.rearrange("p (g e j) -> p (g e) j", g=2, e=32, j=32), axis=AX.X, op=ALU.add)
        VEC.drain()
    VEC.tensor_scalar(out=C, in0=A, scalar1=5.5, scalar2=None, op0=ALU.is_lt)
    VEC.drain()
    VEC.tensor_tensor(out=A, in0=D, in1=C, op=ALU.mult)
    VEC.drain()
    VEC.tensor_reduce(s16b, A.rearrange("p (g e) -> p g e", g=16), axis=AX.X, op=ALU.add)
    VEC.drain()
    VEC.tensor_scalar(out=s16b, in0=s16b, scalar1=1e-12, scalar2=None, op0=ALU.max)
    VEC.drain()
    VEC.reciprocal(r16b, s16b)
    VEC.drain()
    rb = r16b.unsqueeze(2).broadcast_to([128, 16, 32])
    VEC.tensor_tensor(out=D.rearrange("p (g e) -> p g e", g=16), in0=A.rearrange("p (g e) -> p g e", g=16),
                      in1=rb, op=ALU.mult)
    sV.inc(VEC.drain())
    for it in range(16):
        ins = VEC.tensor_copy(cmb_bf[:, 0:32], D[:, it * 32:(it + 1) * 32])
        sV.inc(ins)
        TEN.wait_ge(sV.h, sV.n)
        ins = TEN.transpose(pbt[:, 0:128], cmb_bf[:, 0:128], identb[:])
        sP.inc(ins)
        VEC.wait_ge(sP.h, sP.n)
        ins = VEC.tensor_copy(cmbT[:, it * 128:(it + 1) * 128], pbt[0:32, 0:128])
        sV.inc(ins)
        TEN.wait_ge(sV.h, sV.n)

    if dbg == 3:
        VEC.wait_ge(sV.h, sV.n)
        sV.inc(VEC.drain())
        SYN.wait_ge(sV.h, sV.n)
        GPS.wait_ge(sV.h, sV.n)
        gdma(sDB, dbg_outs["dxn2"][:, :], xn2T[:])
        gdma(sDB, dbg_outs["dcmb"][:, :], cmbT[:, :])
        GPS.wait_ge(sDB.h, sDB.n)
        return nc, dbg_outs

    SCA.preload_act_table(AF.Silu)
    gu_free = [qT_region_free, qT_region_free]
    wd_free = [wq_region_free, wq_region_free, wout_region_free, wout_region_free]
    cbc_gate = [0]
    gu_ch = {"silu": [0, 0], "tt1": [0, 0], "tt2": [0, 0], "dq02": 0}
    dn_ev = [0, 0]
    rin_tp = {}
    out_oc = dict(o_count)
    out_tp = {}

    for tb in range(2):
        for tt in range(8):
            git = tb * 8 + tt
            slot = tt % 2
            if (tb, tt) in ((0, 0), (0, 1)):
                gv, gs = slot_gate[slot]
                if gv:
                    SYN.wait_ge(sV.h, gv)
                if gs:
                    SYN.wait_ge(sS.h, gs)
            else:
                pg = git - 2
                SYN.wait_ge(sV.h, rin_tp[pg])
            if tb == 1 and tt < 2:
                SYN.wait_ge(sOS[tt % 2].h, out_oc[16 + 6 + tt])
            rs_n = xdma(sXS[slot], xstage[:, slot * 1024:(slot + 1) * 1024],
                        d_yout[git * 128:(git + 1) * 128, :])
            TEN.wait_ge(sXS[slot].h, rs_n)
            TEN.wait_ge(sV.h, sV.n)
            if tt == 0:
                TEN.wait_ge(sS.h, sS.n)
            for hc in range(8):
                ins = TEN.transpose(pbB[:, hc * 128:(hc + 1) * 128],
                                    xstage[:, slot * 1024 + hc * 128: slot * 1024 + (hc + 1) * 128],
                                    identf[:])
            sP.inc(ins)
            VEC.wait_ge(sP.h, sP.n)
            dst = resT[:].rearrange("p (hc t) -> p hc t", hc=8)[:, :, tt * 128:(tt + 1) * 128]
            ins = VEC.tensor_copy(dst, pbB[:].rearrange("p (hc t) -> p hc t", hc=8))
            rin_tp[git] = sV.inc(ins)

        for tkb in range(2):
            T0 = tb * 1024 + tkb * 512
            GPS.wait_ge(sP.h, max(gu_free[0], gu_free[1], wd_free[0], wd_free[1]))
            n_g0 = wload(sWG[0], wgu_slot(0), d_wsgT, 8, 1024)
            n_g1 = wload(sWG[1], wgu_slot(1), d_wsuT, 8, 1024)
            n_d0 = wload(sWD[0], wd_slot(0), d_wsdT[0:512, :], 4, 1024)
            n_d1 = wload(sWD[1], wd_slot(1), d_wsdT[512:1024, :], 4, 1024)

            TEN.wait_ge(sV.h, sV.n)
            TEN.wait_ge(sWG[0].h, n_g0)
            TEN.wait_ge(sWG[1].h, n_g1)
            TEN.wait_ge(sWD[0].h, n_d0)
            TEN.wait_ge(sWD[1].h, n_d1)
            sh_silu = [0, 0]
            sh_tt = []
            for ic in range(8):
                gsl = (ic % 2) * 512
                if ic >= 2:
                    TEN.wait_ge(sS.h, sh_silu[ic % 2])
                for cc in range(8):
                    ins = TEN.matmul(pbB[:, gsl:gsl + 512],
                                     wgu_slot(0)[:, cc * 1024 + ic * 128: cc * 1024 + (ic + 1) * 128],
                                     xn2T[:, cc * 2048 + T0: cc * 2048 + T0 + 512],
                                     start=(cc == 0), stop=(cc == 7))
                gmm = sP.inc(ins)
                if ic >= 2:
                    TEN.wait_ge(sV.h, sh_tt[ic - 2])
                for cc in range(8):
                    ins2 = TEN.matmul(pbS[:, gsl:gsl + 512],
                                      wgu_slot(1)[:, cc * 1024 + ic * 128: cc * 1024 + (ic + 1) * 128],
                                      xn2T[:, cc * 2048 + T0: cc * 2048 + T0 + 512],
                                      start=(cc == 0), stop=(cc == 7))
                umm = sP.inc(ins2)
                SCA.wait_ge(sP.h, gmm)
                if ic >= 2:
                    SCA.wait_ge(sV.h, sh_tt[ic - 2])
                ins = SCA.activation(out=hsilu[:, gsl:gsl + 512], in_=pbB[:, gsl:gsl + 512], func=AF.Silu)
                sh_silu[ic % 2] = sS.inc(ins)
                VEC.wait_ge(sP.h, umm)
                VEC.wait_ge(sS.h, sh_silu[ic % 2])
                ins = VEC.tensor_tensor(out=smid[:, ic * 512:(ic + 1) * 512],
                                        in0=pbS[:, gsl:gsl + 512], in1=hsilu[:, gsl:gsl + 512], op=ALU.mult)
                sh_tt.append(sV.inc(ins))
            gu_ch["silu"] = sh_silu[:]
            gu_ch["tt1"] = [sh_tt[6], sh_tt[7]]
            gu_ch["tt2"] = [sh_tt[6], sh_tt[7]]
            gu_free[0] = gu_free[1] = sP.n
            for qi, qd in enumerate((1, 0, 3, 2)):
                qh = (qd % 2) * 1024
                if qi == 0:
                    TEN.wait_ge(sV.h, dn_ev[qd % 2])
                elif qi == 1:
                    TEN.wait_ge(sV.h, max(dn_ev[qd % 2], sh_tt[7]))
                else:
                    TEN.wait_ge(sV.h, dn_ev[qd % 2])
                for ic in range(8):
                    if qi == 0:
                        TEN.wait_ge(sV.h, sh_tt[ic])
                    for m in range(2):
                        hc = qd * 2 + m
                        ins = TEN.matmul(pbS[:, qh + m * 512: qh + (m + 1) * 512],
                                         wd_slot(ic // 4)[:, (ic % 4) * 1024 + hc * 128: (ic % 4) * 1024 + (hc + 1) * 128],
                                         smid[:, ic * 512:(ic + 1) * 512],
                                         start=(ic == 0), stop=(ic == 7))
                dmm = sP.inc(ins)
                VEC.wait_ge(sP.h, dmm)
                src = pbS[:, qh:qh + 1024].rearrange("p (m t) -> p m t", m=2)
                dst = resT[:].rearrange("p (hc t) -> p hc t", hc=8)[
                    :, qd * 2:qd * 2 + 2, tkb * 512:tkb * 512 + 512]
                ins = VEC.tensor_tensor(out=dst, in0=src, in1=dst, op=ALU.add)
                dn_ev[qd % 2] = sV.inc(ins)
            cbc_gate[0] = max(cbc_gate[0], dn_ev[1])
            gu_ch["dq02"] = max(gu_ch["dq02"], dn_ev[0])
            wd_free[0] = wd_free[1] = sP.n
            if dbg == 5 and tb == 0 and tkb == 0:
                GPS.wait_ge(sV.h, sV.n)
                gdma(sDB, dbg_outs["dresT"][:, :], resT[:])
                GPS.wait_ge(sDB.h, sDB.n)
                return nc, dbg_outs

            for pr in range(16):
                e0 = pr * 2
                gu_w = [0, 0]
                wd_w = [0, 0]
                for ei in range(2):
                    e = e0 + ei
                    GPS.wait_ge(sP.h, gu_free[e % 2])
                    gu_w[ei] = wload(sWG[e % 2], wgu_slot(e % 2), d_wguT[e], 8, 1024)
                    GPS.wait_ge(sP.h, wd_free[2 * (pr % 2) + ei])
                    wd_w[ei] = wload(sWD[2 * (pr % 2) + ei], wd_slot(2 * (pr % 2) + ei), d_wdT[e], 4, 1024)
                TEN.wait_ge(sV.h, cbc_gate[0])
                cb_mm = {}
                for ei in range(2):
                    e = e0 + ei
                    sel = identb[:, e:e + 1].broadcast_to([128, 128])
                    ins = TEN.matmul(pbS[:, 1024 + ei * 512: 1536 + ei * 512], sel,
                                     wkvuT_sb[:, T0:T0 + 512], start=True, stop=True)
                    cb_mm[ei] = sP.inc(ins)
                for ei in range(2):
                    VEC.wait_ge(sP.h, cb_mm[ei])
                    ins = VEC.tensor_copy(cbc_sb[:, ei * 512:(ei + 1) * 512],
                                          pbS[:, 1024 + ei * 512: 1536 + ei * 512])
                    cbc_gate[0] = sV.inc(ins)
                tt2_list = []
                for g in range(8):
                    ei, gp = g // 4, g % 4
                    e = e0 + ei
                    gsl = (g % 2) * 512
                    if gp == 0:
                        TEN.wait_ge(sWG[e % 2].h, gu_w[ei])
                    if gu_ch["silu"][g % 2]:
                        TEN.wait_ge(sS.h, gu_ch["silu"][g % 2])
                    for cc in range(8):
                        ins = TEN.matmul(pbB[:, gsl:gsl + 512],
                                         wgu_slot(e % 2)[:, cc * 1024 + gp * 128: cc * 1024 + (gp + 1) * 128],
                                         xn2T[:, cc * 2048 + T0: cc * 2048 + T0 + 512],
                                         start=(cc == 0), stop=(cc == 7))
                    gmm = sP.inc(ins)
                    TEN.wait_ge(sV.h, max(gu_ch["tt1"][g % 2], gu_ch["dq02"]))
                    for cc in range(8):
                        ins2 = TEN.matmul(pbS[:, gsl:gsl + 512],
                                          wgu_slot(e % 2)[:, cc * 1024 + (4 + gp) * 128: cc * 1024 + (5 + gp) * 128],
                                          xn2T[:, cc * 2048 + T0: cc * 2048 + T0 + 512],
                                          start=(cc == 0), stop=(cc == 7))
                    umm = sP.inc(ins2)
                    SCA.wait_ge(sP.h, gmm)
                    if gu_ch["tt2"][g % 2]:
                        SCA.wait_ge(sV.h, gu_ch["tt2"][g % 2])
                    ins = SCA.activation(out=hsilu[:, gsl:gsl + 512], in_=pbB[:, gsl:gsl + 512], func=AF.Silu)
                    gu_ch["silu"][g % 2] = sS.inc(ins)
                    VEC.wait_ge(sP.h, umm)
                    VEC.wait_ge(sS.h, gu_ch["silu"][g % 2])
                    ins = VEC.tensor_tensor(out=hsilu[:, gsl:gsl + 512], in0=pbS[:, gsl:gsl + 512],
                                            in1=hsilu[:, gsl:gsl + 512], op=ALU.mult)
                    gu_ch["tt1"][g % 2] = sV.inc(ins)
                    VEC.drain()
                    ins = VEC.tensor_tensor(out=hm_slot(ei, gp),
                                            in0=hsilu[:, gsl:gsl + 512], in1=cbc_sb[:, ei * 512:(ei + 1) * 512],
                                            op=ALU.mult)
                    gu_ch["tt2"][g % 2] = sV.inc(ins)
                    tt2_list.append(gu_ch["tt2"][g % 2])
                    if gp == 3:
                        gu_free[e % 2] = sP.n
                TEN.wait_ge(sWD[2 * (pr % 2)].h, wd_w[0])
                TEN.wait_ge(sWD[2 * (pr % 2) + 1].h, wd_w[1])
                for qi, qd in enumerate((1, 0, 3, 2)):
                    qh = (qd % 2) * 1024
                    if qi == 0:
                        TEN.wait_ge(sV.h, dn_ev[qd % 2])
                    elif qi == 1:
                        TEN.wait_ge(sV.h, max(dn_ev[qd % 2], tt2_list[7]))
                    else:
                        TEN.wait_ge(sV.h, dn_ev[qd % 2])
                    for ei in range(2):
                        for gp in range(4):
                            if qi == 0:
                                TEN.wait_ge(sV.h, tt2_list[ei * 4 + gp])
                            for m in range(2):
                                hc = qd * 2 + m
                                ins = TEN.matmul(pbS[:, qh + m * 512: qh + (m + 1) * 512],
                                                 wd_slot(2 * (pr % 2) + ei)[:, gp * 1024 + hc * 128: gp * 1024 + (hc + 1) * 128],
                                                 hm_slot(ei, gp),
                                                 start=(ei == 0 and gp == 0), stop=(ei == 1 and gp == 3))
                    dmm = sP.inc(ins)
                    VEC.wait_ge(sP.h, dmm)
                    src = pbS[:, qh:qh + 1024].rearrange("p (m t) -> p m t", m=2)
                    dst = resT[:].rearrange("p (hc t) -> p hc t", hc=8)[
                        :, qd * 2:qd * 2 + 2, tkb * 512:tkb * 512 + 512]
                    ins = VEC.tensor_tensor(out=dst, in0=src, in1=dst, op=ALU.add)
                    dn_ev[qd % 2] = sV.inc(ins)
                cbc_gate[0] = max(cbc_gate[0], dn_ev[1])
                gu_ch["dq02"] = max(gu_ch["dq02"], dn_ev[0])
                wd_free[2 * (pr % 2)] = sP.n
                wd_free[2 * (pr % 2) + 1] = sP.n
                if dbg == 6 and tb == 0 and tkb == 0 and pr == 0:
                    GPS.wait_ge(sV.h, sV.n)
                    gdma(sDB, dbg_outs["dresT"][:, :], resT[:])
                    GPS.wait_ge(sDB.h, sDB.n)
                    return nc, dbg_outs

        TEN.wait_ge(sV.h, sV.n)
        TEN.wait_ge(sS.h, sS.n)
        for tt in range(8):
            git = tb * 8 + tt
            if tt >= 1:
                TEN.wait_ge(sV.h, out_tp[git - 1])
            for hc in range(8):
                ins = TEN.transpose(pbB[:, hc * 128:(hc + 1) * 128],
                                    resT[:, hc * 1024 + tt * 128: hc * 1024 + (tt + 1) * 128],
                                    identf[:])
            sP.inc(ins)
            VEC.wait_ge(sP.h, sP.n)
            prev_store = (14 + tt) if (tb == 0 and tt < 2) else (16 + git - 2)
            VEC.wait_ge(sOS[tt % 2].h, out_oc[prev_store])
            ins = VEC.tensor_copy(outst[:, (tt % 2) * 1024:(tt % 2 + 1) * 1024], pbB[:])
            out_tp[git] = sV.inc(ins)
            GPS.wait_ge(sV.h, sV.n)
            out_oc[16 + git] = gdma(sOS[tt % 2], d_yout[git * 128:(git + 1) * 128, :],
                                    outst[:, (tt % 2) * 1024:(tt % 2 + 1) * 1024])

    GPS.wait_ge(sOS[0].h, sOS[0].n)
    GPS.wait_ge(sOS[1].h, sOS[1].n)
    return nc, dbg_outs


_CACHE = {}


def _wkvuTz(wkvu):
    out = np.zeros((128, 16 * 128), np.float32)
    for h in range(16):
        po = (h % 2) * 64
        out[po:po + 64, h * 128:(h + 1) * 128] = wkvu[:, h * 64:(h + 1) * 64].T
    return np.ascontiguousarray(out)


def _host_inputs(L0_attn_x, L0_attn_w_kv_down, L0_attn_w_kv_up, L0_attn_w_q, L0_attn_w_out,
                 L0_ffn_router_weight, L0_ffn_e_score_bias, L0_ffn_gate_up_proj,
                 L0_ffn_down_proj, L0_ffn_shared_gate_w, L0_ffn_shared_up_w,
                 L0_ffn_shared_down_w):
    c = np.ascontiguousarray
    x = c(np.asarray(L0_attn_x, dtype=np.float32))
    shared = {
        "wkvd": c(np.asarray(L0_attn_w_kv_down, np.float32)),
        "wq": c(np.asarray(L0_attn_w_q, np.float32)),
        "wkvu": c(np.asarray(L0_attn_w_kv_up, np.float32)),
        "wkvuT": _wkvuTz(np.asarray(L0_attn_w_kv_up, np.float32)),
        "wout": c(np.asarray(L0_attn_w_out, np.float32)),
        "wrT": c(np.asarray(L0_ffn_router_weight, np.float32).T),
        "ebias": c(np.tile(np.asarray(L0_ffn_e_score_bias, np.float32)[None, :], (128, 1))),
        "wguT": c(np.transpose(np.asarray(L0_ffn_gate_up_proj, np.float32), (0, 2, 1))),
        "wdT": c(np.transpose(np.asarray(L0_ffn_down_proj, np.float32), (0, 2, 1))),
        "wsgT": c(np.asarray(L0_ffn_shared_gate_w, np.float32).T),
        "wsuT": c(np.asarray(L0_ffn_shared_up_w, np.float32).T),
        "wsdT": c(np.asarray(L0_ffn_shared_down_w, np.float32).T),
        "identb": np.eye(128, dtype=ml_dtypes.bfloat16),
        "identf": np.eye(128, dtype=np.float32),
    }
    tri = (np.arange(32)[None, :] < np.arange(32)[:, None]).astype(np.float32)
    shared["tri"] = c(np.tile(tri.reshape(1, 1024), (128, 1))).astype(ml_dtypes.bfloat16)

    in_maps, meta = [], []
    for core in range(8):
        s, role = core // 2, core % 2
        chunks = CHUNKS[role]
        rows = np.concatenate([np.arange(ch * 256, (ch + 1) * 256) for ch in chunks])
        meta.append((s, rows))
        masks = np.zeros((8, 256, 512), np.float32)
        for sl in range(8):
            q = chunks[sl] * 256 + np.arange(256)[:, None]
            k = KUNI[sl] - 512 + np.arange(512)[None, :]
            masks[sl] = np.where(k <= q, 0.0, NEG)
        m = dict(shared)
        m["xseq"] = c(x[s])
        m["xown"] = c(x[s][rows])
        m["mask"] = masks.astype(ml_dtypes.bfloat16)
        in_maps.append(m)
    return in_maps, meta


def kernel(**inputs):
    if "nc" not in _CACHE:
        _CACHE["nc"] = build_nc(dbg=0)[0]
    nc = _CACHE["nc"]
    in_maps, meta = _host_inputs(**inputs)
    res = run_bass_kernel_spmd(nc, in_maps, core_ids=list(range(8)))
    out = np.empty((4, 4096, 1024), np.float32)
    for core in range(8):
        s, rows = meta[core]
        out[s][rows] = res.results[core]["yout"]
    return out


# version 19 (final)
# speedup vs baseline: 1.0542x; 1.0542x over previous
import numpy as np
import ml_dtypes
import concourse.bass as bass
import concourse.mybir as mybir
import concourse.bass_utils as _bass_utils
from concourse.bass_utils import run_bass_kernel_spmd

if not getattr(_bass_utils, "_ldwopt_patched", False):
    _orig_run_command = _bass_utils.run_command

    def _run_command_ldwopt(cmd, **kw):
        cmd = [c.replace("--enable-ldw-opt=false", "--enable-ldw-opt=true")
               if isinstance(c, str) else c for c in cmd]
        return _orig_run_command(cmd, **kw)

    _bass_utils.run_command = _run_command_ldwopt
    _bass_utils._ldwopt_patched = True

F32 = mybir.dt.float32
BF16 = mybir.dt.bfloat16
AF = mybir.ActivationFunctionType
ALU = mybir.AluOpType
AX = mybir.AxisListType

KUNI = [4096, 3584, 3072, 2560, 2048, 1536, 1024, 512]
CHUNKS = [[15, 12, 11, 8, 7, 4, 3, 0], [14, 13, 10, 9, 6, 5, 2, 1]]
NH, RK, DM = 16, 128, 1024
NT = 2048
NQS = 16
TSEQ = 4096
NEG = -1.0e30


class Sem:
    def __init__(self, nc, name):
        self.h = nc.alloc_semaphore(name)
        self.n = 0

    def inc(self, instr, k=1):
        instr.then_inc(self.h, k)
        self.n += k
        return self.n


def build_nc(dbg=0):
    nc = bass.Bass("TRN2", target_bir_lowering=False)
    TEN, VEC, SCA, SYN, GPS = nc.tensor, nc.vector, nc.scalar, nc.sync, nc.gpsimd

    d_xseq = nc.dram_tensor("xseq", [TSEQ, DM], F32, kind="ExternalInput").ap()
    d_xown = nc.dram_tensor("xown", [NT, DM], F32, kind="ExternalInput").ap()
    d_mask = nc.dram_tensor("mask", [8, 256, 512], BF16, kind="ExternalInput").ap()
    d_idb = nc.dram_tensor("identb", [128, 128], BF16, kind="ExternalInput").ap()
    d_idf = nc.dram_tensor("identf", [128, 128], F32, kind="ExternalInput").ap()
    d_tri = nc.dram_tensor("tri", [128, 1024], BF16, kind="ExternalInput").ap()
    d_ebias = nc.dram_tensor("ebias", [128, 32], F32, kind="ExternalInput").ap()
    d_wkvd = nc.dram_tensor("wkvd", [DM, RK], F32, kind="ExternalInput").ap()
    d_wq = nc.dram_tensor("wq", [DM, DM], F32, kind="ExternalInput").ap()
    d_wkvu = nc.dram_tensor("wkvu", [RK, 2048], F32, kind="ExternalInput").ap()
    d_wkvuT = nc.dram_tensor("wkvuT", [128, 2048], F32, kind="ExternalInput").ap()
    d_wout = nc.dram_tensor("wout", [DM, DM], F32, kind="ExternalInput").ap()
    d_wrT = nc.dram_tensor("wrT", [DM, 32], F32, kind="ExternalInput").ap()
    d_wguT = nc.dram_tensor("wguT", [32, DM, 1024], F32, kind="ExternalInput").ap()
    d_wdT = nc.dram_tensor("wdT", [32, 512, DM], F32, kind="ExternalInput").ap()
    d_wsgT = nc.dram_tensor("wsgT", [DM, 1024], F32, kind="ExternalInput").ap()
    d_wsuT = nc.dram_tensor("wsuT", [DM, 1024], F32, kind="ExternalInput").ap()
    d_wsdT = nc.dram_tensor("wsdT", [1024, DM], F32, kind="ExternalInput").ap()
    d_yout = nc.dram_tensor("yout", [NT, DM], F32, kind="ExternalOutput").ap()

    dbg_outs = {}
    if dbg:
        for nm, shp in [("dlat", [128, 4096]), ("dq", [128, 16384]),
                        ("dxn2", [128, 16384]), ("dcmb", [32, 2048]),
                        ("dqt", [128, 2048]), ("dS", [128, 4096]),
                        ("dz", [128, 2048]), ("dL", [128, 128]),
                        ("dresT", [128, 8192])]:
            dbg_outs[nm] = nc.dram_tensor(nm, shp, F32, kind="ExternalOutput").ap()

    sb = nc.alloc_sbuf_tensor
    bigA = sb("bigA", [128, 16384], BF16)
    bigB = sb("bigB", [128, 16384], BF16)
    bigC = sb("bigC", [128, 16384], BF16)
    bigD = sb("bigD", [128, 8192], BF16)
    bigE = sb("bigE", [128, 8192], F32)
    latT = sb("latT", [128, 4096], BF16)
    latkm = sb("latkm", [128, 4096], BF16)
    identb = sb("identb_sb", [128, 128], BF16)
    identf = sb("identf_sb", [128, 128], F32)
    xstage = sb("xstage", [128, 2048], F32)
    xntm = sb("xntm", [128, 2048], BF16)
    wkvd_sb = sb("wkvd_sb", [128, 1024], BF16)
    wkvu_sb = sb("wkvu_sb", [128, 2048], BF16)
    wkvuT_sb = sb("wkvuT_sb", [128, 2048], BF16)
    wrT_sb = sb("wrT_sb", [128, 256], BF16)
    ebias_sb = sb("ebias_sb", [128, 32], F32)
    qt_sb = sb("qt_sb", [128, 2048], BF16)
    z_sb = sb("z_sb", [128, 2048], BF16)
    yy_sb = sb("yy_sb", [128, 2048], BF16)
    xownst = sb("xownst", [128, 2048], F32)
    st_sums = sb("st_sums", [128, 32], F32)
    st_sqs = sb("st_sqs", [128, 32], F32)
    st_nms = sb("st_nms", [128, 32], F32)
    st_e2s = sb("st_e2s", [128, 32], F32)
    st_vars = sb("st_vars", [128, 32], F32)
    st_stds = sb("st_stds", [128, 32], F32)
    st_rstds = sb("st_rstds", [128, 32], F32)
    st_mx = sb("st_mx", [128, 1], F32)
    st_mlat = sb("st_mlat", [128, 1], F32)
    st_mlatb = sb("st_mlatb", [128, 1], BF16)
    eps_sb = sb("eps_sb", [128, 1], F32)
    scl8 = sb("scl8", [128, 1], F32)
    one_sb = sb("one_sb", [128, 1], F32)
    mxp = sb("mxp", [128, 8], F32)
    nmx8 = sb("nmx8", [128, 16], F32)
    Lbuf = sb("Lbuf", [128, 128], F32)
    Lsum = sb("Lsum", [128, 16], F32)
    Linv = sb("Linv", [128, 16], F32)

    pbS = nc.alloc_psum_tensor("pbS", [128, 2048], F32)
    pbB = nc.alloc_psum_tensor("pbB", [128, 1024], F32)
    pbt = nc.alloc_psum_tensor("pbt", [128, 2048], BF16)

    sWA = Sem(nc, "sWA")
    sWG = [Sem(nc, "sWG0"), Sem(nc, "sWG1")]
    sWD = [Sem(nc, f"sWD{i}") for i in range(4)]
    sXC = Sem(nc, "sXC")
    sXS = [Sem(nc, "sXS0"), Sem(nc, "sXS1")]
    sXO = [Sem(nc, "sXO0"), Sem(nc, "sXO1")]
    sZB = [Sem(nc, "sZB0"), Sem(nc, "sZB1")]
    sCM = Sem(nc, "sCM")
    sOS = [Sem(nc, "sOS0"), Sem(nc, "sOS1")]
    sDB = Sem(nc, "sDB")
    sP = Sem(nc, "sP")
    sV = Sem(nc, "sV")
    sS = Sem(nc, "sS")
    sBAR = Sem(nc, "sBAR")

    xnstage = bigE[:, 0:2048].bitcast(BF16)
    y_sb = yy_sb[:, 0:1024]
    yT_sb = yy_sb[:, 1024:2048]
    absq = yy_sb[:, 0:2048]
    Ssb2 = bigE[:, 2048:4096].bitcast(BF16)
    outst = xstage
    tri_sb = y_sb
    _rsc = yT_sb[:, 0:1024].bitcast(F32)
    cmbT = wkvuT_sb[0:32, 0:2048]
    Ssb = bigE[:, 0:2048].bitcast(BF16)
    Ssb3 = bigE[:, 4096:6144].bitcast(BF16)
    Ssb4 = bigE[:, 6144:8192].bitcast(BF16)
    resT = bigE
    smid = latT[:, 0:4096]
    cbc_sb = latkm[:, 0:1024]
    hsilu = latkm[:, 1024:2048]
    cmb_bf = latkm[:, 2048:2176]
    xn1ownT = bigA
    STsb = bigA
    xn2T = bigA
    qT = bigB
    wq_sb = bigC[:, 0:8192]
    wout_sb = bigC[:, 8192:16384]
    mask_sb = bigD

    def wgu_slot(s):
        return bigB[:, s * 8192:(s + 1) * 8192]

    def wd_slot(s):
        return bigC[:, s * 4096:(s + 1) * 4096]

    def hm_slot(ei, gp):
        off = ei * 4096 + gp * 1024
        return bigD[:, off:off + 512]

    VEC.memset(eps_sb[:], 1e-5)
    VEC.memset(scl8[:], 0.125)
    VEC.memset(one_sb[:], 1.0)
    VEC.memset(Lbuf[:], 0.0)
    sV.inc(VEC.drain())

    def gdma(sem, out, in_):
        GPS.dma_start(out=out, in_=in_).then_inc(sem.h, 16)
        sem.n += 16
        return sem.n

    def xdma(sem, out, in_):
        SYN.dma_start(out=out, in_=in_).then_inc(sem.h, 16)
        sem.n += 16
        return sem.n

    def wload(sem, dst2d, src2d, g, n):
        dst3 = dst2d.rearrange("p (g n) -> p g n", g=g)
        src3 = src2d.rearrange("(g p) n -> p g n", p=128)
        return gdma(sem, dst3, src3)

    w_kvd = wload(sZB[0], wkvd_sb[:], d_wkvd, 8, 128)
    w_q = wload(sZB[1], wq_sb, d_wq, 8, 1024)
    gdma(sCM, wkvu_sb[:], d_wkvu[:, :])
    w_kvu = gdma(sCM, wkvuT_sb[:], d_wkvuT[:, :])
    wload(sWA, wout_sb, d_wout, 8, 1024)
    w_all = wload(sWA, wrT_sb[:], d_wrT, 8, 32)

    xdma(sXC, identb[:], d_idb[:, :])
    xdma(sXC, identf[:], d_idf[:, :])
    xdma(sXC, tri_sb[:], d_tri[:, :])
    xdma(sXC, ebias_sb[:], d_ebias[:, :])
    _mdst = mask_sb[:, :].rearrange("p (s qq k) -> p s qq k", s=8, qq=2)
    _msrc = d_mask.rearrange("s (qq p) k -> p s qq k", p=128)
    x_const = xdma(sXC, _mdst, _msrc)

    SCA.preload_act_table(AF.Sqrt)

    slot_gate = {0: (0, 0), 1: (0, 0)}
    tp_gate = {0: ("V", 0), 1: ("V", 0)}

    def wait_tp(eng, g):
        if g[1]:
            eng.wait_ge(sS.h if g[0] == "S" else sV.h, g[1])

    def emit_ln_batch(srcs, dsts3, post_cb=None):
        nt = len(srcs)
        for it, src in enumerate(srcs):
            slot = it % 2
            gv, gs = slot_gate[slot]
            if gv:
                SYN.wait_ge(sV.h, gv)
            if gs:
                SYN.wait_ge(sS.h, gs)
            n = xdma(sXS[slot], xstage[:, slot * 1024:(slot + 1) * 1024], src)
            VEC.wait_ge(sXS[slot].h, n)
            SCA.wait_ge(sXS[slot].h, n)
            ins = VEC.reduce_sum(st_sums[:, it:it + 1], xstage[:, slot * 1024:(slot + 1) * 1024], axis=AX.X)
            gv = sV.inc(ins)
            SCA.activation(out=pbB[:, 0:1024], in_=xstage[:, slot * 1024:(slot + 1) * 1024],
                           func=AF.Square, accum_out=st_sqs[:, it:it + 1])
            gs = sS.inc(SCA.drain())
            slot_gate[slot] = (gv, gs)
        VEC.wait_ge(sS.h, sS.n)
        VEC.drain()
        VEC.tensor_scalar(out=st_nms[:, 0:nt], in0=st_sums[:, 0:nt], scalar1=-1.0 / DM, scalar2=None, op0=ALU.mult)
        VEC.tensor_scalar(out=st_e2s[:, 0:nt], in0=st_sqs[:, 0:nt], scalar1=1.0 / DM, scalar2=None, op0=ALU.mult)
        VEC.drain()
        VEC.tensor_tensor(out=st_vars[:, 0:nt], in0=st_nms[:, 0:nt], in1=st_nms[:, 0:nt], op=ALU.mult)
        VEC.drain()
        VEC.tensor_tensor(out=st_vars[:, 0:nt], in0=st_e2s[:, 0:nt], in1=st_vars[:, 0:nt], op=ALU.subtract)
        sV.inc(VEC.drain())
        SCA.wait_ge(sV.h, sV.n)
        SCA.activation(out=st_stds[:, 0:nt], in_=st_vars[:, 0:nt], func=AF.Sqrt,
                       bias=eps_sb[:], scale=one_sb[:])
        sS.inc(SCA.drain())
        VEC.wait_ge(sS.h, sS.n)
        VEC.reciprocal(st_rstds[:, 0:nt], st_stds[:, 0:nt])
        stats_v = sV.inc(VEC.drain())
        for it, (src, dst3) in enumerate(zip(srcs, dsts3)):
            slot = it % 2
            gv, gs = slot_gate[slot]
            if gv:
                SYN.wait_ge(sV.h, gv)
            if gs:
                SYN.wait_ge(sS.h, gs)
            n = xdma(sXS[slot], xstage[:, slot * 1024:(slot + 1) * 1024], src)
            VEC.wait_ge(sXS[slot].h, n)
            ins = VEC.tensor_scalar(out=xntm[:, slot * 1024:(slot + 1) * 1024],
                                    in0=xstage[:, slot * 1024:(slot + 1) * 1024],
                                    scalar1=st_nms[:, it:it + 1], scalar2=st_rstds[:, it:it + 1],
                                    op0=ALU.add, op1=ALU.mult)
            gv = sV.inc(ins)
            slot_gate[slot] = (gv, 0)
            TEN.wait_ge(sV.h, gv)
            half = slot * 1024
            wait_tp(TEN, tp_gate[slot])
            for cc in range(8):
                ins = TEN.transpose(pbt[:, half + cc * 128: half + (cc + 1) * 128],
                                    xntm[:, slot * 1024 + cc * 128: slot * 1024 + (cc + 1) * 128],
                                    identb[:])
            sP.inc(ins)
            VEC.wait_ge(sP.h, sP.n)
            ins = VEC.tensor_copy(dst3, pbt[:, half:half + 1024].rearrange("p (cc k) -> p cc k", cc=8))
            tp_gate[slot] = ("V", sV.inc(ins))
            if post_cb is not None:
                post_cb(it)

    TEN.wait_ge(sXC.h, x_const)
    tags = []

    def a1_cb(it):
        if it % 4 != 3:
            return
        g = it // 4
        TEN.wait_ge(sV.h, sV.n)
        if g == 0:
            TEN.wait_ge(sZB[0].h, w_kvd)
        ph = (g % 2) * 512
        for cc in range(8):
            ins = TEN.matmul(pbS[:, ph:ph + 512],
                             wkvd_sb[:, cc * 128:(cc + 1) * 128],
                             xnstage[:, cc * 512:(cc + 1) * 512],
                             start=(cc == 0), stop=(cc == 7))
        sP.inc(ins)
        VEC.wait_ge(sP.h, sP.n)
        ins = VEC.tensor_copy(latT[:, g * 512:(g + 1) * 512], pbS[:, ph:ph + 512])
        sV.inc(ins)

    srcs = [d_xseq[it * 128:(it + 1) * 128, :] for it in range(32)]
    dsts = [xnstage[:].rearrange("p (cc k) -> p cc k", cc=8)[:, :, (it % 4) * 128:(it % 4) * 128 + 128]
            for it in range(32)]
    emit_ln_batch(srcs, dsts, post_cb=a1_cb)

    TEN.wait_ge(sV.h, sV.n)
    lk_ev = {}
    for b in range(4):
        half = (b % 2) * 1024
        if b >= 2:
            TEN.wait_ge(sV.h, lk_ev[b - 2])
        for j in range(8):
            kc = b * 8 + j
            ins = TEN.transpose(pbt[:, half + j * 128: half + (j + 1) * 128],
                                latT[:, kc * 128:(kc + 1) * 128], identb[:])
        sP.inc(ins)
        VEC.wait_ge(sP.h, sP.n)
        ins = VEC.tensor_copy(latkm[:, b * 1024:(b + 1) * 1024], pbt[:, half:half + 1024])
        lk_ev[b] = sV.inc(ins)

    VEC.reduce_max(st_mlat[:], latT[:], axis=AX.X, apply_absolute_value=True)
    sV.inc(VEC.drain())
    VEC.tensor_copy(st_mlatb[:], st_mlat[:])
    sV.inc(VEC.drain())
    TEN.wait_ge(sV.h, sV.n)
    tp_gate[0] = ("V", sV.n)
    tp_gate[1] = ("V", sV.n)

    srcs = [d_xown[it * 128:(it + 1) * 128, :] for it in range(16)]
    dsts = [xn1ownT[:].rearrange("p (cc k) -> p cc k", cc=8)[:, :, it * 128:(it + 1) * 128]
            for it in range(16)]
    emit_ln_batch(srcs, dsts)

    TEN.wait_ge(sV.h, sV.n)
    TEN.wait_ge(sZB[1].h, w_q)
    qp_ev = {}
    j = 0
    for hdc in range(8):
        for tq in range(4):
            ph = (j % 2) * 512
            if j >= 2:
                TEN.wait_ge(sV.h, qp_ev[j - 2])
            for cc in range(8):
                ins = TEN.matmul(pbB[:, ph:ph + 512],
                                 wq_sb[:, cc * 1024 + hdc * 128: cc * 1024 + (hdc + 1) * 128],
                                 xn1ownT[:, cc * 2048 + tq * 512: cc * 2048 + (tq + 1) * 512],
                                 start=(cc == 0), stop=(cc == 7))
            sP.inc(ins)
            VEC.wait_ge(sP.h, sP.n)
            ins = VEC.tensor_copy(qT[:, hdc * 2048 + tq * 512: hdc * 2048 + (tq + 1) * 512],
                                  pbB[:, ph:ph + 512])
            qp_ev[j] = sV.inc(ins)
            j += 1
    wq_region_free = sP.n

    if dbg == 1:
        VEC.wait_ge(sV.h, sV.n)
        sV.inc(VEC.drain())
        SYN.wait_ge(sV.h, sV.n)
        GPS.wait_ge(sV.h, sV.n)
        gdma(sDB, dbg_outs["dlat"][:, :], latT[:])
        gdma(sDB, dbg_outs["dq"][:, :], qT[:])
    if dbg == 1:
        GPS.wait_ge(sDB.h, sDB.n)
        return nc, dbg_outs

    SCA.preload_act_table(AF.Exp)
    TEN.wait_ge(sCM.h, w_kvu)

    res_tt_done = {}
    o_count = {}
    exp_pair_done = [0, 0]

    for qs in range(NQS):
        slot = qs // 2
        kuni = KUNI[slot]
        nk5 = kuni // 512
        nkc = kuni // 128

        if qs >= 2:
            SYN.wait_ge(sV.h, res_tt_done[qs - 2])
        xo_n = xdma(sXO[qs % 2], xownst[:, (qs % 2) * 1024:(qs % 2 + 1) * 1024],
                    d_xown[qs * 128:(qs + 1) * 128, :])

        TEN.wait_ge(sV.h, sV.n)
        qt_ev = {}
        for hb in range(4):
            ph = (hb % 2) * 512
            if hb >= 2:
                TEN.wait_ge(sV.h, qt_ev[hb - 2])
            for hh in range(4):
                h = hb * 4 + hh
                ins = TEN.matmul(pbB[:, ph + hh * 128: ph + (hh + 1) * 128],
                                 wkvuT_sb[:, h * 128:(h + 1) * 128],
                                 qT[:, (h // 2) * 2048 + qs * 128:(h // 2) * 2048 + (qs + 1) * 128],
                                 start=True, stop=True)
            sP.inc(ins)
            VEC.wait_ge(sP.h, sP.n)
            ins = VEC.tensor_scalar(out=qt_sb[:, hb * 512:(hb + 1) * 512], in0=pbB[:, ph:ph + 512],
                                    scalar1=0.125, scalar2=None, op0=ALU.mult)
            qt_ev[hb] = sV.inc(ins)
            SCA.wait_ge(sV.h, qt_ev[hb])
            insA = SCA.activation(out=absq[:, hb * 512:(hb + 1) * 512],
                                  in_=qt_sb[:, hb * 512:(hb + 1) * 512], func=AF.Abs)
            abs_cnt = sS.inc(insA)
        if qs == NQS - 1:
            qT_region_free = sP.n

        if dbg == 21 and qs == 0:
            GPS.wait_ge(sV.h, sV.n)
            gdma(sDB, dbg_outs["dqt"][:, :], qt_sb[:])
            GPS.wait_ge(sDB.h, sDB.n)
            return nc, dbg_outs

        VEC.wait_ge(sS.h, sS.n)
        VEC.memset(Lbuf[:], 0.0)
        sV.inc(VEC.drain())
        SCA.wait_ge(sV.h, sV.n)
        exp_done_cnt = {}

        tp_done = {}
        av_gate = [0]

        def emit_tp(h_t, exp_cnt):
            TEN.wait_ge(sS.h, exp_cnt)
            ng8 = (nkc + 7) // 8
            tp_ev = {}
            for b8 in range(ng8):
                half = (b8 % 2) * 1024
                g = tp_ev[b8 - 2] if b8 >= 2 else tp_gate[b8 % 2]
                wait_tp(TEN, g)
                cnt = min(8, nkc - b8 * 8)
                Sbuf = (Ssb, Ssb2, Ssb3, Ssb4)[h_t % 4]
                for jj in range(cnt):
                    kc = b8 * 8 + jj
                    ins2 = TEN.transpose(pbt[:, half + jj * 128: half + (jj + 1) * 128],
                                         Sbuf[:, kc * 128:(kc + 1) * 128], identb[:])
                sP.inc(ins2)
                dst = STsb[:].rearrange("p (kc r) -> p kc r", kc=32, r=512)[
                    :, b8 * 8: b8 * 8 + cnt, (h_t % 4) * 128:(h_t % 4 + 1) * 128]
                src = pbt[:, half: half + cnt * 128].rearrange("p (kc r) -> p kc r", kc=cnt)
                VEC.wait_ge(sP.h, sP.n)
                ins2 = VEC.tensor_copy(dst, src)
                tp_ev[b8] = ("V", sV.inc(ins2))
                tp_gate[b8 % 2] = tp_ev[b8]
            tp_done[h_t] = tp_ev[ng8 - 1][1]

        def emit_av(hg, part):
            nk2 = nkc // 2
            lo, hi = (0, nk2) if part == 0 else (nk2, nkc)
            if part == 0:
                TEN.wait_ge(sV.h, max(tp_done[hg * 4], tp_done[hg * 4 + 1],
                                      tp_done[hg * 4 + 2], tp_done[hg * 4 + 3], av_gate[0]))
            for kc in range(lo, hi):
                ins2 = TEN.matmul(pbB[:, 0:512], latkm[:, kc * 128:(kc + 1) * 128],
                                  STsb[:, kc * 512:(kc + 1) * 512],
                                  start=(kc == 0), stop=(kc == nkc - 1))
            sP.inc(ins2)
            if part == 1:
                VEC.wait_ge(sP.h, sP.n)
                ins2 = VEC.tensor_copy(z_sb[:, hg * 512:(hg + 1) * 512], pbB[:, 0:512])
                av_gate[0] = sV.inc(ins2)

        TEN.wait_ge(sS.h, abs_cnt)
        for h in range(16):
            ins = TEN.matmul(pbB[:, h:h + 1], absq[:, h * 128:(h + 1) * 128],
                             st_mlatb[:], start=True, stop=True)
        sP.inc(ins)
        VEC.wait_ge(sP.h, sP.n)
        VEC.tensor_scalar(out=nmx8[:, 0:16], in0=pbB[:, 0:16], scalar1=-1.0, scalar2=None, op0=ALU.mult)
        nmx_ready = sV.inc(VEC.drain())
        SCA.wait_ge(sV.h, nmx_ready)

        ngrp = (nk5 + 1) // 2
        exp_last = {}
        for hp in range(9):
            todo = []
            if hp >= 1:
                pA = 2 * (hp - 1)
                todo.append(("tp", pA))
                todo.append(("tp", pA + 1))
                if hp >= 2 and hp % 2 == 0:
                    todo.append(("av", (hp - 2) // 2, 0))
                    todo.append(("av", (hp - 2) // 2, 1))

            def run_blob():
                job = todo.pop(0)
                if job[0] == "tp":
                    emit_tp(job[1], exp_done_cnt[job[1]])
                else:
                    emit_av(job[1], job[2])

            if hp == 8:
                while todo:
                    run_blob()
                continue
            hA, hB = 2 * hp, 2 * hp + 1
            p2 = {hA: {}, hB: {}}
            mask_cnts = {}
            for kj in range(nk5):
                last = (kj == nk5 - 1)
                for h in (hA, hB):
                    pp = (h % 2) * 1024
                    sl = pp + (kj % 2) * 512
                    if kj >= 2:
                        TEN.wait_ge(sS.h, p2[h][(kj - 2) // 2])
                    else:
                        TEN.wait_ge(sS.h, exp_pair_done[h % 2])
                    if last:
                        mslice = mask_sb[:, slot * 1024 + (qs % 2) * 512: slot * 1024 + (qs % 2) * 512 + 512]
                        TEN.matmul(pbS[:, sl:sl + 512], identb[:, 0:128], mslice,
                                   start=True, stop=False)
                    ins = TEN.matmul(pbS[:, sl:sl + 512],
                                     qt_sb[:, h * 128:(h + 1) * 128],
                                     latT[:, kj * 512:(kj + 1) * 512],
                                     start=(not last), stop=True)
                    pe_mm = sP.inc(ins)
                    if kj % 2 == 1 or last:
                        grp = kj // 2
                        glen = 1024 if kj % 2 == 1 else 512
                        SCA.wait_ge(sP.h, pe_mm)
                        Sout = (Ssb, Ssb2, Ssb3, Ssb4)[h % 4]
                        ins = SCA.activation(out=Sout[:, grp * 1024: grp * 1024 + glen],
                                             in_=pbS[:, pp:pp + glen],
                                             func=AF.Exp, scale=one_sb[:], bias=nmx8[:, h:h + 1],
                                             accum_out=Lbuf[:, h * 8 + grp: h * 8 + grp + 1])
                        p2[h][grp] = sS.inc(ins)
                        exp_last[h] = p2[h][grp]
                if kj % 2 == 1 and not last and todo:
                    run_blob()
            while todo:
                run_blob()
            for h in (hA, hB):
                exp_pair_done[h % 2] = exp_last[h]
                exp_done_cnt[h] = exp_last[h]

        VEC.wait_ge(sS.h, sS.n)
        VEC.tensor_reduce(Lsum[:], Lbuf[:].rearrange("p (h k) -> p h k", h=16), axis=AX.X, op=ALU.add)
        VEC.drain()
        VEC.reciprocal(Linv[:], Lsum[:])
        sV.inc(VEC.drain())

        TEN.wait_ge(sV.h, sV.n)
        for h in range(NH):
            ins = TEN.matmul(pbB[:, h * 64:(h + 1) * 64],
                             z_sb[:, h * 128:(h + 1) * 128],
                             wkvu_sb[:, 1024 + h * 64: 1024 + (h + 1) * 64],
                             start=True, stop=True)
        sP.inc(ins)
        VEC.wait_ge(sP.h, sP.n)
        linv_b = Linv[:].unsqueeze(2).broadcast_to([128, 16, 64])
        ins = VEC.tensor_tensor(out=y_sb[:].rearrange("p (h d) -> p h d", h=16),
                                in0=pbB[:].rearrange("p (h d) -> p h d", h=16),
                                in1=linv_b, op=ALU.mult)
        sV.inc(ins)
        TEN.wait_ge(sV.h, sV.n)
        for cc in range(8):
            ins = TEN.transpose(pbt[:, cc * 128:(cc + 1) * 128],
                                y_sb[:, cc * 128:(cc + 1) * 128], identb[:])
        sP.inc(ins)
        VEC.wait_ge(sP.h, sP.n)
        ins = VEC.tensor_copy(yT_sb[:], pbt[:, 0:1024])
        sV.inc(ins)
        TEN.wait_ge(sV.h, sV.n)
        if qs == 0:
            TEN.wait_ge(sWA.h, w_all)
        for cc in range(2):
            for hdc in range(8):
                ins = TEN.matmul(pbB[:, cc * 512:(cc + 1) * 512],
                                 yT_sb[:, hdc * 128:(hdc + 1) * 128],
                                 wout_sb[:, hdc * 1024 + cc * 512: hdc * 1024 + cc * 512 + 512],
                                 start=(hdc == 0), stop=(hdc == 7))
        sP.inc(ins)
        VEC.wait_ge(sP.h, sP.n)
        VEC.wait_ge(sXO[qs % 2].h, xo_n)
        if qs >= 2:
            VEC.wait_ge(sOS[qs % 2].h, o_count[qs - 2])
        ins = VEC.tensor_tensor(out=outst[:, (qs % 2) * 1024:(qs % 2 + 1) * 1024],
                                in0=pbB[:],
                                in1=xownst[:, (qs % 2) * 1024:(qs % 2 + 1) * 1024], op=ALU.add)
        res_tt_done[qs] = sV.inc(ins)
        if qs == NQS - 1:
            wout_region_free = sP.n
        GPS.wait_ge(sV.h, sV.n)
        o_count[qs] = gdma(sOS[qs % 2], d_yout[qs * 128:(qs + 1) * 128, :],
                           outst[:, (qs % 2) * 1024:(qs % 2 + 1) * 1024])
        if dbg == 24 and qs == 0:
            GPS.wait_ge(sOS[0].h, sOS[0].n)
            return nc, dbg_outs

    if dbg == 2:
        GPS.wait_ge(sOS[0].h, sOS[0].n)
        GPS.wait_ge(sOS[1].h, sOS[1].n)
        return nc, dbg_outs

    sBAR.inc(TEN.nop())
    sBAR.inc(VEC.drain())
    sBAR.inc(SCA.drain())
    for eng in (TEN, VEC, SCA, SYN, GPS):
        eng.wait_ge(sBAR.h, 3)

    SCA.preload_act_table(AF.Sqrt)
    SYN.wait_ge(sOS[0].h, sOS[0].n)
    SYN.wait_ge(sOS[1].h, sOS[1].n)
    srcs = [d_yout[it * 128:(it + 1) * 128, :] for it in range(16)]
    dsts = [xn2T[:].rearrange("p (cc k) -> p cc k", cc=8)[:, :, it * 128:(it + 1) * 128]
            for it in range(16)]
    emit_ln_batch(srcs, dsts)

    SCA.preload_act_table(AF.Exp)
    VEC.memset(cmb_bf[:], 0.0)
    sV.inc(VEC.drain())
    TEN.wait_ge(sV.h, sV.n)
    TEN.wait_ge(sS.h, sS.n)
    zF = z_sb[:, 0:2048].bitcast(F32)
    qF = qt_sb[:, 0:2048].bitcast(F32)
    A, B = zF[:, 0:512], zF[:, 512:1024]
    C, D = qF[:, 0:512], qF[:, 512:1024]
    m16 = _rsc[:, 0:16]
    s16 = _rsc[:, 16:32]
    l16 = _rsc[:, 32:48]
    s16b = _rsc[:, 48:64]
    r16b = _rsc[:, 64:80]
    gtb = Ssb[:, 0:2048]
    eqb = Ssb[:, 2048:4096]
    for it in range(16):
        for cc in range(8):
            ins = TEN.matmul(pbB[:, it * 32:(it + 1) * 32],
                             xn2T[:, cc * 2048 + it * 128: cc * 2048 + (it + 1) * 128],
                             wrT_sb[:, cc * 32:(cc + 1) * 32],
                             start=(cc == 0), stop=(cc == 7))
    sP.inc(ins)
    VEC.wait_ge(sP.h, sP.n)
    VEC.tensor_copy(A, pbB[:, 0:512])
    sV.inc(VEC.drain())
    VEC.tensor_reduce(m16, A.rearrange("p (g e) -> p g e", g=16), axis=AX.X, op=ALU.max, negate=True)
    VEC.drain()
    mb = m16.unsqueeze(2).broadcast_to([128, 16, 32])
    VEC.tensor_tensor(out=B.rearrange("p (g e) -> p g e", g=16), in0=A.rearrange("p (g e) -> p g e", g=16),
                      in1=mb, op=ALU.add)
    sV.inc(VEC.drain())
    SCA.wait_ge(sV.h, sV.n)
    ins = SCA.activation(out=C, in_=B, func=AF.Exp)
    sS.inc(ins)
    VEC.wait_ge(sS.h, sS.n)
    VEC.tensor_reduce(s16, C.rearrange("p (g e) -> p g e", g=16), axis=AX.X, op=ALU.add)
    VEC.drain()
    VEC.reciprocal(l16, s16)
    VEC.drain()
    lb = l16.unsqueeze(2).broadcast_to([128, 16, 32])
    VEC.tensor_tensor(out=D.rearrange("p (g e) -> p g e", g=16), in0=C.rearrange("p (g e) -> p g e", g=16),
                      in1=lb, op=ALU.mult)
    VEC.drain()
    eb = ebias_sb[:].unsqueeze(1).broadcast_to([128, 16, 32])
    VEC.tensor_tensor(out=B.rearrange("p (g e) -> p g e", g=16), in0=D.rearrange("p (g e) -> p g e", g=16),
                      in1=eb, op=ALU.add)
    VEC.drain()
    for bt in range(8):
        corr2 = B[:, bt * 64:(bt + 1) * 64].rearrange("p (g e) -> p g e", g=2)
        cj = corr2.unsqueeze(2).broadcast_to([128, 2, 32, 32])
        ce = corr2.unsqueeze(3).broadcast_to([128, 2, 32, 32])
        VEC.tensor_tensor(out=gtb.rearrange("p (g e j) -> p g e j", g=2, e=32, j=32), in0=cj, in1=ce, op=ALU.is_gt)
        VEC.tensor_tensor(out=eqb.rearrange("p (g e j) -> p g e j", g=2, e=32, j=32), in0=cj, in1=ce, op=ALU.is_equal)
        VEC.drain()
        trib = tri_sb[:].unsqueeze(1).broadcast_to([128, 2, 1024]).rearrange("p g (e j) -> p g e j", e=32)
        VEC.tensor_tensor(out=eqb.rearrange("p (g e j) -> p g e j", g=2, e=32, j=32),
                          in0=eqb.rearrange("p (g e j) -> p g e j", g=2, e=32, j=32), in1=trib, op=ALU.mult)
        VEC.drain()
        VEC.tensor_tensor(out=gtb, in0=gtb, in1=eqb, op=ALU.add)
        VEC.drain()
        VEC.tensor_reduce(A[:, bt * 64:(bt + 1) * 64].rearrange("p (g e) -> p g e", g=2),
                          gtb.rearrange("p (g e j) -> p (g e) j", g=2, e=32, j=32), axis=AX.X, op=ALU.add)
        VEC.drain()
    VEC.tensor_scalar(out=C, in0=A, scalar1=5.5, scalar2=None, op0=ALU.is_lt)
    VEC.drain()
    VEC.tensor_tensor(out=A, in0=D, in1=C, op=ALU.mult)
    VEC.drain()
    VEC.tensor_reduce(s16b, A.rearrange("p (g e) -> p g e", g=16), axis=AX.X, op=ALU.add)
    VEC.drain()
    VEC.tensor_scalar(out=s16b, in0=s16b, scalar1=1e-12, scalar2=None, op0=ALU.max)
    VEC.drain()
    VEC.reciprocal(r16b, s16b)
    VEC.drain()
    rb = r16b.unsqueeze(2).broadcast_to([128, 16, 32])
    VEC.tensor_tensor(out=D.rearrange("p (g e) -> p g e", g=16), in0=A.rearrange("p (g e) -> p g e", g=16),
                      in1=rb, op=ALU.mult)
    sV.inc(VEC.drain())
    for it in range(16):
        ins = VEC.tensor_copy(cmb_bf[:, 0:32], D[:, it * 32:(it + 1) * 32])
        sV.inc(ins)
        TEN.wait_ge(sV.h, sV.n)
        ins = TEN.transpose(pbt[:, 0:128], cmb_bf[:, 0:128], identb[:])
        sP.inc(ins)
        VEC.wait_ge(sP.h, sP.n)
        ins = VEC.tensor_copy(cmbT[:, it * 128:(it + 1) * 128], pbt[0:32, 0:128])
        sV.inc(ins)
        TEN.wait_ge(sV.h, sV.n)

    if dbg == 3:
        VEC.wait_ge(sV.h, sV.n)
        sV.inc(VEC.drain())
        SYN.wait_ge(sV.h, sV.n)
        GPS.wait_ge(sV.h, sV.n)
        gdma(sDB, dbg_outs["dxn2"][:, :], xn2T[:])
        gdma(sDB, dbg_outs["dcmb"][:, :], cmbT[:, :])
        GPS.wait_ge(sDB.h, sDB.n)
        return nc, dbg_outs

    SCA.preload_act_table(AF.Silu)
    gu_free = [qT_region_free, qT_region_free]
    wd_free = [wq_region_free, wq_region_free, wout_region_free, wout_region_free]
    cbc_gate = [0]
    gu_ch = {"silu": [0, 0], "tt1": [0, 0], "tt2": [0, 0], "dq02": 0}
    dn_ev = [0, 0]
    rin_tp = {}
    out_oc = dict(o_count)
    out_tp = {}

    for tb in range(2):
        for tt in range(8):
            git = tb * 8 + tt
            slot = tt % 2
            if (tb, tt) in ((0, 0), (0, 1)):
                gv, gs = slot_gate[slot]
                if gv:
                    SYN.wait_ge(sV.h, gv)
                if gs:
                    SYN.wait_ge(sS.h, gs)
            else:
                pg = git - 2
                SYN.wait_ge(sV.h, rin_tp[pg])
            if tb == 1 and tt < 2:
                SYN.wait_ge(sOS[tt % 2].h, out_oc[16 + 6 + tt])
            rs_n = xdma(sXS[slot], xstage[:, slot * 1024:(slot + 1) * 1024],
                        d_yout[git * 128:(git + 1) * 128, :])
            TEN.wait_ge(sXS[slot].h, rs_n)
            TEN.wait_ge(sV.h, sV.n)
            if tt == 0:
                TEN.wait_ge(sS.h, sS.n)
            for hc in range(8):
                ins = TEN.transpose(pbB[:, hc * 128:(hc + 1) * 128],
                                    xstage[:, slot * 1024 + hc * 128: slot * 1024 + (hc + 1) * 128],
                                    identf[:])
            sP.inc(ins)
            VEC.wait_ge(sP.h, sP.n)
            dst = resT[:].rearrange("p (hc t) -> p hc t", hc=8)[:, :, tt * 128:(tt + 1) * 128]
            ins = VEC.tensor_copy(dst, pbB[:].rearrange("p (hc t) -> p hc t", hc=8))
            rin_tp[git] = sV.inc(ins)

        for tkb in range(2):
            T0 = tb * 1024 + tkb * 512
            GPS.wait_ge(sP.h, max(gu_free[0], gu_free[1], wd_free[0], wd_free[1]))
            n_g0 = wload(sWG[0], wgu_slot(0), d_wsgT, 8, 1024)
            n_g1 = wload(sWG[1], wgu_slot(1), d_wsuT, 8, 1024)
            n_d0 = wload(sWD[0], wd_slot(0), d_wsdT[0:512, :], 4, 1024)
            n_d1 = wload(sWD[1], wd_slot(1), d_wsdT[512:1024, :], 4, 1024)

            TEN.wait_ge(sV.h, sV.n)
            TEN.wait_ge(sWG[0].h, n_g0)
            TEN.wait_ge(sWG[1].h, n_g1)
            TEN.wait_ge(sWD[0].h, n_d0)
            TEN.wait_ge(sWD[1].h, n_d1)
            sh_silu = [0, 0]
            sh_tt = []
            for ic in range(8):
                gsl = (ic % 2) * 512
                if ic >= 2:
                    TEN.wait_ge(sS.h, sh_silu[ic % 2])
                for cc in range(8):
                    ins = TEN.matmul(pbB[:, gsl:gsl + 512],
                                     wgu_slot(0)[:, cc * 1024 + ic * 128: cc * 1024 + (ic + 1) * 128],
                                     xn2T[:, cc * 2048 + T0: cc * 2048 + T0 + 512],
                                     start=(cc == 0), stop=(cc == 7))
                gmm = sP.inc(ins)
                if ic >= 2:
                    TEN.wait_ge(sV.h, sh_tt[ic - 2])
                for cc in range(8):
                    ins2 = TEN.matmul(pbS[:, gsl:gsl + 512],
                                      wgu_slot(1)[:, cc * 1024 + ic * 128: cc * 1024 + (ic + 1) * 128],
                                      xn2T[:, cc * 2048 + T0: cc * 2048 + T0 + 512],
                                      start=(cc == 0), stop=(cc == 7))
                umm = sP.inc(ins2)
                SCA.wait_ge(sP.h, gmm)
                if ic >= 2:
                    SCA.wait_ge(sV.h, sh_tt[ic - 2])
                ins = SCA.activation(out=hsilu[:, gsl:gsl + 512], in_=pbB[:, gsl:gsl + 512], func=AF.Silu)
                sh_silu[ic % 2] = sS.inc(ins)
                VEC.wait_ge(sP.h, umm)
                VEC.wait_ge(sS.h, sh_silu[ic % 2])
                ins = VEC.tensor_tensor(out=smid[:, ic * 512:(ic + 1) * 512],
                                        in0=pbS[:, gsl:gsl + 512], in1=hsilu[:, gsl:gsl + 512], op=ALU.mult)
                sh_tt.append(sV.inc(ins))
            gu_ch["silu"] = sh_silu[:]
            gu_ch["tt1"] = [sh_tt[6], sh_tt[7]]
            gu_ch["tt2"] = [sh_tt[6], sh_tt[7]]
            gu_free[0] = gu_free[1] = sP.n
            for qi, qd in enumerate((1, 0, 3, 2)):
                qh = (qd % 2) * 1024
                if qi == 0:
                    TEN.wait_ge(sV.h, dn_ev[qd % 2])
                elif qi == 1:
                    TEN.wait_ge(sV.h, max(dn_ev[qd % 2], sh_tt[7]))
                else:
                    TEN.wait_ge(sV.h, dn_ev[qd % 2])
                for ic in range(8):
                    if qi == 0:
                        TEN.wait_ge(sV.h, sh_tt[ic])
                    for m in range(2):
                        hc = qd * 2 + m
                        ins = TEN.matmul(pbS[:, qh + m * 512: qh + (m + 1) * 512],
                                         wd_slot(ic // 4)[:, (ic % 4) * 1024 + hc * 128: (ic % 4) * 1024 + (hc + 1) * 128],
                                         smid[:, ic * 512:(ic + 1) * 512],
                                         start=(ic == 0), stop=(ic == 7))
                dmm = sP.inc(ins)
                VEC.wait_ge(sP.h, dmm)
                src = pbS[:, qh:qh + 1024].rearrange("p (m t) -> p m t", m=2)
                dst = resT[:].rearrange("p (hc t) -> p hc t", hc=8)[
                    :, qd * 2:qd * 2 + 2, tkb * 512:tkb * 512 + 512]
                ins = VEC.tensor_tensor(out=dst, in0=src, in1=dst, op=ALU.add)
                dn_ev[qd % 2] = sV.inc(ins)
            cbc_gate[0] = max(cbc_gate[0], dn_ev[1])
            gu_ch["dq02"] = max(gu_ch["dq02"], dn_ev[0])
            wd_free[0] = wd_free[1] = sP.n
            if dbg == 5 and tb == 0 and tkb == 0:
                GPS.wait_ge(sV.h, sV.n)
                gdma(sDB, dbg_outs["dresT"][:, :], resT[:])
                GPS.wait_ge(sDB.h, sDB.n)
                return nc, dbg_outs

            for pr in range(16):
                e0 = pr * 2
                gu_w = [0, 0]
                wd_w = [0, 0]
                for ei in range(2):
                    e = e0 + ei
                    GPS.wait_ge(sP.h, gu_free[e % 2])
                    gu_w[ei] = wload(sWG[e % 2], wgu_slot(e % 2), d_wguT[e], 8, 1024)
                    GPS.wait_ge(sP.h, wd_free[2 * (pr % 2) + ei])
                    wd_w[ei] = wload(sWD[2 * (pr % 2) + ei], wd_slot(2 * (pr % 2) + ei), d_wdT[e], 4, 1024)
                TEN.wait_ge(sV.h, cbc_gate[0])
                cb_mm = {}
                for ei in range(2):
                    e = e0 + ei
                    sel = identb[:, e:e + 1].broadcast_to([128, 128])
                    ins = TEN.matmul(pbS[:, 1024 + ei * 512: 1536 + ei * 512], sel,
                                     wkvuT_sb[:, T0:T0 + 512], start=True, stop=True)
                    cb_mm[ei] = sP.inc(ins)
                for ei in range(2):
                    VEC.wait_ge(sP.h, cb_mm[ei])
                    ins = VEC.tensor_copy(cbc_sb[:, ei * 512:(ei + 1) * 512],
                                          pbS[:, 1024 + ei * 512: 1536 + ei * 512])
                    cbc_gate[0] = sV.inc(ins)
                tt2_list = []
                for g in range(8):
                    ei, gp = g // 4, g % 4
                    e = e0 + ei
                    gsl = (g % 2) * 512
                    if gp == 0:
                        TEN.wait_ge(sWG[e % 2].h, gu_w[ei])
                    if gu_ch["silu"][g % 2]:
                        TEN.wait_ge(sS.h, gu_ch["silu"][g % 2])
                    for cc in range(8):
                        ins = TEN.matmul(pbB[:, gsl:gsl + 512],
                                         wgu_slot(e % 2)[:, cc * 1024 + gp * 128: cc * 1024 + (gp + 1) * 128],
                                         xn2T[:, cc * 2048 + T0: cc * 2048 + T0 + 512],
                                         start=(cc == 0), stop=(cc == 7))
                    gmm = sP.inc(ins)
                    TEN.wait_ge(sV.h, max(gu_ch["tt1"][g % 2], gu_ch["dq02"]))
                    for cc in range(8):
                        ins2 = TEN.matmul(pbS[:, gsl:gsl + 512],
                                          wgu_slot(e % 2)[:, cc * 1024 + (4 + gp) * 128: cc * 1024 + (5 + gp) * 128],
                                          xn2T[:, cc * 2048 + T0: cc * 2048 + T0 + 512],
                                          start=(cc == 0), stop=(cc == 7))
                    umm = sP.inc(ins2)
                    SCA.wait_ge(sP.h, gmm)
                    if gu_ch["tt2"][g % 2]:
                        SCA.wait_ge(sV.h, gu_ch["tt2"][g % 2])
                    ins = SCA.activation(out=hsilu[:, gsl:gsl + 512], in_=pbB[:, gsl:gsl + 512], func=AF.Silu)
                    gu_ch["silu"][g % 2] = sS.inc(ins)
                    VEC.wait_ge(sP.h, umm)
                    VEC.wait_ge(sS.h, gu_ch["silu"][g % 2])
                    ins = VEC.tensor_tensor(out=hsilu[:, gsl:gsl + 512], in0=pbS[:, gsl:gsl + 512],
                                            in1=hsilu[:, gsl:gsl + 512], op=ALU.mult)
                    gu_ch["tt1"][g % 2] = sV.inc(ins)
                    VEC.drain()
                    ins = VEC.tensor_tensor(out=hm_slot(ei, gp),
                                            in0=hsilu[:, gsl:gsl + 512], in1=cbc_sb[:, ei * 512:(ei + 1) * 512],
                                            op=ALU.mult)
                    gu_ch["tt2"][g % 2] = sV.inc(ins)
                    tt2_list.append(gu_ch["tt2"][g % 2])
                    if gp == 3:
                        gu_free[e % 2] = sP.n
                TEN.wait_ge(sWD[2 * (pr % 2)].h, wd_w[0])
                TEN.wait_ge(sWD[2 * (pr % 2) + 1].h, wd_w[1])
                for qi, qd in enumerate((1, 0, 3, 2)):
                    qh = (qd % 2) * 1024
                    if qi == 0:
                        TEN.wait_ge(sV.h, dn_ev[qd % 2])
                    elif qi == 1:
                        TEN.wait_ge(sV.h, max(dn_ev[qd % 2], tt2_list[7]))
                    else:
                        TEN.wait_ge(sV.h, dn_ev[qd % 2])
                    for ei in range(2):
                        for gp in range(4):
                            if qi == 0:
                                TEN.wait_ge(sV.h, tt2_list[ei * 4 + gp])
                            for m in range(2):
                                hc = qd * 2 + m
                                ins = TEN.matmul(pbS[:, qh + m * 512: qh + (m + 1) * 512],
                                                 wd_slot(2 * (pr % 2) + ei)[:, gp * 1024 + hc * 128: gp * 1024 + (hc + 1) * 128],
                                                 hm_slot(ei, gp),
                                                 start=(ei == 0 and gp == 0), stop=(ei == 1 and gp == 3))
                    dmm = sP.inc(ins)
                    VEC.wait_ge(sP.h, dmm)
                    src = pbS[:, qh:qh + 1024].rearrange("p (m t) -> p m t", m=2)
                    dst = resT[:].rearrange("p (hc t) -> p hc t", hc=8)[
                        :, qd * 2:qd * 2 + 2, tkb * 512:tkb * 512 + 512]
                    ins = VEC.tensor_tensor(out=dst, in0=src, in1=dst, op=ALU.add)
                    dn_ev[qd % 2] = sV.inc(ins)
                cbc_gate[0] = max(cbc_gate[0], dn_ev[1])
                gu_ch["dq02"] = max(gu_ch["dq02"], dn_ev[0])
                wd_free[2 * (pr % 2)] = sP.n
                wd_free[2 * (pr % 2) + 1] = sP.n
                if dbg == 6 and tb == 0 and tkb == 0 and pr == 0:
                    GPS.wait_ge(sV.h, sV.n)
                    gdma(sDB, dbg_outs["dresT"][:, :], resT[:])
                    GPS.wait_ge(sDB.h, sDB.n)
                    return nc, dbg_outs

        TEN.wait_ge(sV.h, sV.n)
        TEN.wait_ge(sS.h, sS.n)
        for tt in range(8):
            git = tb * 8 + tt
            if tt >= 1:
                TEN.wait_ge(sV.h, out_tp[git - 1])
            for hc in range(8):
                ins = TEN.transpose(pbB[:, hc * 128:(hc + 1) * 128],
                                    resT[:, hc * 1024 + tt * 128: hc * 1024 + (tt + 1) * 128],
                                    identf[:])
            sP.inc(ins)
            VEC.wait_ge(sP.h, sP.n)
            prev_store = (14 + tt) if (tb == 0 and tt < 2) else (16 + git - 2)
            VEC.wait_ge(sOS[tt % 2].h, out_oc[prev_store])
            ins = VEC.tensor_copy(outst[:, (tt % 2) * 1024:(tt % 2 + 1) * 1024], pbB[:])
            out_tp[git] = sV.inc(ins)
            GPS.wait_ge(sV.h, sV.n)
            out_oc[16 + git] = gdma(sOS[tt % 2], d_yout[git * 128:(git + 1) * 128, :],
                                    outst[:, (tt % 2) * 1024:(tt % 2 + 1) * 1024])

    GPS.wait_ge(sOS[0].h, sOS[0].n)
    GPS.wait_ge(sOS[1].h, sOS[1].n)
    return nc, dbg_outs


_CACHE = {}


def _wkvuTz(wkvu):
    out = np.zeros((128, 16 * 128), np.float32)
    for h in range(16):
        po = (h % 2) * 64
        out[po:po + 64, h * 128:(h + 1) * 128] = wkvu[:, h * 64:(h + 1) * 64].T
    return np.ascontiguousarray(out)


def _host_inputs(L0_attn_x, L0_attn_w_kv_down, L0_attn_w_kv_up, L0_attn_w_q, L0_attn_w_out,
                 L0_ffn_router_weight, L0_ffn_e_score_bias, L0_ffn_gate_up_proj,
                 L0_ffn_down_proj, L0_ffn_shared_gate_w, L0_ffn_shared_up_w,
                 L0_ffn_shared_down_w):
    c = np.ascontiguousarray
    x = c(np.asarray(L0_attn_x, dtype=np.float32))
    shared = {
        "wkvd": c(np.asarray(L0_attn_w_kv_down, np.float32)),
        "wq": c(np.asarray(L0_attn_w_q, np.float32)),
        "wkvu": c(np.asarray(L0_attn_w_kv_up, np.float32)),
        "wkvuT": _wkvuTz(np.asarray(L0_attn_w_kv_up, np.float32)),
        "wout": c(np.asarray(L0_attn_w_out, np.float32)),
        "wrT": c(np.asarray(L0_ffn_router_weight, np.float32).T),
        "ebias": c(np.tile(np.asarray(L0_ffn_e_score_bias, np.float32)[None, :], (128, 1))),
        "wguT": c(np.transpose(np.asarray(L0_ffn_gate_up_proj, np.float32), (0, 2, 1))),
        "wdT": c(np.transpose(np.asarray(L0_ffn_down_proj, np.float32), (0, 2, 1))),
        "wsgT": c(np.asarray(L0_ffn_shared_gate_w, np.float32).T),
        "wsuT": c(np.asarray(L0_ffn_shared_up_w, np.float32).T),
        "wsdT": c(np.asarray(L0_ffn_shared_down_w, np.float32).T),
        "identb": np.eye(128, dtype=ml_dtypes.bfloat16),
        "identf": np.eye(128, dtype=np.float32),
    }
    tri = (np.arange(32)[None, :] < np.arange(32)[:, None]).astype(np.float32)
    shared["tri"] = c(np.tile(tri.reshape(1, 1024), (128, 1))).astype(ml_dtypes.bfloat16)

    in_maps, meta = [], []
    for core in range(8):
        s, role = core // 2, core % 2
        chunks = CHUNKS[role]
        rows = np.concatenate([np.arange(ch * 256, (ch + 1) * 256) for ch in chunks])
        meta.append((s, rows))
        masks = np.zeros((8, 256, 512), np.float32)
        for sl in range(8):
            q = chunks[sl] * 256 + np.arange(256)[:, None]
            k = KUNI[sl] - 512 + np.arange(512)[None, :]
            masks[sl] = np.where(k <= q, 0.0, NEG)
        m = dict(shared)
        m["xseq"] = c(x[s])
        m["xown"] = c(x[s][rows])
        m["mask"] = masks.astype(ml_dtypes.bfloat16)
        in_maps.append(m)
    return in_maps, meta


def kernel(**inputs):
    if "nc" not in _CACHE:
        _CACHE["nc"] = build_nc(dbg=0)[0]
    nc = _CACHE["nc"]
    in_maps, meta = _host_inputs(**inputs)
    res = run_bass_kernel_spmd(nc, in_maps, core_ids=list(range(8)))
    out = np.empty((4, 4096, 1024), np.float32)
    for core in range(8):
        s, rows = meta[core]
        out[s][rows] = res.results[core]["yout"]
    return out


# version 20
# speedup vs baseline: 1.0550x; 1.0007x over previous
import numpy as np
import ml_dtypes
import concourse.bass as bass
import concourse.mybir as mybir
import concourse.bass_utils as _bass_utils
from concourse.bass_utils import run_bass_kernel_spmd

if not getattr(_bass_utils, "_ldwopt_patched", False):
    _orig_run_command = _bass_utils.run_command

    def _run_command_ldwopt(cmd, **kw):
        cmd = [c.replace("--enable-ldw-opt=false", "--enable-ldw-opt=true")
               if isinstance(c, str) else c for c in cmd]
        return _orig_run_command(cmd, **kw)

    _bass_utils.run_command = _run_command_ldwopt
    _bass_utils._ldwopt_patched = True

F32 = mybir.dt.float32
BF16 = mybir.dt.bfloat16
AF = mybir.ActivationFunctionType
ALU = mybir.AluOpType
AX = mybir.AxisListType

KUNI = [4096, 3584, 3072, 2560, 2048, 1536, 1024, 512]
CHUNKS = [[15, 12, 11, 8, 7, 4, 3, 0], [14, 13, 10, 9, 6, 5, 2, 1]]
NH, RK, DM = 16, 128, 1024
NT = 2048
NQS = 16
TSEQ = 4096
NEG = -1.0e30


class Sem:
    def __init__(self, nc, name):
        self.h = nc.alloc_semaphore(name)
        self.n = 0

    def inc(self, instr, k=1):
        instr.then_inc(self.h, k)
        self.n += k
        return self.n


def build_nc(dbg=0):
    nc = bass.Bass("TRN2", target_bir_lowering=False)
    TEN, VEC, SCA, SYN, GPS = nc.tensor, nc.vector, nc.scalar, nc.sync, nc.gpsimd

    d_xseq = nc.dram_tensor("xseq", [TSEQ, DM], F32, kind="ExternalInput").ap()
    d_xown = nc.dram_tensor("xown", [NT, DM], F32, kind="ExternalInput").ap()
    d_mask = nc.dram_tensor("mask", [8, 256, 512], BF16, kind="ExternalInput").ap()
    d_idb = nc.dram_tensor("identb", [128, 128], BF16, kind="ExternalInput").ap()
    d_idf = nc.dram_tensor("identf", [128, 128], F32, kind="ExternalInput").ap()
    d_tri = nc.dram_tensor("tri", [128, 1024], BF16, kind="ExternalInput").ap()
    d_ebias = nc.dram_tensor("ebias", [128, 32], F32, kind="ExternalInput").ap()
    d_wkvd = nc.dram_tensor("wkvd", [DM, RK], F32, kind="ExternalInput").ap()
    d_wq = nc.dram_tensor("wq", [DM, DM], F32, kind="ExternalInput").ap()
    d_wkvu = nc.dram_tensor("wkvu", [RK, 2048], F32, kind="ExternalInput").ap()
    d_wkvuT = nc.dram_tensor("wkvuT", [128, 2048], F32, kind="ExternalInput").ap()
    d_wout = nc.dram_tensor("wout", [DM, DM], F32, kind="ExternalInput").ap()
    d_wrT = nc.dram_tensor("wrT", [DM, 32], F32, kind="ExternalInput").ap()
    d_wguT = nc.dram_tensor("wguT", [32, DM, 1024], F32, kind="ExternalInput").ap()
    d_wdT = nc.dram_tensor("wdT", [32, 512, DM], F32, kind="ExternalInput").ap()
    d_wsgT = nc.dram_tensor("wsgT", [DM, 1024], F32, kind="ExternalInput").ap()
    d_wsuT = nc.dram_tensor("wsuT", [DM, 1024], F32, kind="ExternalInput").ap()
    d_wsdT = nc.dram_tensor("wsdT", [1024, DM], F32, kind="ExternalInput").ap()
    d_yout = nc.dram_tensor("yout", [NT, DM], F32, kind="ExternalOutput").ap()

    dbg_outs = {}
    if dbg:
        for nm, shp in [("dlat", [128, 4096]), ("dq", [128, 16384]),
                        ("dxn2", [128, 16384]), ("dcmb", [32, 2048]),
                        ("dqt", [128, 2048]), ("dS", [128, 4096]),
                        ("dz", [128, 2048]), ("dL", [128, 128]),
                        ("dresT", [128, 8192])]:
            dbg_outs[nm] = nc.dram_tensor(nm, shp, F32, kind="ExternalOutput").ap()

    sb = nc.alloc_sbuf_tensor
    bigA = sb("bigA", [128, 16384], BF16)
    bigB = sb("bigB", [128, 16384], BF16)
    bigC = sb("bigC", [128, 16384], BF16)
    bigD = sb("bigD", [128, 8192], BF16)
    bigE = sb("bigE", [128, 8192], F32)
    latT = sb("latT", [128, 4096], BF16)
    latkm = sb("latkm", [128, 4096], BF16)
    identb = sb("identb_sb", [128, 128], BF16)
    identf = sb("identf_sb", [128, 128], F32)
    xstage = sb("xstage", [128, 2048], F32)
    xntm = sb("xntm", [128, 2048], BF16)
    wkvd_sb = sb("wkvd_sb", [128, 1024], BF16)
    wkvu_sb = sb("wkvu_sb", [128, 2048], BF16)
    wkvuT_sb = sb("wkvuT_sb", [128, 2048], BF16)
    wrT_sb = sb("wrT_sb", [128, 256], BF16)
    ebias_sb = sb("ebias_sb", [128, 32], F32)
    qt_sb = sb("qt_sb", [128, 2048], BF16)
    z_sb = sb("z_sb", [128, 2048], BF16)
    yy_sb = sb("yy_sb", [128, 2048], BF16)
    xownst = sb("xownst", [128, 2048], F32)
    st_sums = sb("st_sums", [128, 48], F32)
    st_sqs = sb("st_sqs", [128, 48], F32)
    st_nms = sb("st_nms", [128, 48], F32)
    st_e2s = sb("st_e2s", [128, 48], F32)
    st_vars = sb("st_vars", [128, 48], F32)
    st_stds = sb("st_stds", [128, 48], F32)
    st_rstds = sb("st_rstds", [128, 48], F32)
    st_mx = sb("st_mx", [128, 1], F32)
    st_mlat = sb("st_mlat", [128, 1], F32)
    st_mlatb = sb("st_mlatb", [128, 1], BF16)
    eps_sb = sb("eps_sb", [128, 1], F32)
    scl8 = sb("scl8", [128, 1], F32)
    one_sb = sb("one_sb", [128, 1], F32)
    mxp = sb("mxp", [128, 8], F32)
    nmx8 = sb("nmx8", [128, 16], F32)
    Lbuf = sb("Lbuf", [128, 128], F32)
    Lsum = sb("Lsum", [128, 16], F32)
    Linv = sb("Linv", [128, 16], F32)

    pbS = nc.alloc_psum_tensor("pbS", [128, 2048], F32)
    pbB = nc.alloc_psum_tensor("pbB", [128, 1024], F32)
    pbt = nc.alloc_psum_tensor("pbt", [128, 2048], BF16)

    sWA = Sem(nc, "sWA")
    sWG = [Sem(nc, "sWG0"), Sem(nc, "sWG1")]
    sWD = [Sem(nc, f"sWD{i}") for i in range(4)]
    sXC = Sem(nc, "sXC")
    sXS = [Sem(nc, "sXS0"), Sem(nc, "sXS1")]
    sXO = [Sem(nc, "sXO0"), Sem(nc, "sXO1")]
    sZB = [Sem(nc, "sZB0"), Sem(nc, "sZB1")]
    sCM = Sem(nc, "sCM")
    sOS = [Sem(nc, "sOS0"), Sem(nc, "sOS1")]
    sDB = Sem(nc, "sDB")
    sP = Sem(nc, "sP")
    sV = Sem(nc, "sV")
    sS = Sem(nc, "sS")
    sBAR = Sem(nc, "sBAR")

    xnstage = bigE[:, 0:2048].bitcast(BF16)
    y_sb = yy_sb[:, 0:1024]
    yT_sb = yy_sb[:, 1024:2048]
    absq = yy_sb[:, 0:2048]
    Ssb2 = bigE[:, 2048:4096].bitcast(BF16)
    outst = xstage
    tri_sb = y_sb
    _rsc = yT_sb[:, 0:1024].bitcast(F32)
    cmbT = wkvuT_sb[0:32, 0:2048]
    Ssb = bigE[:, 0:2048].bitcast(BF16)
    Ssb3 = bigE[:, 4096:6144].bitcast(BF16)
    Ssb4 = bigE[:, 6144:8192].bitcast(BF16)
    resT = bigE
    smid = latT[:, 0:4096]
    cbc_sb = latkm[:, 0:1024]
    hsilu = latkm[:, 1024:2048]
    cmb_bf = latkm[:, 2048:2176]
    xn1ownT = bigA
    STsb = bigA
    xn2T = bigA
    qT = bigB
    wq_sb = bigC[:, 0:8192]
    wout_sb = bigC[:, 8192:16384]
    mask_sb = bigD

    def wgu_slot(s):
        return bigB[:, s * 8192:(s + 1) * 8192]

    def wd_slot(s):
        return bigC[:, s * 4096:(s + 1) * 4096]

    def hm_slot(ei, gp):
        off = ei * 4096 + gp * 1024
        return bigD[:, off:off + 512]

    VEC.memset(eps_sb[:], 1e-5)
    VEC.memset(scl8[:], 0.125)
    VEC.memset(one_sb[:], 1.0)
    VEC.memset(Lbuf[:], 0.0)
    sV.inc(VEC.drain())

    def gdma(sem, out, in_):
        GPS.dma_start(out=out, in_=in_).then_inc(sem.h, 16)
        sem.n += 16
        return sem.n

    def xdma(sem, out, in_):
        SYN.dma_start(out=out, in_=in_).then_inc(sem.h, 16)
        sem.n += 16
        return sem.n

    def wload(sem, dst2d, src2d, g, n):
        dst3 = dst2d.rearrange("p (g n) -> p g n", g=g)
        src3 = src2d.rearrange("(g p) n -> p g n", p=128)
        return gdma(sem, dst3, src3)

    w_kvd = wload(sZB[0], wkvd_sb[:], d_wkvd, 8, 128)
    w_q = wload(sZB[1], wq_sb, d_wq, 8, 1024)
    gdma(sCM, wkvu_sb[:], d_wkvu[:, :])
    w_kvu = gdma(sCM, wkvuT_sb[:], d_wkvuT[:, :])
    wload(sWA, wout_sb, d_wout, 8, 1024)
    w_all = wload(sWA, wrT_sb[:], d_wrT, 8, 32)

    xdma(sXC, identb[:], d_idb[:, :])
    xdma(sXC, identf[:], d_idf[:, :])
    xdma(sXC, tri_sb[:], d_tri[:, :])
    xdma(sXC, ebias_sb[:], d_ebias[:, :])
    _mdst = mask_sb[:, :].rearrange("p (s qq k) -> p s qq k", s=8, qq=2)
    _msrc = d_mask.rearrange("s (qq p) k -> p s qq k", p=128)
    x_const = xdma(sXC, _mdst, _msrc)

    SCA.preload_act_table(AF.Sqrt)

    slot_gate = {0: (0, 0), 1: (0, 0)}
    tp_gate = {0: ("V", 0), 1: ("V", 0)}

    def wait_tp(eng, g):
        if g[1]:
            eng.wait_ge(sS.h if g[0] == "S" else sV.h, g[1])

    def emit_ln_batch(srcs, dsts3, post_cb=None):
        nt = len(srcs)
        for it, src in enumerate(srcs):
            slot = it % 2
            gv, gs = slot_gate[slot]
            if gv:
                SYN.wait_ge(sV.h, gv)
            if gs:
                SYN.wait_ge(sS.h, gs)
            n = xdma(sXS[slot], xstage[:, slot * 1024:(slot + 1) * 1024], src)
            VEC.wait_ge(sXS[slot].h, n)
            SCA.wait_ge(sXS[slot].h, n)
            ins = VEC.reduce_sum(st_sums[:, it:it + 1], xstage[:, slot * 1024:(slot + 1) * 1024], axis=AX.X)
            gv = sV.inc(ins)
            SCA.activation(out=pbB[:, 0:1024], in_=xstage[:, slot * 1024:(slot + 1) * 1024],
                           func=AF.Square, accum_out=st_sqs[:, it:it + 1])
            gs = sS.inc(SCA.drain())
            slot_gate[slot] = (gv, gs)
        VEC.wait_ge(sS.h, sS.n)
        VEC.drain()
        VEC.tensor_scalar(out=st_nms[:, 0:nt], in0=st_sums[:, 0:nt], scalar1=-1.0 / DM, scalar2=None, op0=ALU.mult)
        VEC.tensor_scalar(out=st_e2s[:, 0:nt], in0=st_sqs[:, 0:nt], scalar1=1.0 / DM, scalar2=None, op0=ALU.mult)
        VEC.drain()
        VEC.tensor_tensor(out=st_vars[:, 0:nt], in0=st_nms[:, 0:nt], in1=st_nms[:, 0:nt], op=ALU.mult)
        VEC.drain()
        VEC.tensor_tensor(out=st_vars[:, 0:nt], in0=st_e2s[:, 0:nt], in1=st_vars[:, 0:nt], op=ALU.subtract)
        sV.inc(VEC.drain())
        SCA.wait_ge(sV.h, sV.n)
        SCA.activation(out=st_stds[:, 0:nt], in_=st_vars[:, 0:nt], func=AF.Sqrt,
                       bias=eps_sb[:], scale=one_sb[:])
        sS.inc(SCA.drain())
        VEC.wait_ge(sS.h, sS.n)
        VEC.reciprocal(st_rstds[:, 0:nt], st_stds[:, 0:nt])
        stats_v = sV.inc(VEC.drain())
        for it, (src, dst3) in enumerate(zip(srcs, dsts3)):
            slot = it % 2
            gv, gs = slot_gate[slot]
            if gv:
                SYN.wait_ge(sV.h, gv)
            if gs:
                SYN.wait_ge(sS.h, gs)
            n = xdma(sXS[slot], xstage[:, slot * 1024:(slot + 1) * 1024], src)
            VEC.wait_ge(sXS[slot].h, n)
            ins = VEC.tensor_scalar(out=xntm[:, slot * 1024:(slot + 1) * 1024],
                                    in0=xstage[:, slot * 1024:(slot + 1) * 1024],
                                    scalar1=st_nms[:, it:it + 1], scalar2=st_rstds[:, it:it + 1],
                                    op0=ALU.add, op1=ALU.mult)
            gv = sV.inc(ins)
            slot_gate[slot] = (gv, 0)
            TEN.wait_ge(sV.h, gv)
            half = slot * 1024
            wait_tp(TEN, tp_gate[slot])
            for cc in range(8):
                ins = TEN.transpose(pbt[:, half + cc * 128: half + (cc + 1) * 128],
                                    xntm[:, slot * 1024 + cc * 128: slot * 1024 + (cc + 1) * 128],
                                    identb[:])
            sP.inc(ins)
            VEC.wait_ge(sP.h, sP.n)
            ins = VEC.tensor_copy(dst3, pbt[:, half:half + 1024].rearrange("p (cc k) -> p cc k", cc=8))
            tp_gate[slot] = ("V", sV.inc(ins))
            if post_cb is not None:
                post_cb(it)

    TEN.wait_ge(sXC.h, x_const)
    tags = []

    def a1_cb(it):
        if it >= 32 or it % 4 != 3:
            return
        g = it // 4
        TEN.wait_ge(sV.h, sV.n)
        if g == 0:
            TEN.wait_ge(sZB[0].h, w_kvd)
        ph = (g % 2) * 512
        for cc in range(8):
            ins = TEN.matmul(pbS[:, ph:ph + 512],
                             wkvd_sb[:, cc * 128:(cc + 1) * 128],
                             xnstage[:, cc * 512:(cc + 1) * 512],
                             start=(cc == 0), stop=(cc == 7))
        sP.inc(ins)
        VEC.wait_ge(sP.h, sP.n)
        ins = VEC.tensor_copy(latT[:, g * 512:(g + 1) * 512], pbS[:, ph:ph + 512])
        sV.inc(ins)

    srcs = [d_xseq[it * 128:(it + 1) * 128, :] for it in range(32)] + \
           [d_xown[it * 128:(it + 1) * 128, :] for it in range(16)]
    dsts = [xnstage[:].rearrange("p (cc k) -> p cc k", cc=8)[:, :, (it % 4) * 128:(it % 4) * 128 + 128]
            for it in range(32)] + \
           [xn1ownT[:].rearrange("p (cc k) -> p cc k", cc=8)[:, :, it * 128:(it + 1) * 128]
            for it in range(16)]
    emit_ln_batch(srcs, dsts, post_cb=a1_cb)

    TEN.wait_ge(sV.h, sV.n)
    lk_ev = {}
    for b in range(4):
        half = (b % 2) * 1024
        if b >= 2:
            TEN.wait_ge(sV.h, lk_ev[b - 2])
        for j in range(8):
            kc = b * 8 + j
            ins = TEN.transpose(pbt[:, half + j * 128: half + (j + 1) * 128],
                                latT[:, kc * 128:(kc + 1) * 128], identb[:])
        sP.inc(ins)
        VEC.wait_ge(sP.h, sP.n)
        ins = VEC.tensor_copy(latkm[:, b * 1024:(b + 1) * 1024], pbt[:, half:half + 1024])
        lk_ev[b] = sV.inc(ins)

    VEC.reduce_max(st_mlat[:], latT[:], axis=AX.X, apply_absolute_value=True)
    sV.inc(VEC.drain())
    VEC.tensor_copy(st_mlatb[:], st_mlat[:])
    sV.inc(VEC.drain())
    TEN.wait_ge(sV.h, sV.n)
    tp_gate[0] = ("V", sV.n)
    tp_gate[1] = ("V", sV.n)

    TEN.wait_ge(sV.h, sV.n)
    TEN.wait_ge(sZB[1].h, w_q)
    qp_ev = {}
    j = 0
    for hdc in range(8):
        for tq in range(4):
            ph = (j % 2) * 512
            if j >= 2:
                TEN.wait_ge(sV.h, qp_ev[j - 2])
            for cc in range(8):
                ins = TEN.matmul(pbB[:, ph:ph + 512],
                                 wq_sb[:, cc * 1024 + hdc * 128: cc * 1024 + (hdc + 1) * 128],
                                 xn1ownT[:, cc * 2048 + tq * 512: cc * 2048 + (tq + 1) * 512],
                                 start=(cc == 0), stop=(cc == 7))
            sP.inc(ins)
            VEC.wait_ge(sP.h, sP.n)
            ins = VEC.tensor_copy(qT[:, hdc * 2048 + tq * 512: hdc * 2048 + (tq + 1) * 512],
                                  pbB[:, ph:ph + 512])
            qp_ev[j] = sV.inc(ins)
            j += 1
    wq_region_free = sP.n

    if dbg == 1:
        VEC.wait_ge(sV.h, sV.n)
        sV.inc(VEC.drain())
        SYN.wait_ge(sV.h, sV.n)
        GPS.wait_ge(sV.h, sV.n)
        gdma(sDB, dbg_outs["dlat"][:, :], latT[:])
        gdma(sDB, dbg_outs["dq"][:, :], qT[:])
    if dbg == 1:
        GPS.wait_ge(sDB.h, sDB.n)
        return nc, dbg_outs

    SCA.preload_act_table(AF.Exp)
    TEN.wait_ge(sCM.h, w_kvu)

    res_tt_done = {}
    o_count = {}
    exp_pair_done = [0, 0]

    for qs in range(NQS):
        slot = qs // 2
        kuni = KUNI[slot]
        nk5 = kuni // 512
        nkc = kuni // 128

        if qs >= 2:
            SYN.wait_ge(sV.h, res_tt_done[qs - 2])
        xo_n = xdma(sXO[qs % 2], xownst[:, (qs % 2) * 1024:(qs % 2 + 1) * 1024],
                    d_xown[qs * 128:(qs + 1) * 128, :])

        TEN.wait_ge(sV.h, sV.n)
        qt_ev = {}
        for hb in range(4):
            ph = (hb % 2) * 512
            if hb >= 2:
                TEN.wait_ge(sV.h, qt_ev[hb - 2])
            for hh in range(4):
                h = hb * 4 + hh
                ins = TEN.matmul(pbB[:, ph + hh * 128: ph + (hh + 1) * 128],
                                 wkvuT_sb[:, h * 128:(h + 1) * 128],
                                 qT[:, (h // 2) * 2048 + qs * 128:(h // 2) * 2048 + (qs + 1) * 128],
                                 start=True, stop=True)
            sP.inc(ins)
            VEC.wait_ge(sP.h, sP.n)
            ins = VEC.tensor_scalar(out=qt_sb[:, hb * 512:(hb + 1) * 512], in0=pbB[:, ph:ph + 512],
                                    scalar1=0.125, scalar2=None, op0=ALU.mult)
            qt_ev[hb] = sV.inc(ins)
            SCA.wait_ge(sV.h, qt_ev[hb])
            insA = SCA.activation(out=absq[:, hb * 512:(hb + 1) * 512],
                                  in_=qt_sb[:, hb * 512:(hb + 1) * 512], func=AF.Abs)
            abs_cnt = sS.inc(insA)
        if qs == NQS - 1:
            qT_region_free = sP.n

        if dbg == 21 and qs == 0:
            GPS.wait_ge(sV.h, sV.n)
            gdma(sDB, dbg_outs["dqt"][:, :], qt_sb[:])
            GPS.wait_ge(sDB.h, sDB.n)
            return nc, dbg_outs

        VEC.wait_ge(sS.h, sS.n)
        VEC.memset(Lbuf[:], 0.0)
        sV.inc(VEC.drain())
        SCA.wait_ge(sV.h, sV.n)
        exp_done_cnt = {}

        tp_done = {}
        av_gate = [0]

        def emit_tp(h_t, exp_cnt):
            TEN.wait_ge(sS.h, exp_cnt)
            ng8 = (nkc + 7) // 8
            tp_ev = {}
            for b8 in range(ng8):
                half = (b8 % 2) * 1024
                g = tp_ev[b8 - 2] if b8 >= 2 else tp_gate[b8 % 2]
                wait_tp(TEN, g)
                cnt = min(8, nkc - b8 * 8)
                Sbuf = (Ssb, Ssb2, Ssb3, Ssb4)[h_t % 4]
                for jj in range(cnt):
                    kc = b8 * 8 + jj
                    ins2 = TEN.transpose(pbt[:, half + jj * 128: half + (jj + 1) * 128],
                                         Sbuf[:, kc * 128:(kc + 1) * 128], identb[:])
                sP.inc(ins2)
                dst = STsb[:].rearrange("p (kc r) -> p kc r", kc=32, r=512)[
                    :, b8 * 8: b8 * 8 + cnt, (h_t % 4) * 128:(h_t % 4 + 1) * 128]
                src = pbt[:, half: half + cnt * 128].rearrange("p (kc r) -> p kc r", kc=cnt)
                VEC.wait_ge(sP.h, sP.n)
                ins2 = VEC.tensor_copy(dst, src)
                tp_ev[b8] = ("V", sV.inc(ins2))
                tp_gate[b8 % 2] = tp_ev[b8]
            tp_done[h_t] = tp_ev[ng8 - 1][1]

        def emit_av(hg, part):
            nk2 = nkc // 2
            lo, hi = (0, nk2) if part == 0 else (nk2, nkc)
            if part == 0:
                TEN.wait_ge(sV.h, max(tp_done[hg * 4], tp_done[hg * 4 + 1],
                                      tp_done[hg * 4 + 2], tp_done[hg * 4 + 3], av_gate[0]))
            for kc in range(lo, hi):
                ins2 = TEN.matmul(pbB[:, 0:512], latkm[:, kc * 128:(kc + 1) * 128],
                                  STsb[:, kc * 512:(kc + 1) * 512],
                                  start=(kc == 0), stop=(kc == nkc - 1))
            sP.inc(ins2)
            if part == 1:
                VEC.wait_ge(sP.h, sP.n)
                ins2 = VEC.tensor_copy(z_sb[:, hg * 512:(hg + 1) * 512], pbB[:, 0:512])
                av_gate[0] = sV.inc(ins2)

        TEN.wait_ge(sS.h, abs_cnt)
        for h in range(16):
            ins = TEN.matmul(pbB[:, h:h + 1], absq[:, h * 128:(h + 1) * 128],
                             st_mlatb[:], start=True, stop=True)
        sP.inc(ins)
        VEC.wait_ge(sP.h, sP.n)
        VEC.tensor_scalar(out=nmx8[:, 0:16], in0=pbB[:, 0:16], scalar1=-1.0, scalar2=None, op0=ALU.mult)
        nmx_ready = sV.inc(VEC.drain())
        SCA.wait_ge(sV.h, nmx_ready)

        ngrp = (nk5 + 1) // 2
        exp_last = {}
        for hp in range(9):
            todo = []
            if hp >= 1:
                pA = 2 * (hp - 1)
                todo.append(("tp", pA))
                todo.append(("tp", pA + 1))
                if hp >= 2 and hp % 2 == 0:
                    todo.append(("av", (hp - 2) // 2, 0))
                    todo.append(("av", (hp - 2) // 2, 1))

            def run_blob():
                job = todo.pop(0)
                if job[0] == "tp":
                    emit_tp(job[1], exp_done_cnt[job[1]])
                else:
                    emit_av(job[1], job[2])

            if hp == 8:
                while todo:
                    run_blob()
                continue
            hA, hB = 2 * hp, 2 * hp + 1
            p2 = {hA: {}, hB: {}}
            mask_cnts = {}
            for kj in range(nk5):
                last = (kj == nk5 - 1)
                for h in (hA, hB):
                    pp = (h % 2) * 1024
                    sl = pp + (kj % 2) * 512
                    if kj >= 2:
                        TEN.wait_ge(sS.h, p2[h][(kj - 2) // 2])
                    else:
                        TEN.wait_ge(sS.h, exp_pair_done[h % 2])
                    if last:
                        mslice = mask_sb[:, slot * 1024 + (qs % 2) * 512: slot * 1024 + (qs % 2) * 512 + 512]
                        TEN.matmul(pbS[:, sl:sl + 512], identb[:, 0:128], mslice,
                                   start=True, stop=False)
                    ins = TEN.matmul(pbS[:, sl:sl + 512],
                                     qt_sb[:, h * 128:(h + 1) * 128],
                                     latT[:, kj * 512:(kj + 1) * 512],
                                     start=(not last), stop=True)
                    pe_mm = sP.inc(ins)
                    if kj % 2 == 1 or last:
                        grp = kj // 2
                        glen = 1024 if kj % 2 == 1 else 512
                        SCA.wait_ge(sP.h, pe_mm)
                        Sout = (Ssb, Ssb2, Ssb3, Ssb4)[h % 4]
                        ins = SCA.activation(out=Sout[:, grp * 1024: grp * 1024 + glen],
                                             in_=pbS[:, pp:pp + glen],
                                             func=AF.Exp, scale=one_sb[:], bias=nmx8[:, h:h + 1],
                                             accum_out=Lbuf[:, h * 8 + grp: h * 8 + grp + 1])
                        p2[h][grp] = sS.inc(ins)
                        exp_last[h] = p2[h][grp]
                if kj % 2 == 1 and not last and todo:
                    run_blob()
            while todo:
                run_blob()
            for h in (hA, hB):
                exp_pair_done[h % 2] = exp_last[h]
                exp_done_cnt[h] = exp_last[h]

        VEC.wait_ge(sS.h, sS.n)
        VEC.tensor_reduce(Lsum[:], Lbuf[:].rearrange("p (h k) -> p h k", h=16), axis=AX.X, op=ALU.add)
        VEC.drain()
        VEC.reciprocal(Linv[:], Lsum[:])
        sV.inc(VEC.drain())

        TEN.wait_ge(sV.h, sV.n)
        for h in range(NH):
            ins = TEN.matmul(pbB[:, h * 64:(h + 1) * 64],
                             z_sb[:, h * 128:(h + 1) * 128],
                             wkvu_sb[:, 1024 + h * 64: 1024 + (h + 1) * 64],
                             start=True, stop=True)
        sP.inc(ins)
        VEC.wait_ge(sP.h, sP.n)
        linv_b = Linv[:].unsqueeze(2).broadcast_to([128, 16, 64])
        ins = VEC.tensor_tensor(out=y_sb[:].rearrange("p (h d) -> p h d", h=16),
                                in0=pbB[:].rearrange("p (h d) -> p h d", h=16),
                                in1=linv_b, op=ALU.mult)
        sV.inc(ins)
        TEN.wait_ge(sV.h, sV.n)
        for cc in range(8):
            ins = TEN.transpose(pbt[:, cc * 128:(cc + 1) * 128],
                                y_sb[:, cc * 128:(cc + 1) * 128], identb[:])
        sP.inc(ins)
        VEC.wait_ge(sP.h, sP.n)
        ins = VEC.tensor_copy(yT_sb[:], pbt[:, 0:1024])
        sV.inc(ins)
        TEN.wait_ge(sV.h, sV.n)
        if qs == 0:
            TEN.wait_ge(sWA.h, w_all)
        for cc in range(2):
            for hdc in range(8):
                ins = TEN.matmul(pbB[:, cc * 512:(cc + 1) * 512],
                                 yT_sb[:, hdc * 128:(hdc + 1) * 128],
                                 wout_sb[:, hdc * 1024 + cc * 512: hdc * 1024 + cc * 512 + 512],
                                 start=(hdc == 0), stop=(hdc == 7))
        sP.inc(ins)
        VEC.wait_ge(sP.h, sP.n)
        VEC.wait_ge(sXO[qs % 2].h, xo_n)
        if qs >= 2:
            VEC.wait_ge(sOS[qs % 2].h, o_count[qs - 2])
        ins = VEC.tensor_tensor(out=outst[:, (qs % 2) * 1024:(qs % 2 + 1) * 1024],
                                in0=pbB[:],
                                in1=xownst[:, (qs % 2) * 1024:(qs % 2 + 1) * 1024], op=ALU.add)
        res_tt_done[qs] = sV.inc(ins)
        if qs == NQS - 1:
            wout_region_free = sP.n
        GPS.wait_ge(sV.h, sV.n)
        o_count[qs] = gdma(sOS[qs % 2], d_yout[qs * 128:(qs + 1) * 128, :],
                           outst[:, (qs % 2) * 1024:(qs % 2 + 1) * 1024])
        if dbg == 24 and qs == 0:
            GPS.wait_ge(sOS[0].h, sOS[0].n)
            return nc, dbg_outs

    if dbg == 2:
        GPS.wait_ge(sOS[0].h, sOS[0].n)
        GPS.wait_ge(sOS[1].h, sOS[1].n)
        return nc, dbg_outs

    sBAR.inc(TEN.nop())
    sBAR.inc(VEC.drain())
    sBAR.inc(SCA.drain())
    for eng in (TEN, VEC, SCA, SYN, GPS):
        eng.wait_ge(sBAR.h, 3)

    SCA.preload_act_table(AF.Sqrt)
    SYN.wait_ge(sOS[0].h, sOS[0].n)
    SYN.wait_ge(sOS[1].h, sOS[1].n)
    srcs = [d_yout[it * 128:(it + 1) * 128, :] for it in range(16)]
    dsts = [xn2T[:].rearrange("p (cc k) -> p cc k", cc=8)[:, :, it * 128:(it + 1) * 128]
            for it in range(16)]
    emit_ln_batch(srcs, dsts)

    SCA.preload_act_table(AF.Exp)
    VEC.memset(cmb_bf[:], 0.0)
    sV.inc(VEC.drain())
    TEN.wait_ge(sV.h, sV.n)
    TEN.wait_ge(sS.h, sS.n)
    zF = z_sb[:, 0:2048].bitcast(F32)
    qF = qt_sb[:, 0:2048].bitcast(F32)
    A, B = zF[:, 0:512], zF[:, 512:1024]
    C, D = qF[:, 0:512], qF[:, 512:1024]
    m16 = _rsc[:, 0:16]
    s16 = _rsc[:, 16:32]
    l16 = _rsc[:, 32:48]
    s16b = _rsc[:, 48:64]
    r16b = _rsc[:, 64:80]
    gtb = Ssb[:, 0:2048]
    eqb = Ssb[:, 2048:4096]
    for it in range(16):
        for cc in range(8):
            ins = TEN.matmul(pbB[:, it * 32:(it + 1) * 32],
                             xn2T[:, cc * 2048 + it * 128: cc * 2048 + (it + 1) * 128],
                             wrT_sb[:, cc * 32:(cc + 1) * 32],
                             start=(cc == 0), stop=(cc == 7))
    sP.inc(ins)
    VEC.wait_ge(sP.h, sP.n)
    VEC.tensor_copy(A, pbB[:, 0:512])
    sV.inc(VEC.drain())
    VEC.tensor_reduce(m16, A.rearrange("p (g e) -> p g e", g=16), axis=AX.X, op=ALU.max, negate=True)
    VEC.drain()
    mb = m16.unsqueeze(2).broadcast_to([128, 16, 32])
    VEC.tensor_tensor(out=B.rearrange("p (g e) -> p g e", g=16), in0=A.rearrange("p (g e) -> p g e", g=16),
                      in1=mb, op=ALU.add)
    sV.inc(VEC.drain())
    SCA.wait_ge(sV.h, sV.n)
    ins = SCA.activation(out=C, in_=B, func=AF.Exp)
    sS.inc(ins)
    VEC.wait_ge(sS.h, sS.n)
    VEC.tensor_reduce(s16, C.rearrange("p (g e) -> p g e", g=16), axis=AX.X, op=ALU.add)
    VEC.drain()
    VEC.reciprocal(l16, s16)
    VEC.drain()
    lb = l16.unsqueeze(2).broadcast_to([128, 16, 32])
    VEC.tensor_tensor(out=D.rearrange("p (g e) -> p g e", g=16), in0=C.rearrange("p (g e) -> p g e", g=16),
                      in1=lb, op=ALU.mult)
    VEC.drain()
    eb = ebias_sb[:].unsqueeze(1).broadcast_to([128, 16, 32])
    VEC.tensor_tensor(out=B.rearrange("p (g e) -> p g e", g=16), in0=D.rearrange("p (g e) -> p g e", g=16),
                      in1=eb, op=ALU.add)
    VEC.drain()
    for bt in range(8):
        corr2 = B[:, bt * 64:(bt + 1) * 64].rearrange("p (g e) -> p g e", g=2)
        cj = corr2.unsqueeze(2).broadcast_to([128, 2, 32, 32])
        ce = corr2.unsqueeze(3).broadcast_to([128, 2, 32, 32])
        VEC.tensor_tensor(out=gtb.rearrange("p (g e j) -> p g e j", g=2, e=32, j=32), in0=cj, in1=ce, op=ALU.is_gt)
        VEC.tensor_tensor(out=eqb.rearrange("p (g e j) -> p g e j", g=2, e=32, j=32), in0=cj, in1=ce, op=ALU.is_equal)
        VEC.drain()
        trib = tri_sb[:].unsqueeze(1).broadcast_to([128, 2, 1024]).rearrange("p g (e j) -> p g e j", e=32)
        VEC.tensor_tensor(out=eqb.rearrange("p (g e j) -> p g e j", g=2, e=32, j=32),
                          in0=eqb.rearrange("p (g e j) -> p g e j", g=2, e=32, j=32), in1=trib, op=ALU.mult)
        VEC.drain()
        VEC.tensor_tensor(out=gtb, in0=gtb, in1=eqb, op=ALU.add)
        VEC.drain()
        VEC.tensor_reduce(A[:, bt * 64:(bt + 1) * 64].rearrange("p (g e) -> p g e", g=2),
                          gtb.rearrange("p (g e j) -> p (g e) j", g=2, e=32, j=32), axis=AX.X, op=ALU.add)
        VEC.drain()
    VEC.tensor_scalar(out=C, in0=A, scalar1=5.5, scalar2=None, op0=ALU.is_lt)
    VEC.drain()
    VEC.tensor_tensor(out=A, in0=D, in1=C, op=ALU.mult)
    VEC.drain()
    VEC.tensor_reduce(s16b, A.rearrange("p (g e) -> p g e", g=16), axis=AX.X, op=ALU.add)
    VEC.drain()
    VEC.tensor_scalar(out=s16b, in0=s16b, scalar1=1e-12, scalar2=None, op0=ALU.max)
    VEC.drain()
    VEC.reciprocal(r16b, s16b)
    VEC.drain()
    rb = r16b.unsqueeze(2).broadcast_to([128, 16, 32])
    VEC.tensor_tensor(out=D.rearrange("p (g e) -> p g e", g=16), in0=A.rearrange("p (g e) -> p g e", g=16),
                      in1=rb, op=ALU.mult)
    sV.inc(VEC.drain())
    for it in range(16):
        ins = VEC.tensor_copy(cmb_bf[:, 0:32], D[:, it * 32:(it + 1) * 32])
        sV.inc(ins)
        TEN.wait_ge(sV.h, sV.n)
        ins = TEN.transpose(pbt[:, 0:128], cmb_bf[:, 0:128], identb[:])
        sP.inc(ins)
        VEC.wait_ge(sP.h, sP.n)
        ins = VEC.tensor_copy(cmbT[:, it * 128:(it + 1) * 128], pbt[0:32, 0:128])
        sV.inc(ins)
        TEN.wait_ge(sV.h, sV.n)

    if dbg == 3:
        VEC.wait_ge(sV.h, sV.n)
        sV.inc(VEC.drain())
        SYN.wait_ge(sV.h, sV.n)
        GPS.wait_ge(sV.h, sV.n)
        gdma(sDB, dbg_outs["dxn2"][:, :], xn2T[:])
        gdma(sDB, dbg_outs["dcmb"][:, :], cmbT[:, :])
        GPS.wait_ge(sDB.h, sDB.n)
        return nc, dbg_outs

    SCA.preload_act_table(AF.Silu)
    gu_free = [qT_region_free, qT_region_free]
    wd_free = [wq_region_free, wq_region_free, wout_region_free, wout_region_free]
    cbc_gate = [0]
    gu_ch = {"silu": [0, 0], "tt1": [0, 0], "tt2": [0, 0], "dq02": 0}
    dn_ev = [0, 0]
    rin_tp = {}
    out_oc = dict(o_count)
    out_tp = {}

    for tb in range(2):
        for tt in range(8):
            git = tb * 8 + tt
            slot = tt % 2
            if (tb, tt) in ((0, 0), (0, 1)):
                gv, gs = slot_gate[slot]
                if gv:
                    SYN.wait_ge(sV.h, gv)
                if gs:
                    SYN.wait_ge(sS.h, gs)
            else:
                pg = git - 2
                SYN.wait_ge(sV.h, rin_tp[pg])
            if tb == 1 and tt < 2:
                SYN.wait_ge(sOS[tt % 2].h, out_oc[16 + 6 + tt])
            rs_n = xdma(sXS[slot], xstage[:, slot * 1024:(slot + 1) * 1024],
                        d_yout[git * 128:(git + 1) * 128, :])
            TEN.wait_ge(sXS[slot].h, rs_n)
            TEN.wait_ge(sV.h, sV.n)
            if tt == 0:
                TEN.wait_ge(sS.h, sS.n)
            for hc in range(8):
                ins = TEN.transpose(pbB[:, hc * 128:(hc + 1) * 128],
                                    xstage[:, slot * 1024 + hc * 128: slot * 1024 + (hc + 1) * 128],
                                    identf[:])
            sP.inc(ins)
            VEC.wait_ge(sP.h, sP.n)
            dst = resT[:].rearrange("p (hc t) -> p hc t", hc=8)[:, :, tt * 128:(tt + 1) * 128]
            ins = VEC.tensor_copy(dst, pbB[:].rearrange("p (hc t) -> p hc t", hc=8))
            rin_tp[git] = sV.inc(ins)

        for tkb in range(2):
            T0 = tb * 1024 + tkb * 512
            GPS.wait_ge(sP.h, max(gu_free[0], gu_free[1], wd_free[0], wd_free[1]))
            n_g0 = wload(sWG[0], wgu_slot(0), d_wsgT, 8, 1024)
            n_g1 = wload(sWG[1], wgu_slot(1), d_wsuT, 8, 1024)
            n_d0 = wload(sWD[0], wd_slot(0), d_wsdT[0:512, :], 4, 1024)
            n_d1 = wload(sWD[1], wd_slot(1), d_wsdT[512:1024, :], 4, 1024)

            TEN.wait_ge(sV.h, sV.n)
            TEN.wait_ge(sWG[0].h, n_g0)
            TEN.wait_ge(sWG[1].h, n_g1)
            TEN.wait_ge(sWD[0].h, n_d0)
            TEN.wait_ge(sWD[1].h, n_d1)
            sh_silu = [0, 0]
            sh_tt = []
            for ic in range(8):
                gsl = (ic % 2) * 512
                if ic >= 2:
                    TEN.wait_ge(sS.h, sh_silu[ic % 2])
                for cc in range(8):
                    ins = TEN.matmul(pbB[:, gsl:gsl + 512],
                                     wgu_slot(0)[:, cc * 1024 + ic * 128: cc * 1024 + (ic + 1) * 128],
                                     xn2T[:, cc * 2048 + T0: cc * 2048 + T0 + 512],
                                     start=(cc == 0), stop=(cc == 7))
                gmm = sP.inc(ins)
                if ic >= 2:
                    TEN.wait_ge(sV.h, sh_tt[ic - 2])
                for cc in range(8):
                    ins2 = TEN.matmul(pbS[:, gsl:gsl + 512],
                                      wgu_slot(1)[:, cc * 1024 + ic * 128: cc * 1024 + (ic + 1) * 128],
                                      xn2T[:, cc * 2048 + T0: cc * 2048 + T0 + 512],
                                      start=(cc == 0), stop=(cc == 7))
                umm = sP.inc(ins2)
                SCA.wait_ge(sP.h, gmm)
                if ic >= 2:
                    SCA.wait_ge(sV.h, sh_tt[ic - 2])
                ins = SCA.activation(out=hsilu[:, gsl:gsl + 512], in_=pbB[:, gsl:gsl + 512], func=AF.Silu)
                sh_silu[ic % 2] = sS.inc(ins)
                VEC.wait_ge(sP.h, umm)
                VEC.wait_ge(sS.h, sh_silu[ic % 2])
                ins = VEC.tensor_tensor(out=smid[:, ic * 512:(ic + 1) * 512],
                                        in0=pbS[:, gsl:gsl + 512], in1=hsilu[:, gsl:gsl + 512], op=ALU.mult)
                sh_tt.append(sV.inc(ins))
            gu_ch["silu"] = sh_silu[:]
            gu_ch["tt1"] = [sh_tt[6], sh_tt[7]]
            gu_ch["tt2"] = [sh_tt[6], sh_tt[7]]
            gu_free[0] = gu_free[1] = sP.n
            for qi, qd in enumerate((1, 0, 3, 2)):
                qh = (qd % 2) * 1024
                if qi == 0:
                    TEN.wait_ge(sV.h, dn_ev[qd % 2])
                elif qi == 1:
                    TEN.wait_ge(sV.h, max(dn_ev[qd % 2], sh_tt[7]))
                else:
                    TEN.wait_ge(sV.h, dn_ev[qd % 2])
                for ic in range(8):
                    if qi == 0:
                        TEN.wait_ge(sV.h, sh_tt[ic])
                    for m in range(2):
                        hc = qd * 2 + m
                        ins = TEN.matmul(pbS[:, qh + m * 512: qh + (m + 1) * 512],
                                         wd_slot(ic // 4)[:, (ic % 4) * 1024 + hc * 128: (ic % 4) * 1024 + (hc + 1) * 128],
                                         smid[:, ic * 512:(ic + 1) * 512],
                                         start=(ic == 0), stop=(ic == 7))
                dmm = sP.inc(ins)
                VEC.wait_ge(sP.h, dmm)
                src = pbS[:, qh:qh + 1024].rearrange("p (m t) -> p m t", m=2)
                dst = resT[:].rearrange("p (hc t) -> p hc t", hc=8)[
                    :, qd * 2:qd * 2 + 2, tkb * 512:tkb * 512 + 512]
                ins = VEC.tensor_tensor(out=dst, in0=src, in1=dst, op=ALU.add)
                dn_ev[qd % 2] = sV.inc(ins)
            cbc_gate[0] = max(cbc_gate[0], dn_ev[1])
            gu_ch["dq02"] = max(gu_ch["dq02"], dn_ev[0])
            wd_free[0] = wd_free[1] = sP.n
            if dbg == 5 and tb == 0 and tkb == 0:
                GPS.wait_ge(sV.h, sV.n)
                gdma(sDB, dbg_outs["dresT"][:, :], resT[:])
                GPS.wait_ge(sDB.h, sDB.n)
                return nc, dbg_outs

            for pr in range(16):
                e0 = pr * 2
                gu_w = [0, 0]
                wd_w = [0, 0]
                for ei in range(2):
                    e = e0 + ei
                    GPS.wait_ge(sP.h, gu_free[e % 2])
                    gu_w[ei] = wload(sWG[e % 2], wgu_slot(e % 2), d_wguT[e], 8, 1024)
                    GPS.wait_ge(sP.h, wd_free[2 * (pr % 2) + ei])
                    wd_w[ei] = wload(sWD[2 * (pr % 2) + ei], wd_slot(2 * (pr % 2) + ei), d_wdT[e], 4, 1024)
                TEN.wait_ge(sV.h, cbc_gate[0])
                cb_mm = {}
                for ei in range(2):
                    e = e0 + ei
                    sel = identb[:, e:e + 1].broadcast_to([128, 128])
                    ins = TEN.matmul(pbS[:, 1024 + ei * 512: 1536 + ei * 512], sel,
                                     wkvuT_sb[:, T0:T0 + 512], start=True, stop=True)
                    cb_mm[ei] = sP.inc(ins)
                for ei in range(2):
                    VEC.wait_ge(sP.h, cb_mm[ei])
                    ins = VEC.tensor_copy(cbc_sb[:, ei * 512:(ei + 1) * 512],
                                          pbS[:, 1024 + ei * 512: 1536 + ei * 512])
                    cbc_gate[0] = sV.inc(ins)
                tt2_list = []
                for g in range(8):
                    ei, gp = g // 4, g % 4
                    e = e0 + ei
                    gsl = (g % 2) * 512
                    if gp == 0:
                        TEN.wait_ge(sWG[e % 2].h, gu_w[ei])
                    if gu_ch["silu"][g % 2]:
                        TEN.wait_ge(sS.h, gu_ch["silu"][g % 2])
                    for cc in range(8):
                        ins = TEN.matmul(pbB[:, gsl:gsl + 512],
                                         wgu_slot(e % 2)[:, cc * 1024 + gp * 128: cc * 1024 + (gp + 1) * 128],
                                         xn2T[:, cc * 2048 + T0: cc * 2048 + T0 + 512],
                                         start=(cc == 0), stop=(cc == 7))
                    gmm = sP.inc(ins)
                    TEN.wait_ge(sV.h, max(gu_ch["tt1"][g % 2], gu_ch["dq02"]))
                    for cc in range(8):
                        ins2 = TEN.matmul(pbS[:, gsl:gsl + 512],
                                          wgu_slot(e % 2)[:, cc * 1024 + (4 + gp) * 128: cc * 1024 + (5 + gp) * 128],
                                          xn2T[:, cc * 2048 + T0: cc * 2048 + T0 + 512],
                                          start=(cc == 0), stop=(cc == 7))
                    umm = sP.inc(ins2)
                    SCA.wait_ge(sP.h, gmm)
                    if gu_ch["tt2"][g % 2]:
                        SCA.wait_ge(sV.h, gu_ch["tt2"][g % 2])
                    ins = SCA.activation(out=hsilu[:, gsl:gsl + 512], in_=pbB[:, gsl:gsl + 512], func=AF.Silu)
                    gu_ch["silu"][g % 2] = sS.inc(ins)
                    VEC.wait_ge(sP.h, umm)
                    VEC.wait_ge(sS.h, gu_ch["silu"][g % 2])
                    ins = VEC.tensor_tensor(out=hsilu[:, gsl:gsl + 512], in0=pbS[:, gsl:gsl + 512],
                                            in1=hsilu[:, gsl:gsl + 512], op=ALU.mult)
                    gu_ch["tt1"][g % 2] = sV.inc(ins)
                    VEC.drain()
                    ins = VEC.tensor_tensor(out=hm_slot(ei, gp),
                                            in0=hsilu[:, gsl:gsl + 512], in1=cbc_sb[:, ei * 512:(ei + 1) * 512],
                                            op=ALU.mult)
                    gu_ch["tt2"][g % 2] = sV.inc(ins)
                    tt2_list.append(gu_ch["tt2"][g % 2])
                    if gp == 3:
                        gu_free[e % 2] = sP.n
                TEN.wait_ge(sWD[2 * (pr % 2)].h, wd_w[0])
                TEN.wait_ge(sWD[2 * (pr % 2) + 1].h, wd_w[1])
                for qi, qd in enumerate((1, 0, 3, 2)):
                    qh = (qd % 2) * 1024
                    if qi == 0:
                        TEN.wait_ge(sV.h, dn_ev[qd % 2])
                    elif qi == 1:
                        TEN.wait_ge(sV.h, max(dn_ev[qd % 2], tt2_list[7]))
                    else:
                        TEN.wait_ge(sV.h, dn_ev[qd % 2])
                    for ei in range(2):
                        for gp in range(4):
                            if qi == 0:
                                TEN.wait_ge(sV.h, tt2_list[ei * 4 + gp])
                            for m in range(2):
                                hc = qd * 2 + m
                                ins = TEN.matmul(pbS[:, qh + m * 512: qh + (m + 1) * 512],
                                                 wd_slot(2 * (pr % 2) + ei)[:, gp * 1024 + hc * 128: gp * 1024 + (hc + 1) * 128],
                                                 hm_slot(ei, gp),
                                                 start=(ei == 0 and gp == 0), stop=(ei == 1 and gp == 3))
                    dmm = sP.inc(ins)
                    VEC.wait_ge(sP.h, dmm)
                    src = pbS[:, qh:qh + 1024].rearrange("p (m t) -> p m t", m=2)
                    dst = resT[:].rearrange("p (hc t) -> p hc t", hc=8)[
                        :, qd * 2:qd * 2 + 2, tkb * 512:tkb * 512 + 512]
                    ins = VEC.tensor_tensor(out=dst, in0=src, in1=dst, op=ALU.add)
                    dn_ev[qd % 2] = sV.inc(ins)
                cbc_gate[0] = max(cbc_gate[0], dn_ev[1])
                gu_ch["dq02"] = max(gu_ch["dq02"], dn_ev[0])
                wd_free[2 * (pr % 2)] = sP.n
                wd_free[2 * (pr % 2) + 1] = sP.n
                if dbg == 6 and tb == 0 and tkb == 0 and pr == 0:
                    GPS.wait_ge(sV.h, sV.n)
                    gdma(sDB, dbg_outs["dresT"][:, :], resT[:])
                    GPS.wait_ge(sDB.h, sDB.n)
                    return nc, dbg_outs

        TEN.wait_ge(sV.h, sV.n)
        TEN.wait_ge(sS.h, sS.n)
        for tt in range(8):
            git = tb * 8 + tt
            if tt >= 1:
                TEN.wait_ge(sV.h, out_tp[git - 1])
            for hc in range(8):
                ins = TEN.transpose(pbB[:, hc * 128:(hc + 1) * 128],
                                    resT[:, hc * 1024 + tt * 128: hc * 1024 + (tt + 1) * 128],
                                    identf[:])
            sP.inc(ins)
            VEC.wait_ge(sP.h, sP.n)
            prev_store = (14 + tt) if (tb == 0 and tt < 2) else (16 + git - 2)
            VEC.wait_ge(sOS[tt % 2].h, out_oc[prev_store])
            ins = VEC.tensor_copy(outst[:, (tt % 2) * 1024:(tt % 2 + 1) * 1024], pbB[:])
            out_tp[git] = sV.inc(ins)
            GPS.wait_ge(sV.h, sV.n)
            out_oc[16 + git] = gdma(sOS[tt % 2], d_yout[git * 128:(git + 1) * 128, :],
                                    outst[:, (tt % 2) * 1024:(tt % 2 + 1) * 1024])

    GPS.wait_ge(sOS[0].h, sOS[0].n)
    GPS.wait_ge(sOS[1].h, sOS[1].n)
    return nc, dbg_outs


_CACHE = {}


def _wkvuTz(wkvu):
    out = np.zeros((128, 16 * 128), np.float32)
    for h in range(16):
        po = (h % 2) * 64
        out[po:po + 64, h * 128:(h + 1) * 128] = wkvu[:, h * 64:(h + 1) * 64].T
    return np.ascontiguousarray(out)


def _host_inputs(L0_attn_x, L0_attn_w_kv_down, L0_attn_w_kv_up, L0_attn_w_q, L0_attn_w_out,
                 L0_ffn_router_weight, L0_ffn_e_score_bias, L0_ffn_gate_up_proj,
                 L0_ffn_down_proj, L0_ffn_shared_gate_w, L0_ffn_shared_up_w,
                 L0_ffn_shared_down_w):
    c = np.ascontiguousarray
    x = c(np.asarray(L0_attn_x, dtype=np.float32))
    shared = {
        "wkvd": c(np.asarray(L0_attn_w_kv_down, np.float32)),
        "wq": c(np.asarray(L0_attn_w_q, np.float32)),
        "wkvu": c(np.asarray(L0_attn_w_kv_up, np.float32)),
        "wkvuT": _wkvuTz(np.asarray(L0_attn_w_kv_up, np.float32)),
        "wout": c(np.asarray(L0_attn_w_out, np.float32)),
        "wrT": c(np.asarray(L0_ffn_router_weight, np.float32).T),
        "ebias": c(np.tile(np.asarray(L0_ffn_e_score_bias, np.float32)[None, :], (128, 1))),
        "wguT": c(np.transpose(np.asarray(L0_ffn_gate_up_proj, np.float32), (0, 2, 1))),
        "wdT": c(np.transpose(np.asarray(L0_ffn_down_proj, np.float32), (0, 2, 1))),
        "wsgT": c(np.asarray(L0_ffn_shared_gate_w, np.float32).T),
        "wsuT": c(np.asarray(L0_ffn_shared_up_w, np.float32).T),
        "wsdT": c(np.asarray(L0_ffn_shared_down_w, np.float32).T),
        "identb": np.eye(128, dtype=ml_dtypes.bfloat16),
        "identf": np.eye(128, dtype=np.float32),
    }
    tri = (np.arange(32)[None, :] < np.arange(32)[:, None]).astype(np.float32)
    shared["tri"] = c(np.tile(tri.reshape(1, 1024), (128, 1))).astype(ml_dtypes.bfloat16)

    in_maps, meta = [], []
    for core in range(8):
        s, role = core // 2, core % 2
        chunks = CHUNKS[role]
        rows = np.concatenate([np.arange(ch * 256, (ch + 1) * 256) for ch in chunks])
        meta.append((s, rows))
        masks = np.zeros((8, 256, 512), np.float32)
        for sl in range(8):
            q = chunks[sl] * 256 + np.arange(256)[:, None]
            k = KUNI[sl] - 512 + np.arange(512)[None, :]
            masks[sl] = np.where(k <= q, 0.0, NEG)
        m = dict(shared)
        m["xseq"] = c(x[s])
        m["xown"] = c(x[s][rows])
        m["mask"] = masks.astype(ml_dtypes.bfloat16)
        in_maps.append(m)
    return in_maps, meta


def kernel(**inputs):
    if "nc" not in _CACHE:
        _CACHE["nc"] = build_nc(dbg=0)[0]
    nc = _CACHE["nc"]
    in_maps, meta = _host_inputs(**inputs)
    res = run_bass_kernel_spmd(nc, in_maps, core_ids=list(range(8)))
    out = np.empty((4, 4096, 1024), np.float32)
    for core in range(8):
        s, rows = meta[core]
        out[s][rows] = res.results[core]["yout"]
    return out


# version 21
# speedup vs baseline: 1.0651x; 1.0096x over previous
import numpy as np
import ml_dtypes
import concourse.bass as bass
import concourse.mybir as mybir
import concourse.bass_utils as _bass_utils
from concourse.bass_utils import run_bass_kernel_spmd

if not getattr(_bass_utils, "_ldwopt_patched", False):
    _orig_run_command = _bass_utils.run_command

    def _run_command_ldwopt(cmd, **kw):
        cmd = [c.replace("--enable-ldw-opt=false", "--enable-ldw-opt=true")
               if isinstance(c, str) else c for c in cmd]
        return _orig_run_command(cmd, **kw)

    _bass_utils.run_command = _run_command_ldwopt
    _bass_utils._ldwopt_patched = True

F32 = mybir.dt.float32
BF16 = mybir.dt.bfloat16
AF = mybir.ActivationFunctionType
ALU = mybir.AluOpType
AX = mybir.AxisListType

KUNI = [4096, 3584, 3072, 2560, 2048, 1536, 1024, 512]
CHUNKS = [[15, 12, 11, 8, 7, 4, 3, 0], [14, 13, 10, 9, 6, 5, 2, 1]]
NH, RK, DM = 16, 128, 1024
NT = 2048
NQS = 16
TSEQ = 4096
NEG = -1.0e30


class Sem:
    def __init__(self, nc, name):
        self.h = nc.alloc_semaphore(name)
        self.n = 0

    def inc(self, instr, k=1):
        instr.then_inc(self.h, k)
        self.n += k
        return self.n


def build_nc(dbg=0):
    nc = bass.Bass("TRN2", target_bir_lowering=False)
    TEN, VEC, SCA, SYN, GPS = nc.tensor, nc.vector, nc.scalar, nc.sync, nc.gpsimd

    d_xseq = nc.dram_tensor("xseq", [TSEQ, DM], F32, kind="ExternalInput").ap()
    d_xown = nc.dram_tensor("xown", [NT, DM], F32, kind="ExternalInput").ap()
    d_mask = nc.dram_tensor("mask", [8, 256, 512], BF16, kind="ExternalInput").ap()
    d_idb = nc.dram_tensor("identb", [128, 128], BF16, kind="ExternalInput").ap()
    d_idf = nc.dram_tensor("identf", [128, 128], F32, kind="ExternalInput").ap()
    d_tri = nc.dram_tensor("tri", [128, 1024], BF16, kind="ExternalInput").ap()
    d_ebias = nc.dram_tensor("ebias", [128, 32], F32, kind="ExternalInput").ap()
    d_wkvd = nc.dram_tensor("wkvd", [DM, RK], F32, kind="ExternalInput").ap()
    d_wq = nc.dram_tensor("wq", [DM, DM], F32, kind="ExternalInput").ap()
    d_wkvu = nc.dram_tensor("wkvu", [RK, 2048], F32, kind="ExternalInput").ap()
    d_wkvuT = nc.dram_tensor("wkvuT", [128, 2048], F32, kind="ExternalInput").ap()
    d_wout = nc.dram_tensor("wout", [DM, DM], F32, kind="ExternalInput").ap()
    d_wrT = nc.dram_tensor("wrT", [DM, 32], F32, kind="ExternalInput").ap()
    d_wguT = nc.dram_tensor("wguT", [32, DM, 1024], F32, kind="ExternalInput").ap()
    d_wdT = nc.dram_tensor("wdT", [32, 512, DM], F32, kind="ExternalInput").ap()
    d_wsgT = nc.dram_tensor("wsgT", [DM, 1024], F32, kind="ExternalInput").ap()
    d_wsuT = nc.dram_tensor("wsuT", [DM, 1024], F32, kind="ExternalInput").ap()
    d_wsdT = nc.dram_tensor("wsdT", [1024, DM], F32, kind="ExternalInput").ap()
    d_yout = nc.dram_tensor("yout", [NT, DM], F32, kind="ExternalOutput").ap()

    dbg_outs = {}
    if dbg:
        for nm, shp in [("dlat", [128, 4096]), ("dq", [128, 16384]),
                        ("dxn2", [128, 16384]), ("dcmb", [32, 2048]),
                        ("dqt", [128, 2048]), ("dS", [128, 4096]),
                        ("dz", [128, 2048]), ("dL", [128, 128]),
                        ("dresT", [128, 8192])]:
            dbg_outs[nm] = nc.dram_tensor(nm, shp, F32, kind="ExternalOutput").ap()

    sb = nc.alloc_sbuf_tensor
    bigA = sb("bigA", [128, 16384], BF16)
    bigB = sb("bigB", [128, 16384], BF16)
    bigC = sb("bigC", [128, 16384], BF16)
    bigD = sb("bigD", [128, 8192], BF16)
    bigE = sb("bigE", [128, 8192], F32)
    latT = sb("latT", [128, 4096], BF16)
    latkm = sb("latkm", [128, 4096], BF16)
    identb = sb("identb_sb", [128, 128], BF16)
    identf = sb("identf_sb", [128, 128], F32)
    xstage = sb("xstage", [128, 2048], F32)
    xntm = sb("xntm", [128, 2048], BF16)
    wkvd_sb = sb("wkvd_sb", [128, 1024], BF16)
    wkvu_sb = sb("wkvu_sb", [128, 2048], BF16)
    wkvuT_sb = sb("wkvuT_sb", [128, 2048], BF16)
    wrT_sb = sb("wrT_sb", [128, 256], BF16)
    ebias_sb = sb("ebias_sb", [128, 32], F32)
    qt_sb = sb("qt_sb", [128, 2048], BF16)
    z_sb = sb("z_sb", [128, 2048], BF16)
    yy_sb = sb("yy_sb", [128, 2048], BF16)
    xownst = sb("xownst", [128, 2048], F32)
    st_sums = sb("st_sums", [128, 48], F32)
    st_sqs = sb("st_sqs", [128, 48], F32)
    st_nms = sb("st_nms", [128, 48], F32)
    st_e2s = sb("st_e2s", [128, 48], F32)
    st_vars = sb("st_vars", [128, 48], F32)
    st_stds = sb("st_stds", [128, 48], F32)
    st_rstds = sb("st_rstds", [128, 48], F32)
    st_mx = sb("st_mx", [128, 1], F32)
    st_mlat = sb("st_mlat", [128, 1], F32)
    st_mlatb = sb("st_mlatb", [128, 1], BF16)
    eps_sb = sb("eps_sb", [128, 1], F32)
    scl8 = sb("scl8", [128, 1], F32)
    one_sb = sb("one_sb", [128, 1], F32)
    mxp = sb("mxp", [128, 8], F32)
    nmx8 = sb("nmx8", [128, 16], F32)
    Lbuf = sb("Lbuf", [128, 128], F32)
    Lsum = sb("Lsum", [128, 16], F32)
    Linv = sb("Linv", [128, 16], F32)

    pbS = nc.alloc_psum_tensor("pbS", [128, 2048], F32)
    pbB = nc.alloc_psum_tensor("pbB", [128, 1024], F32)
    pbt = nc.alloc_psum_tensor("pbt", [128, 2048], BF16)

    sWA = Sem(nc, "sWA")
    sWG = [Sem(nc, "sWG0"), Sem(nc, "sWG1")]
    sWD = [Sem(nc, f"sWD{i}") for i in range(4)]
    sXC = Sem(nc, "sXC")
    sXS = [Sem(nc, "sXS0"), Sem(nc, "sXS1")]
    sXO = [Sem(nc, "sXO0"), Sem(nc, "sXO1")]
    sZB = [Sem(nc, "sZB0"), Sem(nc, "sZB1")]
    sCM = Sem(nc, "sCM")
    sOS = [Sem(nc, "sOS0"), Sem(nc, "sOS1")]
    sDB = Sem(nc, "sDB")
    sP = Sem(nc, "sP")
    sV = Sem(nc, "sV")
    sS = Sem(nc, "sS")
    sBAR = Sem(nc, "sBAR")

    xnstage = bigE[:, 0:2048].bitcast(BF16)
    y_sb = yy_sb[:, 0:1024]
    yT_sb = yy_sb[:, 1024:2048]
    absq = yy_sb[:, 0:2048]
    Ssb2 = bigE[:, 2048:4096].bitcast(BF16)
    outst = xstage
    tri_sb = y_sb
    _rsc = yT_sb[:, 0:1024].bitcast(F32)
    cmbT = wkvuT_sb[0:32, 0:2048]
    Ssb = bigE[:, 0:2048].bitcast(BF16)
    Ssb3 = bigE[:, 4096:6144].bitcast(BF16)
    Ssb4 = bigE[:, 6144:8192].bitcast(BF16)
    resT = bigE
    smid = latT[:, 0:4096]
    cbc_sb = latkm[:, 0:1024]
    hsilu = latkm[:, 1024:2048]
    cmb_bf = latkm[:, 2048:2176]
    xn1ownT = bigA
    STsb = bigA
    xn2T = bigA
    qT = bigB
    wq_sb = bigC[:, 0:8192]
    wout_sb = bigC[:, 8192:16384]
    mask_sb = bigD

    def wgu_slot(s):
        return bigB[:, s * 8192:(s + 1) * 8192]

    def wd_slot(s):
        return bigC[:, s * 4096:(s + 1) * 4096]

    def hm_slot(ei, gp):
        off = ei * 4096 + gp * 1024
        return bigD[:, off:off + 512]

    VEC.memset(eps_sb[:], 1e-5)
    VEC.memset(scl8[:], 0.125)
    VEC.memset(one_sb[:], 1.0)
    VEC.memset(Lbuf[:], 0.0)
    sV.inc(VEC.drain())

    def gdma(sem, out, in_):
        GPS.dma_start(out=out, in_=in_).then_inc(sem.h, 16)
        sem.n += 16
        return sem.n

    def xdma(sem, out, in_):
        SYN.dma_start(out=out, in_=in_).then_inc(sem.h, 16)
        sem.n += 16
        return sem.n

    def wload(sem, dst2d, src2d, g, n):
        dst3 = dst2d.rearrange("p (g n) -> p g n", g=g)
        src3 = src2d.rearrange("(g p) n -> p g n", p=128)
        return gdma(sem, dst3, src3)

    w_kvd = wload(sZB[0], wkvd_sb[:], d_wkvd, 8, 128)
    w_q = wload(sZB[1], wq_sb, d_wq, 8, 1024)
    gdma(sCM, wkvu_sb[:], d_wkvu[:, :])
    w_kvu = gdma(sCM, wkvuT_sb[:], d_wkvuT[:, :])
    wload(sWA, wout_sb, d_wout, 8, 1024)
    w_all = wload(sWA, wrT_sb[:], d_wrT, 8, 32)

    xdma(sXC, identb[:], d_idb[:, :])
    xdma(sXC, identf[:], d_idf[:, :])
    xdma(sXC, tri_sb[:], d_tri[:, :])
    xdma(sXC, ebias_sb[:], d_ebias[:, :])
    _mdst = mask_sb[:, :].rearrange("p (s qq k) -> p s qq k", s=8, qq=2)
    _msrc = d_mask.rearrange("s (qq p) k -> p s qq k", p=128)
    x_const = xdma(sXC, _mdst, _msrc)

    SCA.preload_act_table(AF.Sqrt)

    slot_gate = {0: (0, 0), 1: (0, 0)}
    tp_gate = {0: ("V", 0), 1: ("V", 0)}

    def wait_tp(eng, g):
        if g[1]:
            eng.wait_ge(sS.h if g[0] == "S" else sV.h, g[1])

    def emit_ln_batch(srcs, dsts3, post_cb=None, so=0, pass1=True):
        nt = len(srcs)
        for it, src in enumerate(srcs if pass1 else []):
            slot = it % 2
            gv, gs = slot_gate[slot]
            if gv:
                SYN.wait_ge(sV.h, gv)
            if gs:
                SYN.wait_ge(sS.h, gs)
            n = xdma(sXS[slot], xstage[:, slot * 1024:(slot + 1) * 1024], src)
            VEC.wait_ge(sXS[slot].h, n)
            SCA.wait_ge(sXS[slot].h, n)
            ins = VEC.reduce_sum(st_sums[:, so + it:so + it + 1], xstage[:, slot * 1024:(slot + 1) * 1024], axis=AX.X)
            gv = sV.inc(ins)
            SCA.activation(out=pbB[:, 0:1024], in_=xstage[:, slot * 1024:(slot + 1) * 1024],
                           func=AF.Square, accum_out=st_sqs[:, so + it:so + it + 1])
            gs = sS.inc(SCA.drain())
            slot_gate[slot] = (gv, gs)
        VEC.wait_ge(sS.h, sS.n)
        VEC.drain()
        VEC.tensor_scalar(out=st_nms[:, so:so + nt], in0=st_sums[:, so:so + nt], scalar1=-1.0 / DM, scalar2=None, op0=ALU.mult)
        VEC.tensor_scalar(out=st_e2s[:, so:so + nt], in0=st_sqs[:, so:so + nt], scalar1=1.0 / DM, scalar2=None, op0=ALU.mult)
        VEC.drain()
        VEC.tensor_tensor(out=st_vars[:, so:so + nt], in0=st_nms[:, so:so + nt], in1=st_nms[:, so:so + nt], op=ALU.mult)
        VEC.drain()
        VEC.tensor_tensor(out=st_vars[:, so:so + nt], in0=st_e2s[:, so:so + nt], in1=st_vars[:, so:so + nt], op=ALU.subtract)
        sV.inc(VEC.drain())
        SCA.wait_ge(sV.h, sV.n)
        SCA.activation(out=st_stds[:, so:so + nt], in_=st_vars[:, so:so + nt], func=AF.Sqrt,
                       bias=eps_sb[:], scale=one_sb[:])
        sS.inc(SCA.drain())
        VEC.wait_ge(sS.h, sS.n)
        VEC.reciprocal(st_rstds[:, so:so + nt], st_stds[:, so:so + nt])
        stats_v = sV.inc(VEC.drain())
        for it, (src, dst3) in enumerate(zip(srcs, dsts3)):
            slot = it % 2
            gv, gs = slot_gate[slot]
            if gv:
                SYN.wait_ge(sV.h, gv)
            if gs:
                SYN.wait_ge(sS.h, gs)
            n = xdma(sXS[slot], xstage[:, slot * 1024:(slot + 1) * 1024], src)
            VEC.wait_ge(sXS[slot].h, n)
            ins = VEC.tensor_scalar(out=xntm[:, slot * 1024:(slot + 1) * 1024],
                                    in0=xstage[:, slot * 1024:(slot + 1) * 1024],
                                    scalar1=st_nms[:, so + it:so + it + 1], scalar2=st_rstds[:, so + it:so + it + 1],
                                    op0=ALU.add, op1=ALU.mult)
            gv = sV.inc(ins)
            slot_gate[slot] = (gv, 0)
            TEN.wait_ge(sV.h, gv)
            half = slot * 1024
            wait_tp(TEN, tp_gate[slot])
            for cc in range(8):
                ins = TEN.transpose(pbt[:, half + cc * 128: half + (cc + 1) * 128],
                                    xntm[:, slot * 1024 + cc * 128: slot * 1024 + (cc + 1) * 128],
                                    identb[:])
            sP.inc(ins)
            VEC.wait_ge(sP.h, sP.n)
            ins = VEC.tensor_copy(dst3, pbt[:, half:half + 1024].rearrange("p (cc k) -> p cc k", cc=8))
            tp_gate[slot] = ("V", sV.inc(ins))
            if post_cb is not None:
                post_cb(it)

    TEN.wait_ge(sXC.h, x_const)
    tags = []

    a2_gate = {0: (0, 0), 1: (0, 0)}

    def a1_cb(it):
        if it % 2 == 1:
            t2 = it // 2
            oslot = t2 % 2
            gv2, gs2 = a2_gate[oslot]
            if gv2:
                SYN.wait_ge(sV.h, gv2)
            if gs2:
                SYN.wait_ge(sS.h, gs2)
            n2 = xdma(sXO[oslot], outst[:, oslot * 1024:(oslot + 1) * 1024],
                      d_xown[t2 * 128:(t2 + 1) * 128, :])
            VEC.wait_ge(sXO[oslot].h, n2)
            SCA.wait_ge(sXO[oslot].h, n2)
            ins2 = VEC.reduce_sum(st_sums[:, 32 + t2:33 + t2],
                                  outst[:, oslot * 1024:(oslot + 1) * 1024], axis=AX.X)
            SCA.activation(out=pbB[:, 0:1024], in_=outst[:, oslot * 1024:(oslot + 1) * 1024],
                           func=AF.Square, accum_out=st_sqs[:, 32 + t2:33 + t2])
            a2_gate[oslot] = (sV.inc(ins2), sS.inc(SCA.drain()))
        if it % 4 != 3:
            return
        g = it // 4
        TEN.wait_ge(sV.h, sV.n)
        if g == 0:
            TEN.wait_ge(sZB[0].h, w_kvd)
        ph = (g % 2) * 512
        for cc in range(8):
            ins = TEN.matmul(pbS[:, ph:ph + 512],
                             wkvd_sb[:, cc * 128:(cc + 1) * 128],
                             xnstage[:, cc * 512:(cc + 1) * 512],
                             start=(cc == 0), stop=(cc == 7))
        sP.inc(ins)
        VEC.wait_ge(sP.h, sP.n)
        ins = VEC.tensor_copy(latT[:, g * 512:(g + 1) * 512], pbS[:, ph:ph + 512])
        sV.inc(ins)

    srcs = [d_xseq[it * 128:(it + 1) * 128, :] for it in range(32)]
    dsts = [xnstage[:].rearrange("p (cc k) -> p cc k", cc=8)[:, :, (it % 4) * 128:(it % 4) * 128 + 128]
            for it in range(32)]
    emit_ln_batch(srcs, dsts, post_cb=a1_cb)

    TEN.wait_ge(sV.h, sV.n)
    lk_ev = {}
    for b in range(4):
        half = (b % 2) * 1024
        if b >= 2:
            TEN.wait_ge(sV.h, lk_ev[b - 2])
        for j in range(8):
            kc = b * 8 + j
            ins = TEN.transpose(pbt[:, half + j * 128: half + (j + 1) * 128],
                                latT[:, kc * 128:(kc + 1) * 128], identb[:])
        sP.inc(ins)
        VEC.wait_ge(sP.h, sP.n)
        ins = VEC.tensor_copy(latkm[:, b * 1024:(b + 1) * 1024], pbt[:, half:half + 1024])
        lk_ev[b] = sV.inc(ins)

    VEC.reduce_max(st_mlat[:], latT[:], axis=AX.X, apply_absolute_value=True)
    sV.inc(VEC.drain())
    VEC.tensor_copy(st_mlatb[:], st_mlat[:])
    sV.inc(VEC.drain())
    TEN.wait_ge(sV.h, sV.n)
    tp_gate[0] = ("V", sV.n)
    tp_gate[1] = ("V", sV.n)

    srcs = [d_xown[it * 128:(it + 1) * 128, :] for it in range(16)]
    dsts = [xn1ownT[:].rearrange("p (cc k) -> p cc k", cc=8)[:, :, it * 128:(it + 1) * 128]
            for it in range(16)]
    emit_ln_batch(srcs, dsts, so=32, pass1=False)

    TEN.wait_ge(sV.h, sV.n)
    TEN.wait_ge(sZB[1].h, w_q)
    qp_ev = {}
    j = 0
    for hdc in range(8):
        for tq in range(4):
            ph = (j % 2) * 512
            if j >= 2:
                TEN.wait_ge(sV.h, qp_ev[j - 2])
            for cc in range(8):
                ins = TEN.matmul(pbB[:, ph:ph + 512],
                                 wq_sb[:, cc * 1024 + hdc * 128: cc * 1024 + (hdc + 1) * 128],
                                 xn1ownT[:, cc * 2048 + tq * 512: cc * 2048 + (tq + 1) * 512],
                                 start=(cc == 0), stop=(cc == 7))
            sP.inc(ins)
            VEC.wait_ge(sP.h, sP.n)
            ins = VEC.tensor_copy(qT[:, hdc * 2048 + tq * 512: hdc * 2048 + (tq + 1) * 512],
                                  pbB[:, ph:ph + 512])
            qp_ev[j] = sV.inc(ins)
            j += 1
    wq_region_free = sP.n

    if dbg == 1:
        VEC.wait_ge(sV.h, sV.n)
        sV.inc(VEC.drain())
        SYN.wait_ge(sV.h, sV.n)
        GPS.wait_ge(sV.h, sV.n)
        gdma(sDB, dbg_outs["dlat"][:, :], latT[:])
        gdma(sDB, dbg_outs["dq"][:, :], qT[:])
    if dbg == 1:
        GPS.wait_ge(sDB.h, sDB.n)
        return nc, dbg_outs

    SCA.preload_act_table(AF.Exp)
    TEN.wait_ge(sCM.h, w_kvu)

    res_tt_done = {}
    o_count = {}
    exp_pair_done = [0, 0]

    for qs in range(NQS):
        slot = qs // 2
        kuni = KUNI[slot]
        nk5 = kuni // 512
        nkc = kuni // 128

        if qs >= 2:
            SYN.wait_ge(sV.h, res_tt_done[qs - 2])
        xo_n = xdma(sXO[qs % 2], xownst[:, (qs % 2) * 1024:(qs % 2 + 1) * 1024],
                    d_xown[qs * 128:(qs + 1) * 128, :])

        TEN.wait_ge(sV.h, sV.n)
        qt_ev = {}
        for hb in range(4):
            ph = (hb % 2) * 512
            if hb >= 2:
                TEN.wait_ge(sV.h, qt_ev[hb - 2])
            for hh in range(4):
                h = hb * 4 + hh
                ins = TEN.matmul(pbB[:, ph + hh * 128: ph + (hh + 1) * 128],
                                 wkvuT_sb[:, h * 128:(h + 1) * 128],
                                 qT[:, (h // 2) * 2048 + qs * 128:(h // 2) * 2048 + (qs + 1) * 128],
                                 start=True, stop=True)
            sP.inc(ins)
            VEC.wait_ge(sP.h, sP.n)
            ins = VEC.tensor_scalar(out=qt_sb[:, hb * 512:(hb + 1) * 512], in0=pbB[:, ph:ph + 512],
                                    scalar1=0.125, scalar2=None, op0=ALU.mult)
            qt_ev[hb] = sV.inc(ins)
            SCA.wait_ge(sV.h, qt_ev[hb])
            insA = SCA.activation(out=absq[:, hb * 512:(hb + 1) * 512],
                                  in_=qt_sb[:, hb * 512:(hb + 1) * 512], func=AF.Abs)
            abs_cnt = sS.inc(insA)
        if qs == NQS - 1:
            qT_region_free = sP.n

        if dbg == 21 and qs == 0:
            GPS.wait_ge(sV.h, sV.n)
            gdma(sDB, dbg_outs["dqt"][:, :], qt_sb[:])
            GPS.wait_ge(sDB.h, sDB.n)
            return nc, dbg_outs

        VEC.wait_ge(sS.h, sS.n)
        VEC.memset(Lbuf[:], 0.0)
        sV.inc(VEC.drain())
        SCA.wait_ge(sV.h, sV.n)
        exp_done_cnt = {}

        tp_done = {}
        av_gate = [0]

        def emit_tp(h_t, exp_cnt):
            TEN.wait_ge(sS.h, exp_cnt)
            ng8 = (nkc + 7) // 8
            tp_ev = {}
            for b8 in range(ng8):
                half = (b8 % 2) * 1024
                g = tp_ev[b8 - 2] if b8 >= 2 else tp_gate[b8 % 2]
                wait_tp(TEN, g)
                cnt = min(8, nkc - b8 * 8)
                Sbuf = (Ssb, Ssb2, Ssb3, Ssb4)[h_t % 4]
                for jj in range(cnt):
                    kc = b8 * 8 + jj
                    ins2 = TEN.transpose(pbt[:, half + jj * 128: half + (jj + 1) * 128],
                                         Sbuf[:, kc * 128:(kc + 1) * 128], identb[:])
                sP.inc(ins2)
                dst = STsb[:].rearrange("p (kc r) -> p kc r", kc=32, r=512)[
                    :, b8 * 8: b8 * 8 + cnt, (h_t % 4) * 128:(h_t % 4 + 1) * 128]
                src = pbt[:, half: half + cnt * 128].rearrange("p (kc r) -> p kc r", kc=cnt)
                VEC.wait_ge(sP.h, sP.n)
                ins2 = VEC.tensor_copy(dst, src)
                tp_ev[b8] = ("V", sV.inc(ins2))
                tp_gate[b8 % 2] = tp_ev[b8]
            tp_done[h_t] = tp_ev[ng8 - 1][1]

        def emit_av(hg, part):
            nk2 = nkc // 2
            lo, hi = (0, nk2) if part == 0 else (nk2, nkc)
            if part == 0:
                TEN.wait_ge(sV.h, max(tp_done[hg * 4], tp_done[hg * 4 + 1],
                                      tp_done[hg * 4 + 2], tp_done[hg * 4 + 3], av_gate[0]))
            for kc in range(lo, hi):
                ins2 = TEN.matmul(pbB[:, 0:512], latkm[:, kc * 128:(kc + 1) * 128],
                                  STsb[:, kc * 512:(kc + 1) * 512],
                                  start=(kc == 0), stop=(kc == nkc - 1))
            sP.inc(ins2)
            if part == 1:
                VEC.wait_ge(sP.h, sP.n)
                ins2 = VEC.tensor_copy(z_sb[:, hg * 512:(hg + 1) * 512], pbB[:, 0:512])
                av_gate[0] = sV.inc(ins2)

        TEN.wait_ge(sS.h, abs_cnt)
        for h in range(16):
            ins = TEN.matmul(pbB[:, h:h + 1], absq[:, h * 128:(h + 1) * 128],
                             st_mlatb[:], start=True, stop=True)
        sP.inc(ins)
        VEC.wait_ge(sP.h, sP.n)
        VEC.tensor_scalar(out=nmx8[:, 0:16], in0=pbB[:, 0:16], scalar1=-1.0, scalar2=None, op0=ALU.mult)
        nmx_ready = sV.inc(VEC.drain())
        SCA.wait_ge(sV.h, nmx_ready)

        ngrp = (nk5 + 1) // 2
        exp_last = {}
        for hp in range(9):
            todo = []
            if hp >= 1:
                pA = 2 * (hp - 1)
                todo.append(("tp", pA))
                todo.append(("tp", pA + 1))
                if hp >= 2 and hp % 2 == 0:
                    todo.append(("av", (hp - 2) // 2, 0))
                    todo.append(("av", (hp - 2) // 2, 1))

            def run_blob():
                job = todo.pop(0)
                if job[0] == "tp":
                    emit_tp(job[1], exp_done_cnt[job[1]])
                else:
                    emit_av(job[1], job[2])

            if hp == 8:
                while todo:
                    run_blob()
                continue
            hA, hB = 2 * hp, 2 * hp + 1
            p2 = {hA: {}, hB: {}}
            mask_cnts = {}
            for kj in range(nk5):
                last = (kj == nk5 - 1)
                for h in (hA, hB):
                    pp = (h % 2) * 1024
                    sl = pp + (kj % 2) * 512
                    if kj >= 2:
                        TEN.wait_ge(sS.h, p2[h][(kj - 2) // 2])
                    else:
                        TEN.wait_ge(sS.h, exp_pair_done[h % 2])
                    if last:
                        mslice = mask_sb[:, slot * 1024 + (qs % 2) * 512: slot * 1024 + (qs % 2) * 512 + 512]
                        TEN.matmul(pbS[:, sl:sl + 512], identb[:, 0:128], mslice,
                                   start=True, stop=False)
                    ins = TEN.matmul(pbS[:, sl:sl + 512],
                                     qt_sb[:, h * 128:(h + 1) * 128],
                                     latT[:, kj * 512:(kj + 1) * 512],
                                     start=(not last), stop=True)
                    pe_mm = sP.inc(ins)
                    if kj % 2 == 1 or last:
                        grp = kj // 2
                        glen = 1024 if kj % 2 == 1 else 512
                        SCA.wait_ge(sP.h, pe_mm)
                        Sout = (Ssb, Ssb2, Ssb3, Ssb4)[h % 4]
                        ins = SCA.activation(out=Sout[:, grp * 1024: grp * 1024 + glen],
                                             in_=pbS[:, pp:pp + glen],
                                             func=AF.Exp, scale=one_sb[:], bias=nmx8[:, h:h + 1],
                                             accum_out=Lbuf[:, h * 8 + grp: h * 8 + grp + 1])
                        p2[h][grp] = sS.inc(ins)
                        exp_last[h] = p2[h][grp]
                if kj % 2 == 1 and not last and todo:
                    run_blob()
            while todo:
                run_blob()
            for h in (hA, hB):
                exp_pair_done[h % 2] = exp_last[h]
                exp_done_cnt[h] = exp_last[h]

        VEC.wait_ge(sS.h, sS.n)
        VEC.tensor_reduce(Lsum[:], Lbuf[:].rearrange("p (h k) -> p h k", h=16), axis=AX.X, op=ALU.add)
        VEC.drain()
        VEC.reciprocal(Linv[:], Lsum[:])
        sV.inc(VEC.drain())

        TEN.wait_ge(sV.h, sV.n)
        for h in range(NH):
            ins = TEN.matmul(pbB[:, h * 64:(h + 1) * 64],
                             z_sb[:, h * 128:(h + 1) * 128],
                             wkvu_sb[:, 1024 + h * 64: 1024 + (h + 1) * 64],
                             start=True, stop=True)
        sP.inc(ins)
        VEC.wait_ge(sP.h, sP.n)
        linv_b = Linv[:].unsqueeze(2).broadcast_to([128, 16, 64])
        ins = VEC.tensor_tensor(out=y_sb[:].rearrange("p (h d) -> p h d", h=16),
                                in0=pbB[:].rearrange("p (h d) -> p h d", h=16),
                                in1=linv_b, op=ALU.mult)
        sV.inc(ins)
        TEN.wait_ge(sV.h, sV.n)
        for cc in range(8):
            ins = TEN.transpose(pbt[:, cc * 128:(cc + 1) * 128],
                                y_sb[:, cc * 128:(cc + 1) * 128], identb[:])
        sP.inc(ins)
        VEC.wait_ge(sP.h, sP.n)
        ins = VEC.tensor_copy(yT_sb[:], pbt[:, 0:1024])
        sV.inc(ins)
        TEN.wait_ge(sV.h, sV.n)
        if qs == 0:
            TEN.wait_ge(sWA.h, w_all)
        for cc in range(2):
            for hdc in range(8):
                ins = TEN.matmul(pbB[:, cc * 512:(cc + 1) * 512],
                                 yT_sb[:, hdc * 128:(hdc + 1) * 128],
                                 wout_sb[:, hdc * 1024 + cc * 512: hdc * 1024 + cc * 512 + 512],
                                 start=(hdc == 0), stop=(hdc == 7))
        sP.inc(ins)
        VEC.wait_ge(sP.h, sP.n)
        VEC.wait_ge(sXO[qs % 2].h, xo_n)
        if qs >= 2:
            VEC.wait_ge(sOS[qs % 2].h, o_count[qs - 2])
        ins = VEC.tensor_tensor(out=outst[:, (qs % 2) * 1024:(qs % 2 + 1) * 1024],
                                in0=pbB[:],
                                in1=xownst[:, (qs % 2) * 1024:(qs % 2 + 1) * 1024], op=ALU.add)
        res_tt_done[qs] = sV.inc(ins)
        if qs == NQS - 1:
            wout_region_free = sP.n
        GPS.wait_ge(sV.h, sV.n)
        o_count[qs] = gdma(sOS[qs % 2], d_yout[qs * 128:(qs + 1) * 128, :],
                           outst[:, (qs % 2) * 1024:(qs % 2 + 1) * 1024])
        if dbg == 24 and qs == 0:
            GPS.wait_ge(sOS[0].h, sOS[0].n)
            return nc, dbg_outs

    if dbg == 2:
        GPS.wait_ge(sOS[0].h, sOS[0].n)
        GPS.wait_ge(sOS[1].h, sOS[1].n)
        return nc, dbg_outs

    sBAR.inc(TEN.nop())
    sBAR.inc(VEC.drain())
    sBAR.inc(SCA.drain())
    for eng in (TEN, VEC, SCA, SYN, GPS):
        eng.wait_ge(sBAR.h, 3)

    SCA.preload_act_table(AF.Sqrt)
    SYN.wait_ge(sOS[0].h, sOS[0].n)
    SYN.wait_ge(sOS[1].h, sOS[1].n)
    srcs = [d_yout[it * 128:(it + 1) * 128, :] for it in range(16)]
    dsts = [xn2T[:].rearrange("p (cc k) -> p cc k", cc=8)[:, :, it * 128:(it + 1) * 128]
            for it in range(16)]
    emit_ln_batch(srcs, dsts)

    SCA.preload_act_table(AF.Exp)
    VEC.memset(cmb_bf[:], 0.0)
    sV.inc(VEC.drain())
    TEN.wait_ge(sV.h, sV.n)
    TEN.wait_ge(sS.h, sS.n)
    zF = z_sb[:, 0:2048].bitcast(F32)
    qF = qt_sb[:, 0:2048].bitcast(F32)
    A, B = zF[:, 0:512], zF[:, 512:1024]
    C, D = qF[:, 0:512], qF[:, 512:1024]
    m16 = _rsc[:, 0:16]
    s16 = _rsc[:, 16:32]
    l16 = _rsc[:, 32:48]
    s16b = _rsc[:, 48:64]
    r16b = _rsc[:, 64:80]
    gtb = Ssb[:, 0:2048]
    eqb = Ssb[:, 2048:4096]
    for it in range(16):
        for cc in range(8):
            ins = TEN.matmul(pbB[:, it * 32:(it + 1) * 32],
                             xn2T[:, cc * 2048 + it * 128: cc * 2048 + (it + 1) * 128],
                             wrT_sb[:, cc * 32:(cc + 1) * 32],
                             start=(cc == 0), stop=(cc == 7))
    sP.inc(ins)
    VEC.wait_ge(sP.h, sP.n)
    VEC.tensor_copy(A, pbB[:, 0:512])
    sV.inc(VEC.drain())
    VEC.tensor_reduce(m16, A.rearrange("p (g e) -> p g e", g=16), axis=AX.X, op=ALU.max, negate=True)
    VEC.drain()
    mb = m16.unsqueeze(2).broadcast_to([128, 16, 32])
    VEC.tensor_tensor(out=B.rearrange("p (g e) -> p g e", g=16), in0=A.rearrange("p (g e) -> p g e", g=16),
                      in1=mb, op=ALU.add)
    sV.inc(VEC.drain())
    SCA.wait_ge(sV.h, sV.n)
    ins = SCA.activation(out=C, in_=B, func=AF.Exp)
    sS.inc(ins)
    VEC.wait_ge(sS.h, sS.n)
    VEC.tensor_reduce(s16, C.rearrange("p (g e) -> p g e", g=16), axis=AX.X, op=ALU.add)
    VEC.drain()
    VEC.reciprocal(l16, s16)
    VEC.drain()
    lb = l16.unsqueeze(2).broadcast_to([128, 16, 32])
    VEC.tensor_tensor(out=D.rearrange("p (g e) -> p g e", g=16), in0=C.rearrange("p (g e) -> p g e", g=16),
                      in1=lb, op=ALU.mult)
    VEC.drain()
    eb = ebias_sb[:].unsqueeze(1).broadcast_to([128, 16, 32])
    VEC.tensor_tensor(out=B.rearrange("p (g e) -> p g e", g=16), in0=D.rearrange("p (g e) -> p g e", g=16),
                      in1=eb, op=ALU.add)
    VEC.drain()
    for bt in range(8):
        corr2 = B[:, bt * 64:(bt + 1) * 64].rearrange("p (g e) -> p g e", g=2)
        cj = corr2.unsqueeze(2).broadcast_to([128, 2, 32, 32])
        ce = corr2.unsqueeze(3).broadcast_to([128, 2, 32, 32])
        VEC.tensor_tensor(out=gtb.rearrange("p (g e j) -> p g e j", g=2, e=32, j=32), in0=cj, in1=ce, op=ALU.is_gt)
        VEC.tensor_tensor(out=eqb.rearrange("p (g e j) -> p g e j", g=2, e=32, j=32), in0=cj, in1=ce, op=ALU.is_equal)
        VEC.drain()
        trib = tri_sb[:].unsqueeze(1).broadcast_to([128, 2, 1024]).rearrange("p g (e j) -> p g e j", e=32)
        VEC.tensor_tensor(out=eqb.rearrange("p (g e j) -> p g e j", g=2, e=32, j=32),
                          in0=eqb.rearrange("p (g e j) -> p g e j", g=2, e=32, j=32), in1=trib, op=ALU.mult)
        VEC.drain()
        VEC.tensor_tensor(out=gtb, in0=gtb, in1=eqb, op=ALU.add)
        VEC.drain()
        VEC.tensor_reduce(A[:, bt * 64:(bt + 1) * 64].rearrange("p (g e) -> p g e", g=2),
                          gtb.rearrange("p (g e j) -> p (g e) j", g=2, e=32, j=32), axis=AX.X, op=ALU.add)
        VEC.drain()
    VEC.tensor_scalar(out=C, in0=A, scalar1=5.5, scalar2=None, op0=ALU.is_lt)
    VEC.drain()
    VEC.tensor_tensor(out=A, in0=D, in1=C, op=ALU.mult)
    VEC.drain()
    VEC.tensor_reduce(s16b, A.rearrange("p (g e) -> p g e", g=16), axis=AX.X, op=ALU.add)
    VEC.drain()
    VEC.tensor_scalar(out=s16b, in0=s16b, scalar1=1e-12, scalar2=None, op0=ALU.max)
    VEC.drain()
    VEC.reciprocal(r16b, s16b)
    VEC.drain()
    rb = r16b.unsqueeze(2).broadcast_to([128, 16, 32])
    VEC.tensor_tensor(out=D.rearrange("p (g e) -> p g e", g=16), in0=A.rearrange("p (g e) -> p g e", g=16),
                      in1=rb, op=ALU.mult)
    sV.inc(VEC.drain())
    for it in range(16):
        ins = VEC.tensor_copy(cmb_bf[:, 0:32], D[:, it * 32:(it + 1) * 32])
        sV.inc(ins)
        TEN.wait_ge(sV.h, sV.n)
        ins = TEN.transpose(pbt[:, 0:128], cmb_bf[:, 0:128], identb[:])
        sP.inc(ins)
        VEC.wait_ge(sP.h, sP.n)
        ins = VEC.tensor_copy(cmbT[:, it * 128:(it + 1) * 128], pbt[0:32, 0:128])
        sV.inc(ins)
        TEN.wait_ge(sV.h, sV.n)

    if dbg == 3:
        VEC.wait_ge(sV.h, sV.n)
        sV.inc(VEC.drain())
        SYN.wait_ge(sV.h, sV.n)
        GPS.wait_ge(sV.h, sV.n)
        gdma(sDB, dbg_outs["dxn2"][:, :], xn2T[:])
        gdma(sDB, dbg_outs["dcmb"][:, :], cmbT[:, :])
        GPS.wait_ge(sDB.h, sDB.n)
        return nc, dbg_outs

    SCA.preload_act_table(AF.Silu)
    gu_free = [qT_region_free, qT_region_free]
    wd_free = [wq_region_free, wq_region_free, wout_region_free, wout_region_free]
    cbc_gate = [0]
    gu_ch = {"silu": [0, 0], "tt1": [0, 0], "tt2": [0, 0], "dq02": 0}
    dn_ev = [0, 0]
    rin_tp = {}
    out_oc = dict(o_count)
    out_tp = {}

    for tb in range(2):
        for tt in range(8):
            git = tb * 8 + tt
            slot = tt % 2
            if (tb, tt) in ((0, 0), (0, 1)):
                gv, gs = slot_gate[slot]
                if gv:
                    SYN.wait_ge(sV.h, gv)
                if gs:
                    SYN.wait_ge(sS.h, gs)
            else:
                pg = git - 2
                SYN.wait_ge(sV.h, rin_tp[pg])
            if tb == 1 and tt < 2:
                SYN.wait_ge(sOS[tt % 2].h, out_oc[16 + 6 + tt])
            rs_n = xdma(sXS[slot], xstage[:, slot * 1024:(slot + 1) * 1024],
                        d_yout[git * 128:(git + 1) * 128, :])
            TEN.wait_ge(sXS[slot].h, rs_n)
            TEN.wait_ge(sV.h, sV.n)
            if tt == 0:
                TEN.wait_ge(sS.h, sS.n)
            for hc in range(8):
                ins = TEN.transpose(pbB[:, hc * 128:(hc + 1) * 128],
                                    xstage[:, slot * 1024 + hc * 128: slot * 1024 + (hc + 1) * 128],
                                    identf[:])
            sP.inc(ins)
            VEC.wait_ge(sP.h, sP.n)
            dst = resT[:].rearrange("p (hc t) -> p hc t", hc=8)[:, :, tt * 128:(tt + 1) * 128]
            ins = VEC.tensor_copy(dst, pbB[:].rearrange("p (hc t) -> p hc t", hc=8))
            rin_tp[git] = sV.inc(ins)

        for tkb in range(2):
            T0 = tb * 1024 + tkb * 512
            GPS.wait_ge(sP.h, max(gu_free[0], gu_free[1], wd_free[0], wd_free[1]))
            n_g0 = wload(sWG[0], wgu_slot(0), d_wsgT, 8, 1024)
            n_g1 = wload(sWG[1], wgu_slot(1), d_wsuT, 8, 1024)
            n_d0 = wload(sWD[0], wd_slot(0), d_wsdT[0:512, :], 4, 1024)
            n_d1 = wload(sWD[1], wd_slot(1), d_wsdT[512:1024, :], 4, 1024)

            TEN.wait_ge(sV.h, sV.n)
            TEN.wait_ge(sWG[0].h, n_g0)
            TEN.wait_ge(sWG[1].h, n_g1)
            TEN.wait_ge(sWD[0].h, n_d0)
            TEN.wait_ge(sWD[1].h, n_d1)
            sh_silu = [0, 0]
            sh_tt = []
            for ic in range(8):
                gsl = (ic % 2) * 512
                if ic >= 2:
                    TEN.wait_ge(sS.h, sh_silu[ic % 2])
                for cc in range(8):
                    ins = TEN.matmul(pbB[:, gsl:gsl + 512],
                                     wgu_slot(0)[:, cc * 1024 + ic * 128: cc * 1024 + (ic + 1) * 128],
                                     xn2T[:, cc * 2048 + T0: cc * 2048 + T0 + 512],
                                     start=(cc == 0), stop=(cc == 7))
                gmm = sP.inc(ins)
                if ic >= 2:
                    TEN.wait_ge(sV.h, sh_tt[ic - 2])
                for cc in range(8):
                    ins2 = TEN.matmul(pbS[:, gsl:gsl + 512],
                                      wgu_slot(1)[:, cc * 1024 + ic * 128: cc * 1024 + (ic + 1) * 128],
                                      xn2T[:, cc * 2048 + T0: cc * 2048 + T0 + 512],
                                      start=(cc == 0), stop=(cc == 7))
                umm = sP.inc(ins2)
                SCA.wait_ge(sP.h, gmm)
                if ic >= 2:
                    SCA.wait_ge(sV.h, sh_tt[ic - 2])
                ins = SCA.activation(out=hsilu[:, gsl:gsl + 512], in_=pbB[:, gsl:gsl + 512], func=AF.Silu)
                sh_silu[ic % 2] = sS.inc(ins)
                VEC.wait_ge(sP.h, umm)
                VEC.wait_ge(sS.h, sh_silu[ic % 2])
                ins = VEC.tensor_tensor(out=smid[:, ic * 512:(ic + 1) * 512],
                                        in0=pbS[:, gsl:gsl + 512], in1=hsilu[:, gsl:gsl + 512], op=ALU.mult)
                sh_tt.append(sV.inc(ins))
            gu_ch["silu"] = sh_silu[:]
            gu_ch["tt1"] = [sh_tt[6], sh_tt[7]]
            gu_ch["tt2"] = [sh_tt[6], sh_tt[7]]
            gu_free[0] = gu_free[1] = sP.n
            for qi, qd in enumerate((1, 0, 3, 2)):
                qh = (qd % 2) * 1024
                if qi == 0:
                    TEN.wait_ge(sV.h, dn_ev[qd % 2])
                elif qi == 1:
                    TEN.wait_ge(sV.h, max(dn_ev[qd % 2], sh_tt[7]))
                else:
                    TEN.wait_ge(sV.h, dn_ev[qd % 2])
                for ic in range(8):
                    if qi == 0:
                        TEN.wait_ge(sV.h, sh_tt[ic])
                    for m in range(2):
                        hc = qd * 2 + m
                        ins = TEN.matmul(pbS[:, qh + m * 512: qh + (m + 1) * 512],
                                         wd_slot(ic // 4)[:, (ic % 4) * 1024 + hc * 128: (ic % 4) * 1024 + (hc + 1) * 128],
                                         smid[:, ic * 512:(ic + 1) * 512],
                                         start=(ic == 0), stop=(ic == 7))
                dmm = sP.inc(ins)
                VEC.wait_ge(sP.h, dmm)
                src = pbS[:, qh:qh + 1024].rearrange("p (m t) -> p m t", m=2)
                dst = resT[:].rearrange("p (hc t) -> p hc t", hc=8)[
                    :, qd * 2:qd * 2 + 2, tkb * 512:tkb * 512 + 512]
                ins = VEC.tensor_tensor(out=dst, in0=src, in1=dst, op=ALU.add)
                dn_ev[qd % 2] = sV.inc(ins)
            cbc_gate[0] = max(cbc_gate[0], dn_ev[1])
            gu_ch["dq02"] = max(gu_ch["dq02"], dn_ev[0])
            wd_free[0] = wd_free[1] = sP.n
            if dbg == 5 and tb == 0 and tkb == 0:
                GPS.wait_ge(sV.h, sV.n)
                gdma(sDB, dbg_outs["dresT"][:, :], resT[:])
                GPS.wait_ge(sDB.h, sDB.n)
                return nc, dbg_outs

            for pr in range(16):
                e0 = pr * 2
                gu_w = [0, 0]
                wd_w = [0, 0]
                for ei in range(2):
                    e = e0 + ei
                    GPS.wait_ge(sP.h, gu_free[e % 2])
                    gu_w[ei] = wload(sWG[e % 2], wgu_slot(e % 2), d_wguT[e], 8, 1024)
                    GPS.wait_ge(sP.h, wd_free[2 * (pr % 2) + ei])
                    wd_w[ei] = wload(sWD[2 * (pr % 2) + ei], wd_slot(2 * (pr % 2) + ei), d_wdT[e], 4, 1024)
                TEN.wait_ge(sV.h, cbc_gate[0])
                cb_mm = {}
                for ei in range(2):
                    e = e0 + ei
                    sel = identb[:, e:e + 1].broadcast_to([128, 128])
                    ins = TEN.matmul(pbS[:, 1024 + ei * 512: 1536 + ei * 512], sel,
                                     wkvuT_sb[:, T0:T0 + 512], start=True, stop=True)
                    cb_mm[ei] = sP.inc(ins)
                for ei in range(2):
                    VEC.wait_ge(sP.h, cb_mm[ei])
                    ins = VEC.tensor_copy(cbc_sb[:, ei * 512:(ei + 1) * 512],
                                          pbS[:, 1024 + ei * 512: 1536 + ei * 512])
                    cbc_gate[0] = sV.inc(ins)
                tt2_list = []
                for g in range(8):
                    ei, gp = g // 4, g % 4
                    e = e0 + ei
                    gsl = (g % 2) * 512
                    if gp == 0:
                        TEN.wait_ge(sWG[e % 2].h, gu_w[ei])
                    if gu_ch["silu"][g % 2]:
                        TEN.wait_ge(sS.h, gu_ch["silu"][g % 2])
                    for cc in range(8):
                        ins = TEN.matmul(pbB[:, gsl:gsl + 512],
                                         wgu_slot(e % 2)[:, cc * 1024 + gp * 128: cc * 1024 + (gp + 1) * 128],
                                         xn2T[:, cc * 2048 + T0: cc * 2048 + T0 + 512],
                                         start=(cc == 0), stop=(cc == 7))
                    gmm = sP.inc(ins)
                    TEN.wait_ge(sV.h, max(gu_ch["tt1"][g % 2], gu_ch["dq02"]))
                    for cc in range(8):
                        ins2 = TEN.matmul(pbS[:, gsl:gsl + 512],
                                          wgu_slot(e % 2)[:, cc * 1024 + (4 + gp) * 128: cc * 1024 + (5 + gp) * 128],
                                          xn2T[:, cc * 2048 + T0: cc * 2048 + T0 + 512],
                                          start=(cc == 0), stop=(cc == 7))
                    umm = sP.inc(ins2)
                    SCA.wait_ge(sP.h, gmm)
                    if gu_ch["tt2"][g % 2]:
                        SCA.wait_ge(sV.h, gu_ch["tt2"][g % 2])
                    ins = SCA.activation(out=hsilu[:, gsl:gsl + 512], in_=pbB[:, gsl:gsl + 512], func=AF.Silu)
                    gu_ch["silu"][g % 2] = sS.inc(ins)
                    VEC.wait_ge(sP.h, umm)
                    VEC.wait_ge(sS.h, gu_ch["silu"][g % 2])
                    ins = VEC.tensor_tensor(out=hsilu[:, gsl:gsl + 512], in0=pbS[:, gsl:gsl + 512],
                                            in1=hsilu[:, gsl:gsl + 512], op=ALU.mult)
                    gu_ch["tt1"][g % 2] = sV.inc(ins)
                    VEC.drain()
                    ins = VEC.tensor_tensor(out=hm_slot(ei, gp),
                                            in0=hsilu[:, gsl:gsl + 512], in1=cbc_sb[:, ei * 512:(ei + 1) * 512],
                                            op=ALU.mult)
                    gu_ch["tt2"][g % 2] = sV.inc(ins)
                    tt2_list.append(gu_ch["tt2"][g % 2])
                    if gp == 3:
                        gu_free[e % 2] = sP.n
                TEN.wait_ge(sWD[2 * (pr % 2)].h, wd_w[0])
                TEN.wait_ge(sWD[2 * (pr % 2) + 1].h, wd_w[1])
                for qi, qd in enumerate((1, 0, 3, 2)):
                    qh = (qd % 2) * 1024
                    if qi == 0:
                        TEN.wait_ge(sV.h, dn_ev[qd % 2])
                    elif qi == 1:
                        TEN.wait_ge(sV.h, max(dn_ev[qd % 2], tt2_list[7]))
                    else:
                        TEN.wait_ge(sV.h, dn_ev[qd % 2])
                    for ei in range(2):
                        for gp in range(4):
                            if qi == 0:
                                TEN.wait_ge(sV.h, tt2_list[ei * 4 + gp])
                            for m in range(2):
                                hc = qd * 2 + m
                                ins = TEN.matmul(pbS[:, qh + m * 512: qh + (m + 1) * 512],
                                                 wd_slot(2 * (pr % 2) + ei)[:, gp * 1024 + hc * 128: gp * 1024 + (hc + 1) * 128],
                                                 hm_slot(ei, gp),
                                                 start=(ei == 0 and gp == 0), stop=(ei == 1 and gp == 3))
                    dmm = sP.inc(ins)
                    VEC.wait_ge(sP.h, dmm)
                    src = pbS[:, qh:qh + 1024].rearrange("p (m t) -> p m t", m=2)
                    dst = resT[:].rearrange("p (hc t) -> p hc t", hc=8)[
                        :, qd * 2:qd * 2 + 2, tkb * 512:tkb * 512 + 512]
                    ins = VEC.tensor_tensor(out=dst, in0=src, in1=dst, op=ALU.add)
                    dn_ev[qd % 2] = sV.inc(ins)
                cbc_gate[0] = max(cbc_gate[0], dn_ev[1])
                gu_ch["dq02"] = max(gu_ch["dq02"], dn_ev[0])
                wd_free[2 * (pr % 2)] = sP.n
                wd_free[2 * (pr % 2) + 1] = sP.n
                if dbg == 6 and tb == 0 and tkb == 0 and pr == 0:
                    GPS.wait_ge(sV.h, sV.n)
                    gdma(sDB, dbg_outs["dresT"][:, :], resT[:])
                    GPS.wait_ge(sDB.h, sDB.n)
                    return nc, dbg_outs

        TEN.wait_ge(sV.h, sV.n)
        TEN.wait_ge(sS.h, sS.n)
        for tt in range(8):
            git = tb * 8 + tt
            if tt >= 1:
                TEN.wait_ge(sV.h, out_tp[git - 1])
            for hc in range(8):
                ins = TEN.transpose(pbB[:, hc * 128:(hc + 1) * 128],
                                    resT[:, hc * 1024 + tt * 128: hc * 1024 + (tt + 1) * 128],
                                    identf[:])
            sP.inc(ins)
            VEC.wait_ge(sP.h, sP.n)
            prev_store = (14 + tt) if (tb == 0 and tt < 2) else (16 + git - 2)
            VEC.wait_ge(sOS[tt % 2].h, out_oc[prev_store])
            ins = VEC.tensor_copy(outst[:, (tt % 2) * 1024:(tt % 2 + 1) * 1024], pbB[:])
            out_tp[git] = sV.inc(ins)
            GPS.wait_ge(sV.h, sV.n)
            out_oc[16 + git] = gdma(sOS[tt % 2], d_yout[git * 128:(git + 1) * 128, :],
                                    outst[:, (tt % 2) * 1024:(tt % 2 + 1) * 1024])

    GPS.wait_ge(sOS[0].h, sOS[0].n)
    GPS.wait_ge(sOS[1].h, sOS[1].n)
    return nc, dbg_outs


_CACHE = {}


def _wkvuTz(wkvu):
    out = np.zeros((128, 16 * 128), np.float32)
    for h in range(16):
        po = (h % 2) * 64
        out[po:po + 64, h * 128:(h + 1) * 128] = wkvu[:, h * 64:(h + 1) * 64].T
    return np.ascontiguousarray(out)


def _host_inputs(L0_attn_x, L0_attn_w_kv_down, L0_attn_w_kv_up, L0_attn_w_q, L0_attn_w_out,
                 L0_ffn_router_weight, L0_ffn_e_score_bias, L0_ffn_gate_up_proj,
                 L0_ffn_down_proj, L0_ffn_shared_gate_w, L0_ffn_shared_up_w,
                 L0_ffn_shared_down_w):
    c = np.ascontiguousarray
    x = c(np.asarray(L0_attn_x, dtype=np.float32))
    shared = {
        "wkvd": c(np.asarray(L0_attn_w_kv_down, np.float32)),
        "wq": c(np.asarray(L0_attn_w_q, np.float32)),
        "wkvu": c(np.asarray(L0_attn_w_kv_up, np.float32)),
        "wkvuT": _wkvuTz(np.asarray(L0_attn_w_kv_up, np.float32)),
        "wout": c(np.asarray(L0_attn_w_out, np.float32)),
        "wrT": c(np.asarray(L0_ffn_router_weight, np.float32).T),
        "ebias": c(np.tile(np.asarray(L0_ffn_e_score_bias, np.float32)[None, :], (128, 1))),
        "wguT": c(np.transpose(np.asarray(L0_ffn_gate_up_proj, np.float32), (0, 2, 1))),
        "wdT": c(np.transpose(np.asarray(L0_ffn_down_proj, np.float32), (0, 2, 1))),
        "wsgT": c(np.asarray(L0_ffn_shared_gate_w, np.float32).T),
        "wsuT": c(np.asarray(L0_ffn_shared_up_w, np.float32).T),
        "wsdT": c(np.asarray(L0_ffn_shared_down_w, np.float32).T),
        "identb": np.eye(128, dtype=ml_dtypes.bfloat16),
        "identf": np.eye(128, dtype=np.float32),
    }
    tri = (np.arange(32)[None, :] < np.arange(32)[:, None]).astype(np.float32)
    shared["tri"] = c(np.tile(tri.reshape(1, 1024), (128, 1))).astype(ml_dtypes.bfloat16)

    in_maps, meta = [], []
    for core in range(8):
        s, role = core // 2, core % 2
        chunks = CHUNKS[role]
        rows = np.concatenate([np.arange(ch * 256, (ch + 1) * 256) for ch in chunks])
        meta.append((s, rows))
        masks = np.zeros((8, 256, 512), np.float32)
        for sl in range(8):
            q = chunks[sl] * 256 + np.arange(256)[:, None]
            k = KUNI[sl] - 512 + np.arange(512)[None, :]
            masks[sl] = np.where(k <= q, 0.0, NEG)
        m = dict(shared)
        m["xseq"] = c(x[s])
        m["xown"] = c(x[s][rows])
        m["mask"] = masks.astype(ml_dtypes.bfloat16)
        in_maps.append(m)
    return in_maps, meta


def kernel(**inputs):
    if "nc" not in _CACHE:
        _CACHE["nc"] = build_nc(dbg=0)[0]
    nc = _CACHE["nc"]
    in_maps, meta = _host_inputs(**inputs)
    res = run_bass_kernel_spmd(nc, in_maps, core_ids=list(range(8)))
    out = np.empty((4, 4096, 1024), np.float32)
    for core in range(8):
        s, rows = meta[core]
        out[s][rows] = res.results[core]["yout"]
    return out
